# version 13
# speedup vs baseline: 1.5239x; 1.0051x over previous
import numpy as np
import ml_dtypes
import concourse.bass as bass
import concourse.mybir as mybir
from concourse.bass_utils import run_bass_kernel_spmd
from contextlib import ExitStack

AF = mybir.ActivationFunctionType
ALU = mybir.AluOpType
AX = mybir.AxisListType
F32, BF16, I32 = mybir.dt.float32, mybir.dt.bfloat16, mybir.dt.int32

NCORES = 8
BPC = 4
L = 2048
T = BPC * L
D = 1024
DI = 2048
NH = 32
Q = 128
NCH = L // Q
GRP = 4
NG = NCH // GRP
HEADS = 16
HD = 64
MZ = 256
HID = 2048
NT = T // 128
EPS_RMS = 1.1920929e-07
EPS_LN = 1e-5
TWO_PI = float(2.0 * np.pi)
BIG = 1e30


class CT:
    def __init__(self, nc, es, name):
        self.h = es.enter_context(nc.semaphore(name))
        self.n = 0
    def inc(self, inst, k=1):
        inst.then_inc(self.h, k)
        self.n += k
        return self.n


def _host_consts():
    c = {}
    le = np.where(np.arange(128)[:, None] <= np.arange(128)[None, :], 1.0, 0.0)
    c["c_cum"] = le.astype(np.float32)
    c["c_wmask"] = le.astype(ml_dtypes.bfloat16)
    c["c_idt16"] = np.eye(128, dtype=np.float32).astype(ml_dtypes.bfloat16)
    c["c_idt32"] = np.eye(128, dtype=np.float32)
    oh = np.zeros((32, 32 * 128), np.float32)
    for h in range(32):
        oh[h, h * 128:(h + 1) * 128] = 1.0
    c["c_onehot"] = oh
    inv2pi = (1.0 / 10000.0 ** (np.arange(16, dtype=np.float64) / 16.0)) / (2 * np.pi)
    c["c_if2pi"] = np.broadcast_to(inv2pi.astype(np.float32), (128, 32, 16)).reshape(128, 512).copy()
    am = np.zeros((4, 128, 512), np.float32)
    for i, dd in enumerate((0, 128, 256, 384)):
        am[i] = np.where(np.arange(128)[:, None] + dd <= np.arange(512)[None, :], 0.0, -30000.0)
    c["c_amask"] = np.ascontiguousarray(am.reshape(4 * 128, 512))
    c["c_amask16"] = np.ascontiguousarray(am.reshape(4 * 128, 512)).astype(ml_dtypes.bfloat16)
    c["c_smask16"] = np.where(np.arange(128)[:, None] > np.arange(128)[None, :],
                              np.float32(30000.0), np.float32(0.0)).astype(ml_dtypes.bfloat16)
    ltm = np.zeros((4, 4), np.float32)
    for e in range(4):
        ltm[e, :e] = 1.0
    c["c_ltm"] = np.broadcast_to(ltm.reshape(1, 16), (128, 16)).copy()
    c["c_ones_f"] = np.ones((1, 128), np.float32)
    c["c_onescol"] = np.ones((128, 1), np.float32)
    return c


def _host_layout(inputs, core):
    g = lambda k: np.asarray(inputs[k])
    tr = lambda a: np.ascontiguousarray(np.asarray(a).T)
    d = {}
    d["x"] = np.ascontiguousarray(g("L0_attn_x")[core * BPC:(core + 1) * BPC].reshape(T, D))
    W_in = g("L0_attn_W_in")
    d["w_z"] = np.ascontiguousarray(W_in[:, :DI])
    d["w_x"] = np.ascontiguousarray(W_in[:, DI:2 * DI])
    d["w_bcd"] = np.ascontiguousarray(W_in[:, 2 * DI:])
    d["w_out"] = np.ascontiguousarray(g("L0_attn_W_out"))
    d["wq_t"] = tr(g("L0_ffn_wq")); d["wk_t"] = tr(g("L0_ffn_wk"))
    d["wv_t"] = tr(g("L0_ffn_wv")); d["wo_t"] = tr(g("L0_ffn_wo"))
    d["gw_t"] = tr(g("L0_ffn_gate_w"))
    d["e0_up_t"] = tr(g("L0_ffn_e0_up")); d["e0_dn_t"] = tr(g("L0_ffn_e0_down"))
    d["e1_w1_t"] = tr(g("L0_ffn_e1_w1")); d["e1_w2_t"] = tr(g("L0_ffn_e1_w2"))
    d["e1_dn_t"] = tr(g("L0_ffn_e1_down"))
    d["e2_l1_t"] = tr(g("L0_ffn_e2_l1")); d["e2_l2_t"] = tr(g("L0_ffn_e2_l2"))
    d["e2_l3_t"] = tr(g("L0_ffn_e2_l3")); d["e2_l4_t"] = tr(g("L0_ffn_e2_l4"))
    d["e3_dn_t"] = tr(g("L0_ffn_e3_down")); d["e3_up_t"] = tr(g("L0_ffn_e3_up"))
    d["e3_out_t"] = tr(g("L0_ffn_e3_out"))
    d["a_log_r"] = np.ascontiguousarray(g("L0_attn_A_log").reshape(1, NH))
    d["qn_r"] = np.ascontiguousarray(g("L0_ffn_qn_w").reshape(1, HD))
    d["kn_r"] = np.ascontiguousarray(g("L0_ffn_kn_w").reshape(1, HD))
    d["dtb_bc"] = np.ascontiguousarray(np.broadcast_to(g("L0_attn_dt_bias"), (128, NH)))
    d["dsk_bc"] = np.ascontiguousarray(np.broadcast_to(g("L0_attn_D_skip"), (128, NH)))
    d["n1t"] = np.ascontiguousarray(np.broadcast_to(g("L0_ffn_n1_w"), (128, D)))
    d["n1_dm"] = np.ascontiguousarray(g("L0_ffn_n1_w").reshape(8, 128).T)
    d["n2t"] = np.ascontiguousarray(np.broadcast_to(g("L0_ffn_n2_w"), (128, D)))
    d["qnt"] = np.ascontiguousarray(np.broadcast_to(np.tile(g("L0_ffn_qn_w"), HEADS), (128, D)))
    d["knt"] = np.ascontiguousarray(np.broadcast_to(np.tile(g("L0_ffn_kn_w"), HEADS), (128, D)))
    d["nw_c"] = np.ascontiguousarray(g("L0_attn_norm_weight").reshape(16, 128).T)
    return d


class Ctx:
    pass


def _declare(nc, dbg):
    cx = Ctx()
    cx.nc = nc
    cx.es = ExitStack()
    cx.SY, cx.TE, cx.VE, cx.SC, cx.GP = nc.sync, nc.tensor, nc.vector, nc.scalar, nc.gpsimd
    def din(name, shape, dt=F32):
        ap = nc.dram_tensor(name, list(shape), dt, kind="ExternalInput").ap()
        setattr(cx, name, ap)
    din("x", [T, D])
    din("w_z", [D, DI]); din("w_x", [D, DI]); din("w_bcd", [D, 160])
    din("w_out", [DI, D])
    din("wq_t", [D, D]); din("wk_t", [D, D]); din("wv_t", [D, D])
    din("wo_t", [D, D]); din("gw_t", [D, 4])
    din("e0_up_t", [D, HID]); din("e0_dn_t", [HID, D])
    din("e1_w1_t", [D, HID]); din("e1_w2_t", [D, HID]); din("e1_dn_t", [HID, D])
    din("e2_l1_t", [D, HID]); din("e2_l2_t", [HID, HID])
    din("e2_l3_t", [HID, HID]); din("e2_l4_t", [HID, D])
    din("e3_dn_t", [D, D]); din("e3_up_t", [D, HID]); din("e3_out_t", [HID, D])
    din("a_log_r", [1, NH]); din("qn_r", [1, HD]); din("kn_r", [1, HD])
    din("dtb_bc", [128, NH]); din("dsk_bc", [128, NH])
    din("n1t", [128, D]); din("n2t", [128, D]); din("qnt", [128, D]); din("knt", [128, D])
    din("n1_dm", [128, 8])
    din("nw_c", [128, 16])
    din("c_cum", [128, 128]); din("c_wmask", [128, 128], BF16)
    din("c_idt16", [128, 128], BF16); din("c_idt32", [128, 128])
    din("c_onehot", [32, 32 * 128]); din("c_if2pi", [128, 512])
    din("c_amask", [4 * 128, 512]); din("c_ltm", [128, 16])
    din("c_amask16", [4 * 128, 512], BF16)
    din("c_smask16", [128, 128], BF16)
    din("c_ones_f", [1, 128]); din("c_onescol", [128, 1])
    skind = "ExternalOutput" if dbg else "Internal"
    def dscr(name, shape, dt):
        ap = nc.dram_tensor(name, list(shape), dt, kind=skind).ap()
        setattr(cx, name, ap)
    dscr("xn_tm", [T, D], BF16)
    dscr("xn_tm32", [T, D], F32)
    dscr("xn_dm", [D, T], BF16)
    dscr("sz_tm", [T, DI], BF16)
    dscr("xs_tm", [T, DI], BF16)
    dscr("bc_tm", [T, 128], BF16)
    dscr("dtr_tm", [T, NH], F32)
    dscr("mamba_tm", [T, D], F32)
    dscr("q_dm", [D, T], BF16)
    dscr("k_dm", [D, T], BF16)
    dscr("v_tm", [T, D], BF16)
    dscr("ao_dm", [D, T], BF16)
    dscr("acc_tm", [T, D - MZ], F32)
    dscr("xf_dm", [D, T], BF16)
    dscr("wts_tm", [T, 4], F32)
    dscr("h2_dm", [HID, T], BF16)
    dscr("h3_dm", [HID, T], BF16)
    dscr("xpn_tm", [T, D], F32)
    cx.out = nc.dram_tensor("out", [T, D], F32, kind="ExternalOutput").ap()
    def psb(name, shape, dt):
        setattr(cx, name, cx.es.enter_context(nc.sbuf_tensor(name, list(shape), dt)))
    psb("P_idt16", [128, 128], BF16)
    psb("P_idt32", [128, 128], F32)
    psb("P_cum", [128, 128], F32)
    psb("P_wmask", [128, 128], BF16)
    psb("P_onehot", [32, 32 * 128], F32)
    psb("P_if2pi", [128, 512], F32)
    psb("P_amask16", [128, 4 * 512], BF16)
    psb("P_smask16", [128, 128], BF16)
    psb("P_ltm", [128, 16], F32)
    psb("P_ones_f", [1, 128], F32)
    psb("P_onescol", [128, 1], F32)
    psb("P_dtb", [128, NH], F32)
    psb("P_dsk", [128, NH], F32)
    psb("P_n1t", [128, D], BF16)
    psb("P_n2t", [128, D], BF16)
    psb("P_qnt", [128, D], BF16)
    psb("P_knt", [128, D], BF16)
    psb("P_nwc", [128, 16], F32)
    psb("P_A", [128, NH], F32)
    psb("P_negA", [128, NH], F32)
    psb("P_negc", [128, 1], F32)
    psb("P_tiny", [1, 512], F32)
    psb("P_rowA", [1, 128], F32)
    psb("P_rowNegA", [1, 128], F32)
    psb("P_rowNegc", [1, 128], F32)
    psb("P_const", [128, 2], F32)
    psb("P_zero", [128, 1], F32)
    nc.const_aps.register(F32, EPS_LN, cx.P_const[:, 0:1])
    nc.const_aps.register(F32, EPS_RMS, cx.P_const[:, 1:2])
    return cx


def _init_stage(cx):
    nc = cx.nc
    SY, TE, VE, SC, GP = cx.SY, cx.TE, cx.VE, cx.SC, cx.GP
    with nc.Block("init") as blk:
        def emit(_):
            ies = ExitStack()
            dL = CT(nc, ies, "i_dL"); vC = CT(nc, ies, "i_vC")
            sC = CT(nc, ies, "i_sC"); tC = CT(nc, ies, "i_tC")
            pb = ies.enter_context(nc.psum_tensor("i_pb", [128, 512], F32))
            for dst, src in [(cx.P_cum, cx.c_cum), (cx.P_wmask, cx.c_wmask),
                             (cx.P_idt16, cx.c_idt16), (cx.P_idt32, cx.c_idt32),
                             (cx.P_smask16, cx.c_smask16),
                             (cx.P_onehot, cx.c_onehot), (cx.P_if2pi, cx.c_if2pi),
                             (cx.P_ltm, cx.c_ltm), (cx.P_ones_f, cx.c_ones_f),
                             (cx.P_onescol, cx.c_onescol), (cx.P_dtb, cx.dtb_bc),
                             (cx.P_dsk, cx.dsk_bc), (cx.P_nwc, cx.nw_c)]:
                dL.inc(SY.dma_start(out=dst[:], in_=src[:]), 16)
            dL.inc(SY.dma_start(out=cx.P_amask16[:].rearrange("p (d n) -> p d n", n=512),
                                in_=cx.c_amask16.rearrange("(d p) n -> p d n", p=128)), 16)
            for dst, src in [(cx.P_n1t, cx.n1t), (cx.P_n2t, cx.n2t),
                             (cx.P_qnt, cx.qnt), (cx.P_knt, cx.knt)]:
                dL.inc(GP.dma_start(out=dst[:], in_=src[:]), 16)
            vC.inc(VE.memset(cx.P_const[:, 0:1], EPS_LN))
            vC.inc(VE.memset(cx.P_zero[:], 0.0))
            vC.inc(VE.memset(cx.P_const[:, 1:2], EPS_RMS))
            dL.inc(SY.dma_start(out=cx.P_tiny[0:1, 0:NH], in_=cx.a_log_r[:]), 16)
            dL.inc(SY.dma_start(out=cx.P_tiny[0:1, 32:96], in_=cx.qn_r[:]), 16)
            dL.inc(SY.dma_start(out=cx.P_tiny[0:1, 96:160], in_=cx.kn_r[:]), 16)
            VE.wait_ge(dL.h, dL.n); SC.wait_ge(dL.h, dL.n)
            sC.inc(SC.activation(cx.P_rowA[0:1, 0:NH], cx.P_tiny[0:1, 0:NH], AF.Exp))
            vC.inc(VE.reduce_max(cx.P_tiny[0:1, 160:161], cx.P_tiny[0:1, 32:96], axis=AX.X,
                                 apply_absolute_value=True))
            vC.inc(VE.reduce_max(cx.P_tiny[0:1, 161:162], cx.P_tiny[0:1, 96:160], axis=AX.X,
                                 apply_absolute_value=True))
            VE.wait_ge(vC.h, vC.n)
            vC.inc(VE.tensor_tensor(cx.P_tiny[0:1, 162:163], cx.P_tiny[0:1, 160:161],
                                    cx.P_tiny[0:1, 161:162], ALU.mult))
            VE.wait_ge(vC.h, vC.n)
            vC.inc(VE.tensor_scalar(cx.P_rowNegc[0:1, 0:1], cx.P_tiny[0:1, 162:163],
                                    -8.0, None, ALU.mult))
            SC.wait_ge(sC.h, sC.n)
            sC.inc(SC.mul(cx.P_rowNegA[0:1, 0:NH], cx.P_rowA[0:1, 0:NH], -1.0))
            TE.wait_ge(sC.h, sC.n); TE.wait_ge(vC.h, vC.n)
            tC.inc(TE.matmul(pb[:, 0:NH], cx.P_ones_f[:], cx.P_rowA[0:1, 0:NH],
                             start=True, stop=True))
            VE.wait_ge(tC.h, tC.n)
            vC.inc(VE.tensor_copy(cx.P_A[:], pb[:, 0:NH]))
            VE.wait_ge(vC.h, vC.n)
            TE.wait_ge(vC.h, vC.n)
            tC.inc(TE.matmul(pb[:, 0:NH], cx.P_ones_f[:], cx.P_rowNegA[0:1, 0:NH],
                             start=True, stop=True))
            VE.wait_ge(tC.h, tC.n)
            vC.inc(VE.tensor_copy(cx.P_negA[:], pb[:, 0:NH]))
            VE.wait_ge(vC.h, vC.n)
            TE.wait_ge(vC.h, vC.n)
            tC.inc(TE.matmul(pb[:, 0:1], cx.P_ones_f[:], cx.P_rowNegc[0:1, 0:1],
                             start=True, stop=True))
            VE.wait_ge(tC.h, tC.n)
            vC.inc(VE.tensor_copy(cx.P_negc[:], pb[:, 0:1]))
            SY.wait_ge(vC.h, vC.n)
            ies.close()
        blk.sync(emit)


def _s0(cx):
    nc = cx.nc
    SY, TE, VE, SC = cx.SY, cx.TE, cx.VE, cx.SC
    with nc.Block("s0") as blk:
        def emit(_):
            ies = ExitStack()
            dL = CT(nc, ies, "s0_dL"); dS = CT(nc, ies, "s0_dS")
            tC = CT(nc, ies, "s0_tC"); vC = CT(nc, ies, "s0_vC"); sC = CT(nc, ies, "s0_sC")
            xbuf = [ies.enter_context(nc.sbuf_tensor(f"s0_x{i}", [128, D], F32)) for i in range(2)]
            junk = ies.enter_context(nc.sbuf_tensor("s0_junk", [128, D], F32))
            st = ies.enter_context(nc.sbuf_tensor("s0_st", [128, 8], F32))
            xn16 = [ies.enter_context(nc.sbuf_tensor(f"s0_a{i}", [128, D], BF16)) for i in range(2)]
            xn32 = [ies.enter_context(nc.sbuf_tensor(f"s0_b{i}", [128, D], F32)) for i in range(2)]
            xpn = [ies.enter_context(nc.sbuf_tensor(f"s0_c{i}", [128, D], F32)) for i in range(2)]
            dmb = [ies.enter_context(nc.sbuf_tensor(f"s0_d{i}", [128, 8, 128], BF16)) for i in range(2)]
            ptp = [ies.enter_context(nc.psum_tensor(f"s0_p{i}", [128, 1024], BF16)) for i in range(2)]
            iter_v, iter_s = [], []
            tp_evac = [0, 0]
            for t in range(NT):
                bi = t % 2
                if t >= 2:
                    SY.wait_ge(vC.h, iter_v[t - 2])
                    SY.wait_ge(sC.h, iter_s[t - 2])
                dL.inc(SY.dma_start(out=xbuf[bi][:], in_=cx.x[t * 128:(t + 1) * 128, :]), 16)
                wL = dL.n
                VE.wait_ge(dL.h, wL); SC.wait_ge(dL.h, wL)
                vC.inc(VE.reduce_sum(st[:, 0:1], xbuf[bi][:], axis=AX.X))
                sC.inc(SC.activation(junk[:], xbuf[bi][:], AF.Square, accum_out=st[:, 1:2]))
                VE.wait_ge(sC.h, sC.n); VE.wait_ge(vC.h, vC.n)
                vC.inc(VE.tensor_scalar(st[:, 2:3], st[:, 0:1], 1.0 / D, None, ALU.mult))
                VE.wait_ge(vC.h, vC.n)
                vC.inc(VE.tensor_tensor(st[:, 3:4], st[:, 2:3], st[:, 2:3], ALU.mult))
                VE.wait_ge(vC.h, vC.n)
                vC.inc(VE.tensor_scalar(st[:, 4:5], st[:, 1:2], 1.0 / D, st[:, 3:4],
                                        ALU.mult, ALU.subtract))
                SC.wait_ge(vC.h, vC.n)
                sC.inc(SC.activation(st[:, 5:6], st[:, 4:5], AF.Ln, bias=EPS_LN))
                SC.wait_ge(sC.h, sC.n)
                sC.inc(SC.activation(st[:, 6:7], st[:, 5:6], AF.Exp, scale=-0.5))
                VE.wait_ge(sC.h, sC.n)
                vC.inc(VE.scalar_tensor_tensor(st[:, 7:8], st[:, 2:3], -1.0,
                                               st[:, 6:7], ALU.mult, ALU.mult))
                VE.wait_ge(vC.h, vC.n); SC.wait_ge(vC.h, vC.n)
                sC.inc(SC.activation(xn16[bi][:], xbuf[bi][:], AF.Identity,
                                     bias=st[:, 7:8], scale=st[:, 6:7]))
                vC.inc(VE.scalar_tensor_tensor(xn32[bi][:], xbuf[bi][:], st[:, 6:7],
                                               st[:, 7:8].broadcast_to([128, D]),
                                               ALU.mult, ALU.add))
                VE.wait_ge(vC.h, vC.n)
                vC.inc(VE.tensor_tensor(xpn[bi][:], xbuf[bi][:], xn32[bi][:], ALU.add))
                wS, wV = sC.n, vC.n
                TE.wait_ge(sC.h, wS)
                for k in range(8):
                    ks = k % 2
                    TE.wait_ge(vC.h, tp_evac[ks])
                    tC.inc(TE.matmul(ptp[ks][:, 0:128], xn16[bi][:, k * 128:(k + 1) * 128],
                                     cx.P_idt16[:], is_transpose=True, start=True, stop=True))
                    VE.wait_ge(tC.h, tC.n)
                    vC.inc(VE.tensor_copy(dmb[bi][:, k, :], ptp[ks][:, 0:128]))
                    tp_evac[ks] = vC.n
                SY.wait_ge(sC.h, wS)
                dS.inc(SY.dma_start(out=cx.xn_tm[t * 128:(t + 1) * 128, :], in_=xn16[bi][:]), 16)
                SY.wait_ge(vC.h, wV)
                dS.inc(SY.dma_start(out=cx.xn_tm32[t * 128:(t + 1) * 128, :], in_=xn32[bi][:]), 16)
                dS.inc(SY.dma_start(out=cx.xpn_tm[t * 128:(t + 1) * 128, :], in_=xpn[bi][:]), 16)
                SY.wait_ge(vC.h, vC.n)
                dS.inc(SY.dma_start(
                    out=cx.xn_dm.rearrange("(k p) T2 -> p k T2", p=128)[:, :, t * 128:(t + 1) * 128],
                    in_=dmb[bi][:]), 16)
                iter_v.append(vC.n); iter_s.append(sC.n)
            SY.wait_ge(dS.h, dS.n)
            ies.close()
        blk.sync(emit)


def _s1(cx):
    nc = cx.nc
    SY, TE, VE, SC, GP = cx.SY, cx.TE, cx.VE, cx.SC, cx.GP
    with nc.Block("s1") as blk:
        def emit(_):
            ies = ExitStack()
            dW = CT(nc, ies, "s1_dW"); dL = CT(nc, ies, "s1_dL"); dS = CT(nc, ies, "s1_dS")
            tC = CT(nc, ies, "s1_tC"); vC = CT(nc, ies, "s1_vC"); sC = CT(nc, ies, "s1_sC")
            wz_sb = ies.enter_context(nc.sbuf_tensor("s1_wz", [128, 8, DI], BF16))
            wx_sb = ies.enter_context(nc.sbuf_tensor("s1_wx", [128, 8, DI], BF16))
            wb_sb = ies.enter_context(nc.sbuf_tensor("s1_wb", [128, 8, 160], BF16))
            xb = [ies.enter_context(nc.sbuf_tensor(f"s1_x{i}", [128, 8, 128], BF16)) for i in range(2)]
            zb = [ies.enter_context(nc.sbuf_tensor(f"s1_z{i}", [128, DI], BF16)) for i in range(2)]
            sxb = [ies.enter_context(nc.sbuf_tensor(f"s1_s{i}", [128, DI], BF16)) for i in range(2)]
            bcb = [ies.enter_context(nc.sbuf_tensor(f"s1_b{i}", [128, 128], BF16)) for i in range(2)]
            dtb = [ies.enter_context(nc.sbuf_tensor(f"s1_t{i}", [128, 32], F32)) for i in range(2)]
            pz = [ies.enter_context(nc.psum_tensor(f"s1_p{i}", [128, 512], F32)) for i in range(8)]
            dW.inc(GP.dma_start(out=wz_sb[:], in_=cx.w_z.rearrange("(k p) n -> p k n", p=128)), 16)
            dW.inc(GP.dma_start(out=wx_sb[:], in_=cx.w_x.rearrange("(k p) n -> p k n", p=128)), 16)
            dW.inc(GP.dma_start(out=wb_sb[:], in_=cx.w_bcd.rearrange("(k p) n -> p k n", p=128)), 16)
            iter_v, iter_s = [], []
            for t in range(NT):
                bi = t % 2
                if t >= 2:
                    SY.wait_ge(vC.h, iter_v[t - 2])
                    SY.wait_ge(sC.h, iter_s[t - 2])
                dL.inc(SY.dma_start(
                    out=xb[bi][:],
                    in_=cx.xn_dm.rearrange("(k p) T2 -> p k T2", p=128)[:, :, t * 128:(t + 1) * 128]), 16)
                wL = dL.n
                TE.wait_ge(dL.h, wL)
                if t == 0:
                    TE.wait_ge(dW.h, dW.n)
                if t >= 2:
                    TE.wait_ge(sC.h, iter_s[t - 2])
                    TE.wait_ge(vC.h, iter_v[t - 2])
                for k in range(8):
                    for n in range(8):
                        src = wz_sb[:, k, (n % 4) * 512:(n % 4 + 1) * 512] if n < 4 else \
                              wx_sb[:, k, (n - 4) * 512:(n - 3) * 512]
                        tC.inc(TE.matmul(pz[n][:], xb[bi][:, k, :], src,
                                         start=(k == 0), stop=(k == 7)))
                wT = tC.n
                SC.wait_ge(tC.h, wT); VE.wait_ge(tC.h, wT)
                for n in range(4):
                    sC.inc(SC.activation(zb[bi][:, n * 512:(n + 1) * 512], pz[n][:], AF.Silu))
                for n in range(4):
                    vC.inc(VE.tensor_copy(sxb[bi][:, n * 512:(n + 1) * 512], pz[4 + n][:]))
                wE_s, wE_v = sC.n, vC.n
                TE.wait_ge(sC.h, wE_s)
                for k in range(8):
                    tC.inc(TE.matmul(pz[0][:, 0:160], xb[bi][:, k, :], wb_sb[:, k, :],
                                     start=(k == 0), stop=(k == 7)))
                VE.wait_ge(tC.h, tC.n)
                vC.inc(VE.tensor_copy(bcb[bi][:], pz[0][:, 0:128]))
                vC.inc(VE.tensor_copy(dtb[bi][:], pz[0][:, 128:160]))
                wB = vC.n
                SY.wait_ge(sC.h, wE_s)
                dS.inc(SY.dma_start(out=cx.sz_tm[t * 128:(t + 1) * 128, :], in_=zb[bi][:]), 16)
                SY.wait_ge(vC.h, wE_v)
                dS.inc(SY.dma_start(out=cx.xs_tm[t * 128:(t + 1) * 128, :], in_=sxb[bi][:]), 16)
                SY.wait_ge(vC.h, wB)
                dS.inc(SY.dma_start(out=cx.bc_tm[t * 128:(t + 1) * 128, :], in_=bcb[bi][:]), 16)
                dS.inc(SY.dma_start(out=cx.dtr_tm[t * 128:(t + 1) * 128, :], in_=dtb[bi][:]), 16)
                iter_v.append(vC.n); iter_s.append(sC.n)
            SY.wait_ge(dS.h, dS.n)
            ies.close()
        blk.sync(emit)


S2DBG = False
S3ABL = ("h", "tp", "v", "qk")
S2SEQ = False

def _s2(cx):
    nc = cx.nc
    SY, TE, VE, SC, GP = cx.SY, cx.TE, cx.VE, cx.SC, cx.GP
    dbgt = {}
    if S2DBG:
        for nm_, sh_, dt_ in [("dbg_rot0", [128, GRP * 128], BF16),
                              ("dbg_rot1", [128, GRP * 128], BF16),
                              ("dbg_sd", [128, GRP * 512], BF16),
                              ("dbg_cd", [128, GRP * 512], BF16),
                              ("dbg_Et", [128, 2 * GRP * 128], BF16),
                              ("dbg_Gm", [128, 2 * GRP * 128], BF16),
                              ("dbg_Wt", [128, 2 * GRP * 128], BF16),
                              ("dbg_bnm", [64, 2 * GRP * 128], BF16),
                              ("dbg_cm2", [64, 2 * GRP * 128], BF16),
                              ("dbg_Hst", [64, GRP * 128], BF16),
                              ("dbg_gdtT", [128, GRP * 32], F32),
                              ("dbg_lamT", [128, GRP * 32], F32),
                              ("dbg_cnm", [32, GRP * 128], F32)]:
            dbgt[nm_] = nc.dram_tensor(nm_, sh_, dt_, kind="ExternalOutput").ap()
    with nc.Block("s2") as blk:
        def emit(_):
            ies = ExitStack()
            dW = CT(nc, ies, "s2_dW"); dL = CT(nc, ies, "s2_dL"); dS = CT(nc, ies, "s2_dS")
            tC = CT(nc, ies, "s2_tC"); vC = CT(nc, ies, "s2_vC"); sC = CT(nc, ies, "s2_sC")
            gC = CT(nc, ies, "s2_gC")
            sbt = lambda n_, s, d: ies.enter_context(nc.sbuf_tensor(n_, list(s), d))
            wout_sb = sbt("s2_wo", [128, 16, D], BF16)
            dtr4 = sbt("s2_dtr", [128, GRP, 32], F32)
            bc4 = sbt("s2_bc", [128, GRP, 128], BF16)
            xs4 = sbt("s2_xs", [128, GRP, DI], BF16)
            sz4 = sbt("s2_sz", [128, GRP, DI], BF16)
            dt4 = sbt("s2_dt", [128, GRP, 32], F32)
            tmpa = sbt("s2_tmpa", [128, GRP, 32], F32)
            cumT = sbt("s2_cumT", [128, GRP * 32], F32)
            gcum = sbt("s2_gcum", [128, GRP * 32], F32)
            biasT = [sbt(f"s2_biasT{i}", [128, GRP * 32], F32) for i in range(2)]
            lamT = [sbt(f"s2_lamT{i}", [128, GRP * 32], F32) for i in range(2)]
            gdtT = [sbt(f"s2_gdtT{i}", [128, GRP * 32], F32) for i in range(2)]
            lndt = sbt("s2_lndt", [128, GRP * 32], F32)
            qa = sbt("s2_qa", [128, 512], F32)
            qi = sbt("s2_qi", [128, 512], I32)
            qf = sbt("s2_qf", [128, 512], F32)
            sdup = [sbt(f"s2_sd{i}", [128, GRP * 512], BF16) for i in range(2)]
            cdup = [sbt(f"s2_cd{i}", [128, GRP * 512], BF16) for i in range(2)]
            cnm = [sbt(f"s2_cnm{i}", [32, GRP * 128], F32) for i in range(2)]
            csrow = sbt("s2_cs", [1, 128], F32)
            offr = sbt("s2_off", [1, 160], F32)
            acdr = sbt("s2_acd", [1, 128], F32)
            decr = sbt("s2_dec", [1, 128], F32)
            dec_bc = [sbt(f"s2_decb{i}", [64, 128], F32) for i in range(2)]
            acd_bc = sbt("s2_acdb", [128, 128], F32)
            H = sbt("s2_H", [64, DI], BF16)
            rot = [[sbt(f"s2_rot{i}{j}", [128, GRP, 128], BF16) for j in range(2)]
                   for i in range(2)]
            rtmp = [[sbt(f"s2_rtmp{i}{j}", [128, GRP, 32], F32) for j in range(2)]
                    for i in range(2)]
            bgdt = [sbt(f"s2_bg{i}", [128, 2, GRP, 64], BF16) for i in range(2)]
            cltm = [sbt(f"s2_cl{i}", [128, GRP, 128], BF16) for i in range(2)]
            bnm = [sbt(f"s2_bn{i}", [64, 2, GRP, 128], BF16) for i in range(2)]
            cnm2 = [sbt(f"s2_cm{i}", [64, 2, GRP, 128], BF16) for i in range(2)]
            clnm = [sbt(f"s2_cn{i}", [64, 2, GRP, 128], BF16) for i in range(2)]
            Et = [sbt(f"s2_E{i}", [128, 2, GRP, 128], BF16) for i in range(2)]
            Wt = [sbt(f"s2_W{i}", [128, 2, GRP, 128], BF16) for i in range(2)]
            Hst = [sbt(f"s2_Hs{i}", [64, GRP, 128], BF16) for i in range(2)]
            ybuf = sbt("s2_y", [128, GRP, DI], BF16)
            yst = sbt("s2_yst", [128, GRP, 4], F32)
            ydmb = sbt("s2_ydm", [128, 16, 128], BF16)
            mo = sbt("s2_mo", [128, D], F32)
            pst = lambda n_, s, d=F32: ies.enter_context(nc.psum_tensor(n_, list(s), d))
            tp1 = pst("s2_tp1", [128, 1024], BF16)
            tp2 = pst("s2_tp2", [128, 1024], BF16)
            pM = [pst(f"s2_pm{i}", [128, 512]) for i in range(2)]
            pG = [pst(f"s2_pg{i}", [128, 512]) for i in range(2)]
            pY = [pst(f"s2_py{i}", [128, 512]) for i in range(2)]
            dW.inc(GP.dma_start(out=wout_sb[:], in_=cx.w_out.rearrange("(k p) n -> p k n", p=128)), 16)
            VE.wait_ge(dW.h, dW.n)
            for k in range(16):
                vC.inc(VE.tensor_scalar(wout_sb[:, k, :], wout_sb[:, k, :],
                                        cx.P_nwc[:, k:k + 1], None, ALU.mult))
            wout_ready = vC.n
            mk = {"tp1": 0, "tp2": 0, "pMv": [0, 0], "pMs": [0, 0],
                  "W": [0, 0], "Yg": [0, 0], "Yev": [0, 0],
                  "Hc": [0, 0], "rr": [0, 0], "ybr": 0, "pMd": [0, 0]}

            def phaseA(b, g, gb, first, out_marks):
                g0 = b * L + g * GRP * Q
                if g == 0:
                    VE.wait_ge(tC.h, mk["Yg"][0]); VE.wait_ge(tC.h, mk["Yg"][1])
                    vC.inc(VE.memset(H[:], 0.0))
                    vC.inc(VE.memset(offr[:], 0.0))
                    VE.wait_ge(vC.h, vC.n)
                wP = (vC.n, sC.n, tC.n, gC.n)
                SY.wait_ge(vC.h, wP[0]); SY.wait_ge(sC.h, wP[1])
                dL.inc(SY.dma_start(out=dtr4[:], in_=cx.dtr_tm[g0:g0 + 512, :]
                                    .rearrange("(c p) h -> p c h", p=128)), 16)
                wDTR = dL.n
                SY.wait_ge(tC.h, wP[2]); SY.wait_ge(gC.h, wP[3])
                dL.inc(SY.dma_start(out=bc4[:], in_=cx.bc_tm[g0:g0 + 512, :]
                                    .rearrange("(c p) n -> p c n", p=128)), 16)
                dL.inc(SY.dma_start(out=xs4[:], in_=cx.xs_tm[g0:g0 + 512, :]
                                    .rearrange("(c p) n -> p c n", p=128)), 16)
                if first:
                    dL.inc(SY.dma_start(out=sz4[:], in_=cx.sz_tm[g0:g0 + 512, :]
                                        .rearrange("(c p) n -> p c n", p=128)), 16)
                wL = dL.n
                VE.wait_ge(dL.h, wDTR); SC.wait_ge(dL.h, wDTR)
                vC.inc(VE.tensor_tensor(dt4[:], dtr4[:],
                                        cx.P_dtb[:].unsqueeze(1).broadcast_to([128, GRP, 32]),
                                        ALU.add))
                VE.wait_ge(vC.h, vC.n); SC.wait_ge(vC.h, vC.n)
                sC.inc(SC.activation(tmpa[:], dt4[:], AF.Abs))
                SC.wait_ge(sC.h, sC.n)
                sC.inc(SC.activation(tmpa[:], tmpa[:], AF.Exp, scale=-1.0))
                SC.wait_ge(sC.h, sC.n)
                sC.inc(SC.activation(tmpa[:], tmpa[:], AF.Ln, bias=1.0))
                VE.wait_ge(sC.h, sC.n)
                vC.inc(VE.tensor_scalar(dt4[:], dt4[:], 0.0, None, ALU.max))
                VE.wait_ge(vC.h, vC.n)
                vC.inc(VE.tensor_tensor(dt4[:], dt4[:], tmpa[:], ALU.add))
                VE.wait_ge(vC.h, vC.n)
                wDT = vC.n
                SC.wait_ge(vC.h, wDT)
                sC.inc(SC.activation(lndt[:].rearrange("p (c h) -> p c h", h=32),
                                     dt4[:], AF.Ln))
                yield
                TE.wait_ge(vC.h, max(wDT, mk["W"][0]))
                for c in range(GRP):
                    tC.inc(TE.matmul(pG[0][:, c * 32:(c + 1) * 32], cx.P_cum[:],
                                     dt4[:, c, :], start=(c == 0), stop=(c == GRP - 1)))
                VE.wait_ge(tC.h, tC.n)
                vC.inc(VE.tensor_copy(cumT[:], pG[0][:, 0:128]))
                VE.wait_ge(vC.h, vC.n)
                TE.wait_ge(vC.h, vC.n)
                for c in range(GRP):
                    tC.inc(TE.matmul(pG[0][0:1, c * 32:(c + 1) * 32],
                                     cx.P_onescol[:], dt4[:, c, :],
                                     start=(c == 0), stop=(c == GRP - 1)))
                VE.wait_ge(tC.h, tC.n)
                vC.inc(VE.tensor_copy(csrow[:], pG[0][0:1, 0:128]))
                VE.wait_ge(vC.h, vC.n)
                vC.inc(VE.tensor_copy(offr[0:1, 0:32], offr[0:1, 128:160]))
                for c in range(1, GRP):
                    VE.wait_ge(vC.h, vC.n)
                    vC.inc(VE.tensor_tensor(offr[0:1, c * 32:(c + 1) * 32],
                                            offr[0:1, (c - 1) * 32:c * 32],
                                            csrow[0:1, (c - 1) * 32:c * 32], ALU.add))
                VE.wait_ge(vC.h, vC.n)
                vC.inc(VE.tensor_tensor(offr[0:1, 128:160], offr[0:1, 96:128],
                                        csrow[0:1, 96:128], ALU.add))
                vC.inc(VE.tensor_tensor(acdr[:].rearrange("o (c h) -> o c h", h=32),
                                        csrow[:].rearrange("o (c h) -> o c h", h=32),
                                        cx.P_rowA[0:1, 0:32].unsqueeze(1)
                                        .broadcast_to([1, GRP, 32]), ALU.mult))
                VE.wait_ge(vC.h, vC.n)
                wOFF = vC.n
                SC.wait_ge(vC.h, wOFF)
                sC.inc(SC.activation(decr[:], acdr[:], AF.Exp, scale=-1.0))
                yield
                TE.wait_ge(vC.h, wOFF)
                tC.inc(TE.matmul(pG[0][:, 0:128], cx.P_ones_f[:], offr[0:1, 0:128],
                                 start=True, stop=True))
                VE.wait_ge(tC.h, tC.n)
                vC.inc(VE.tensor_tensor(gcum[:], cumT[:], pG[0][:, 0:128], ALU.add))
                VE.wait_ge(vC.h, vC.n)
                TE.wait_ge(vC.h, vC.n)
                tC.inc(TE.matmul(pG[0][:, 0:128], cx.P_ones_f[:], acdr[:],
                                 start=True, stop=True))
                VE.wait_ge(tC.h, tC.n)
                vC.inc(VE.tensor_copy(acd_bc[:], pG[0][:, 0:128]))
                VE.wait_ge(vC.h, vC.n)
                TE.wait_ge(vC.h, vC.n); TE.wait_ge(sC.h, sC.n)
                tC.inc(TE.matmul(pG[0][0:64, 0:128], cx.P_ones_f[0:1, 0:64], decr[:],
                                 start=True, stop=True))
                VE.wait_ge(tC.h, tC.n)
                vC.inc(VE.tensor_copy(dec_bc[gb][:], pG[0][0:64, 0:128]))
                VE.wait_ge(vC.h, vC.n)
                vC.inc(VE.tensor_tensor(biasT[gb][:].rearrange("p (c h) -> p c h", h=32),
                                        cumT[:].rearrange("p (c h) -> p c h", h=32),
                                        cx.P_A[:, 0:32].unsqueeze(1)
                                        .broadcast_to([128, GRP, 32]), ALU.mult))
                VE.wait_ge(vC.h, vC.n); SC.wait_ge(vC.h, vC.n)
                sC.inc(SC.activation(lamT[gb][:], biasT[gb][:], AF.Exp, scale=-1.0))
                VE.wait_ge(sC.h, sC.n)
                vC.inc(VE.tensor_tensor(biasT[gb][:], biasT[gb][:], lndt[:], ALU.add))
                VE.wait_ge(vC.h, vC.n)
                vC.inc(VE.tensor_tensor(gdtT[gb][:], biasT[gb][:], acd_bc[:], ALU.subtract))
                VE.wait_ge(vC.h, vC.n); SC.wait_ge(vC.h, vC.n)
                sC.inc(SC.activation(gdtT[gb][:], gdtT[gb][:], AF.Exp))
                yield
                for c in range(GRP):
                    VE.wait_ge(sC.h, sC.n)
                    vC.inc(VE.tensor_tensor(
                        qa[:].rearrange("p (h j) -> p h j", j=16),
                        gcum[:, c * 32:(c + 1) * 32].unsqueeze(2).broadcast_to([128, 32, 16]),
                        cx.P_if2pi[:].rearrange("p (h j) -> p h j", j=16), ALU.mult))
                    VE.wait_ge(vC.h, vC.n)
                    vC.inc(VE.tensor_copy(qi[:], qa[:]))
                    VE.wait_ge(vC.h, vC.n)
                    vC.inc(VE.tensor_copy(qf[:], qi[:]))
                    VE.wait_ge(vC.h, vC.n)
                    vC.inc(VE.tensor_tensor(qf[:], qa[:], qf[:], ALU.subtract))
                    VE.wait_ge(vC.h, vC.n)
                    SC.wait_ge(vC.h, vC.n)
                    sC.inc(SC.activation(sdup[gb][:, c * 512:(c + 1) * 512],
                                         qf[:], AF.Sin, scale=TWO_PI))
                    vC.inc(VE.tensor_scalar(qa[:], qa[:], 0.25, None, ALU.add))
                    VE.wait_ge(vC.h, vC.n)
                    vC.inc(VE.tensor_copy(qi[:], qa[:]))
                    VE.wait_ge(vC.h, vC.n)
                    VE.wait_ge(sC.h, sC.n)
                    vC.inc(VE.tensor_copy(qf[:], qi[:]))
                    VE.wait_ge(vC.h, vC.n)
                    vC.inc(VE.tensor_tensor(qf[:], qa[:], qf[:], ALU.subtract))
                    VE.wait_ge(vC.h, vC.n)
                    SC.wait_ge(vC.h, vC.n)
                    sC.inc(SC.activation(cdup[gb][:, c * 512:(c + 1) * 512],
                                         qf[:], AF.Sin, scale=TWO_PI))
                    if c % 2 == 1:
                        yield
                TE.wait_ge(vC.h, vC.n)
                for c in range(GRP):
                    tC.inc(TE.matmul(pG[0][0:32, c * 128:(c + 1) * 128],
                                     cumT[:, c * 32:(c + 1) * 32], cx.P_idt32[:],
                                     is_transpose=True, start=(c == 0), stop=(c == GRP - 1)))
                VE.wait_ge(tC.h, tC.n)
                vC.inc(VE.tensor_copy(cnm[gb][:], pG[0][0:32, 0:512]))
                mk["W"][0] = vC.n
                out_marks[:] = [vC.n, sC.n, wL]

            def rot_head(h, slot, gb):
                hb = h % 2
                co2 = cdup[gb][:].rearrange("p (c h2 j) -> p c h2 j", h2=32, j=16)[:, :, h, :] \
                    .unsqueeze(2).broadcast_to([128, GRP, 2, 16])
                si2 = sdup[gb][:].rearrange("p (c h2 j) -> p c h2 j", h2=32, j=16)[:, :, h, :] \
                    .unsqueeze(2).broadcast_to([128, GRP, 2, 16])
                bview = bc4[:].rearrange("p c (g n) -> p c g n", g=2)
                rview = rot[slot][hb][:].rearrange("p c (g n) -> p c g n", g=2)
                bev, bod = bview[:, :, :, 0:32:2], bview[:, :, :, 1:32:2]
                rev, rod = rview[:, :, :, 0:32:2], rview[:, :, :, 1:32:2]
                rt = rtmp[slot][hb][:].rearrange("p c (g j) -> p c g j", g=2)
                vC.inc(VE.tensor_copy(rview[:, :, :, 32:64], bview[:, :, :, 32:64]))
                vC.inc(VE.tensor_tensor(rev, bev, co2, ALU.mult))
                vC.inc(VE.tensor_tensor(rt, bod, si2, ALU.mult))
                VE.wait_ge(vC.h, vC.n)
                vC.inc(VE.tensor_tensor(rev, rev, rt, ALU.add))
                vC.inc(VE.tensor_tensor(rod, bod, co2, ALU.mult))
                vC.inc(VE.tensor_tensor(rt, bev, si2, ALU.mult))
                VE.wait_ge(vC.h, vC.n)
                vC.inc(VE.tensor_tensor(rod, rod, rt, ALU.subtract))
                VE.wait_ge(vC.h, vC.n)
                vC.inc(VE.tensor_tensor(
                    bgdt[slot][:, hb, :, :], rot[slot][hb][:, :, 0:64],
                    gdtT[gb][:].rearrange("p (c h2) -> p c h2", h2=32)[:, :, h:h + 1]
                        .broadcast_to([128, GRP, 64]), ALU.mult))
                vC.inc(VE.tensor_tensor(
                    cltm[slot][:, :, hb * 64:(hb + 1) * 64], rot[slot][hb][:, :, 64:128],
                    lamT[gb][:].rearrange("p (c h2) -> p c h2", h2=32)[:, :, h:h + 1]
                        .broadcast_to([128, GRP, 64]), ALU.mult))
                return vC.n

            dbg_ctx = [None]

            def pair_job(pj, wA_v, wA_s, wL, gb):
                slot = pj % 2
                h0 = pj * 2
                VE.wait_ge(tC.h, mk["rr"][slot])
                VE.wait_ge(sC.h, wA_s)
                VE.wait_ge(gC.h, mk["ybr"])
                VE.wait_ge(dL.h, wL)
                rot_head(h0, slot, gb)
                wrot = rot_head(h0 + 1, slot, gb)
                yield
                TE.wait_ge(sC.h, mk["tp1"])
                TE.wait_ge(vC.h, wrot)
                for hb in range(2):
                    for c in range(GRP):
                        tC.inc(TE.matmul(tp1[:, (hb * 4 + c) * 128:(hb * 4 + c + 1) * 128],
                                         rot[slot][hb][:, c, :], cx.P_idt16[:],
                                         is_transpose=True, start=(hb == 0 and c == 0),
                                         stop=(hb == 1 and c == GRP - 1)))
                wT1 = tC.n
                TE.wait_ge(sC.h, mk["tp2"])
                for c in range(GRP):
                    tC.inc(TE.matmul(tp2[:, c * 128:(c + 1) * 128],
                                     cltm[slot][:, c, :], cx.P_idt16[:],
                                     is_transpose=True, start=(c == 0),
                                     stop=(c == GRP - 1)))
                wT2 = tC.n
                SC.wait_ge(tC.h, wT1)
                sC.inc(SC.activation(bnm[slot][:].rearrange("q a b2 n -> q (a b2 n)"),
                                     tp1[0:64, :], AF.Identity))
                sC.inc(SC.activation(cnm2[slot][:].rearrange("q a b2 n -> q (a b2 n)"),
                                     tp1[64:128, :], AF.Identity))
                mk["tp1"] = sC.n
                SC.wait_ge(tC.h, wT2)
                sC.inc(SC.activation(clnm[slot][:, 0].rearrange("q b2 n -> q (b2 n)"),
                                     tp2[0:64, 0:512], AF.Identity))
                sC.inc(SC.activation(clnm[slot][:, 1].rearrange("q b2 n -> q (b2 n)"),
                                     tp2[64:128, 0:512], AF.Identity))
                mk["tp2"] = sC.n
                wTev = sC.n
                yield
                for hb in range(2):
                    h = h0 + hb
                    TE.wait_ge(vC.h, max(mk["pMv"][slot], wA_v))
                    TE.wait_ge(sC.h, mk["pMs"][slot])
                    TE.wait_ge(dS.h, mk["pMd"][slot])
                    for c in range(GRP):
                        tC.inc(TE.matmul(pM[slot][:, c * 128:(c + 1) * 128],
                                         cx.P_onehot[:, h * 128:(h + 1) * 128],
                                         cnm[gb][:, c * 128:(c + 1) * 128],
                                         start=(c == 0), stop=False))
                    for c in range(GRP):
                        tC.inc(TE.matmul(pM[slot][:, c * 128:(c + 1) * 128],
                                         cx.P_idt16[:], cx.P_smask16[:],
                                         start=False, stop=(c == GRP - 1)))
                    wR = tC.n
                    SC.wait_ge(tC.h, wR)
                    SC.wait_ge(vC.h, mk["W"][slot])
                    for c in range(GRP):
                        sC.inc(SC.activation(Et[slot][:, hb, c, :],
                                             pM[slot][:, c * 128:(c + 1) * 128], AF.Exp,
                                             scale=cx.P_negA[:, h:h + 1],
                                             bias=biasT[gb][:, c * 32 + h:c * 32 + h + 1]))
                    mk["pMs"][slot] = sC.n
                wE = sC.n
                yield
                TE.wait_ge(sC.h, wTev)
                for hb in range(2):
                    TE.wait_ge(vC.h, mk["W"][slot] if hb == 0 else vC.n)
                    for c in range(GRP):
                        tC.inc(TE.matmul(pG[slot][:, c * 128:(c + 1) * 128],
                                         bnm[slot][:, hb, c, :], cnm2[slot][:, hb, c, :],
                                         start=(c == 0), stop=(c == GRP - 1)))
                    wG = tC.n
                    VE.wait_ge(tC.h, max(wG, mk["Yg"][slot] if hb == 0 else 0))
                    if hb == 0:
                        VE.wait_ge(sC.h, wE)
                    vC.inc(VE.scalar_tensor_tensor(
                        Wt[slot][:, hb], Et[slot][:, hb], BIG,
                        pG[slot][:].rearrange("p (x n) -> p x n", n=128),
                        ALU.min, ALU.mult))
                mk["W"][slot] = vC.n
                yield
                TE.wait_ge(sC.h, wE)
                TE.wait_ge(vC.h, mk["Hc"][slot])
                for c in range(GRP):
                    for hb in range(2):
                        h = h0 + hb
                        ofs = (c * 2 + hb) * 64
                        tC.inc(TE.matmul(pM[slot][0:64, ofs:ofs + 64],
                                         bgdt[slot][:, hb, c, :],
                                         xs4[:, c, h * 64:(h + 1) * 64],
                                         start=(c == 0 and hb == 0),
                                         stop=(c == GRP - 1 and hb == 1)))
                wS = tC.n
                hsl = H[:, h0 * 64:(h0 + 2) * 64]
                VE.wait_ge(tC.h, max(wS, mk["Yg"][slot]))
                vC.inc(VE.tensor_copy(Hst[slot][:, 0, :], hsl))
                r3 = lambda ap_: ap_.rearrange("q (a x) -> q a x", a=2)
                for c in range(GRP):
                    dv = dec_bc[gb][0:64, c * 32 + h0:c * 32 + h0 + 2].unsqueeze(2) \
                        .broadcast_to([64, 2, 64])
                    dst = Hst[slot][:, c + 1, :] if c < GRP - 1 else hsl
                    VE.wait_ge(vC.h, vC.n)
                    vC.inc(VE.tensor_tensor(r3(dst), r3(Hst[slot][:, c, :]), dv, ALU.mult))
                    VE.wait_ge(vC.h, vC.n)
                    vC.inc(VE.tensor_tensor(dst, dst, pM[slot][0:64, c * 128:(c + 1) * 128],
                                            ALU.add))
                mk["Hc"][slot] = vC.n
                mk["pMv"][slot] = vC.n
                yield
                TE.wait_ge(vC.h, max(mk["Yev"][slot], mk["W"][slot], mk["Hc"][slot]))
                TE.wait_ge(sC.h, wTev)
                for hb in range(2):
                    h = h0 + hb
                    for c in range(GRP):
                        base = (hb * 4 + c) * 64
                        tC.inc(TE.matmul(pY[slot][:, base:base + 64], Wt[slot][:, hb, c, :],
                                         xs4[:, c, h * 64:(h + 1) * 64],
                                         start=(hb == 0 and c == 0), stop=False))
                        tC.inc(TE.matmul(pY[slot][:, base:base + 64], clnm[slot][:, hb, c, :],
                                         Hst[slot][:, c, hb * 64:(hb + 1) * 64],
                                         start=False, stop=(hb == 1 and c == GRP - 1)))
                mk["Yg"][slot] = tC.n
                mk["rr"][slot] = tC.n
                wY = tC.n
                VE.wait_ge(tC.h, wY)
                VE.wait_ge(gC.h, mk["ybr"])
                for hb in range(2):
                    h = h0 + hb
                    vC.inc(VE.scalar_tensor_tensor(
                        ybuf[:, :, h * 64:(h + 1) * 64],
                        xs4[:, :, h * 64:(h + 1) * 64],
                        cx.P_dsk[:, h:h + 1],
                        pY[slot][:, hb * 256:hb * 256 + 256].rearrange("p (c x) -> p c x", x=64),
                        ALU.mult, ALU.add))
                mk["Yev"][slot] = vC.n
                if S2DBG and dbg_ctx[0] == (0, 0) and pj == 0:
                    SY.wait_ge(vC.h, vC.n); SY.wait_ge(sC.h, sC.n); SY.wait_ge(gC.h, gC.n)
                    for nm_, src_ in [("dbg_rot0", rot[slot][0][:].rearrange("p a b2 -> p (a b2)")),
                                      ("dbg_rot1", rot[slot][1][:].rearrange("p a b2 -> p (a b2)")),
                                      ("dbg_sd", sdup[:]), ("dbg_cd", cdup[:]),
                                      ("dbg_Et", Et[slot][:].rearrange("p a b2 n -> p (a b2 n)")),
                                      ("dbg_Gm", Gm[slot][:].rearrange("p a b2 n -> p (a b2 n)")),
                                      ("dbg_Wt", Wt[slot][:].rearrange("p a b2 n -> p (a b2 n)")),
                                      ("dbg_bnm", bnm[slot][:].rearrange("q a b2 n -> q (a b2 n)")),
                                      ("dbg_cm2", cnm2[slot][:].rearrange("q a b2 n -> q (a b2 n)")),
                                      ("dbg_Hst", Hst[slot][:].rearrange("q a x -> q (a x)")),
                                      ("dbg_gdtT", gdtT[:]), ("dbg_lamT", lamT[:]),
                                      ("dbg_cnm", cnm[:])]:
                        if nm_ in dbgt:
                            dS.inc(SY.dma_start(out=dbgt[nm_][:], in_=src_), 16)
                    SY.wait_ge(dS.h, dS.n)

            def phaseC(b, g, first, nxt_g0):
                g0 = b * L + g * GRP * Q
                GP.wait_ge(vC.h, vC.n)
                gC.inc(GP.tensor_tensor(ybuf[:], ybuf[:], sz4[:], ALU.mult))
                if nxt_g0 is not None:
                    SY.wait_ge(gC.h, gC.n)
                    dL.inc(SY.dma_start(out=sz4[:], in_=cx.sz_tm[nxt_g0:nxt_g0 + 512, :]
                                        .rearrange("(c p) n -> p c n", p=128)), 16)
                SC.wait_ge(gC.h, gC.n)
                for c in range(GRP):
                    for hf in range(2):
                        SC.wait_ge(sC.h, sC.n)
                        sC.inc(SC.activation(ydmb[:, hf * 8:hf * 8 + 8, :]
                                             .rearrange("p a x -> p (a x)"),
                                             ybuf[:, c, hf * 1024:(hf + 1) * 1024],
                                             AF.Square, accum_out=yst[:, c, hf:hf + 1]))
                        yield
                SC.wait_ge(sC.h, sC.n)
                VE.wait_ge(sC.h, sC.n)
                vC.inc(VE.tensor_tensor(yst[:, :, 0:1], yst[:, :, 0:1],
                                        yst[:, :, 1:2], ALU.add))
                SC.wait_ge(vC.h, vC.n)
                for c in range(GRP):
                    sC.inc(SC.activation(yst[:, c, 1:2], yst[:, c, 0:1], AF.Ln,
                                         scale=1.0 / DI, bias=EPS_LN))
                SC.wait_ge(sC.h, sC.n)
                for c in range(GRP):
                    sC.inc(SC.activation(yst[:, c, 2:3], yst[:, c, 1:2], AF.Exp, scale=-0.5))
                GP.wait_ge(sC.h, sC.n)
                gC.inc(GP.tensor_tensor(ybuf[:], ybuf[:],
                                        yst[:, :, 2:3].broadcast_to([128, GRP, DI]),
                                        ALU.mult))
                wYGN = gC.n
                mk["ybr"] = gC.n
                if first:
                    TE.wait_ge(vC.h, wout_ready)
                yield
                for c in range(GRP):
                    TE.wait_ge(gC.h, wYGN)
                    TE.wait_ge(sC.h, mk["tp1"])
                    for k in range(8):
                        tC.inc(TE.matmul(tp1[:, k * 128:(k + 1) * 128],
                                         ybuf[:, c, k * 128:(k + 1) * 128],
                                         cx.P_idt16[:], is_transpose=True,
                                         start=(k == 0), stop=(k == 7)))
                    wTa = tC.n
                    TE.wait_ge(sC.h, mk["tp2"])
                    for k in range(8):
                        tC.inc(TE.matmul(tp2[:, k * 128:(k + 1) * 128],
                                         ybuf[:, c, (k + 8) * 128:(k + 9) * 128],
                                         cx.P_idt16[:], is_transpose=True,
                                         start=(k == 0), stop=(k == 7)))
                    wTb = tC.n
                    SC.wait_ge(tC.h, wTa)
                    sC.inc(SC.activation(ydmb[:, 0:8, :].rearrange("p a x -> p (a x)"),
                                         tp1[:, :], AF.Identity))
                    mk["tp1"] = sC.n
                    SC.wait_ge(tC.h, wTb)
                    sC.inc(SC.activation(ydmb[:, 8:16, :].rearrange("p a x -> p (a x)"),
                                         tp2[:, :], AF.Identity))
                    mk["tp2"] = sC.n
                    wYD = sC.n
                    TE.wait_ge(sC.h, max(wYD, mk["pMs"][0], mk["pMs"][1]))
                    TE.wait_ge(vC.h, max(mk["pMv"][0], mk["pMv"][1],
                                         mk["Hc"][0], mk["Hc"][1]))
                    for k in range(16):
                        for n in range(2):
                            tC.inc(TE.matmul(pM[n][:, 0:512], ydmb[:, k, :],
                                             wout_sb[:, k, n * 512:(n + 1) * 512],
                                             start=(k == 0), stop=(k == 15)))
                    VE.wait_ge(tC.h, tC.n)
                    VE.wait_ge(dS.h, mk["pMd"][0])
                    vC.inc(VE.tensor_copy(mo[:, 0:512], pM[0][:, 0:512]))
                    vC.inc(VE.tensor_copy(mo[:, 512:1024], pM[1][:, 0:512]))
                    mk["pMv"] = [vC.n, vC.n]
                    r0 = g0 + c * 128
                    SY.wait_ge(vC.h, vC.n)
                    dS.inc(SY.dma_start(out=cx.mamba_tm[r0:r0 + 128, :], in_=mo[:]), 16)
                    mk["pMd"] = [dS.n, dS.n]
                    yield

            seq = [(b, g) for b in range(BPC) for g in range(NG)]
            marks = {}

            def drain(gens):
                gens = [g_ for g_ in gens if g_ is not None]
                while gens:
                    for g_ in list(gens):
                        try:
                            next(g_)
                        except StopIteration:
                            gens.remove(g_)

            marks[0] = []
            drain([phaseA(seq[0][0], seq[0][1], 0, True, marks[0])])
            for i, (b, g) in enumerate(seq):
                dbg_ctx[0] = (b, g)
                gb = i % 2
                wA_v, wA_s, wL = marks[i]
                nxt = [2, 3]
                alive = [pair_job(0, wA_v, wA_s, wL, gb),
                         pair_job(1, wA_v, wA_s, wL, gb)]
                while any(g_ is not None for g_ in alive):
                    for sl in range(2):
                        gj = alive[sl]
                        if gj is None:
                            continue
                        try:
                            next(gj)
                        except StopIteration:
                            if nxt[sl] < HEADS:
                                alive[sl] = pair_job(nxt[sl], wA_v, wA_s, wL, gb)
                                nxt[sl] += 2
                            else:
                                alive[sl] = None
                tail = []
                if i + 1 < len(seq):
                    marks[i + 1] = []
                    tail.append(phaseA(seq[i + 1][0], seq[i + 1][1], (i + 1) % 2,
                                       False, marks[i + 1]))
                    nb, ng = seq[i + 1]
                    nxt_g0 = nb * L + ng * GRP * Q
                else:
                    nxt_g0 = None
                tail.append(phaseC(b, g, i == 0, nxt_g0))
                drain(tail)
            SY.wait_ge(dS.h, dS.n)
            ies.close()
        blk.sync(emit)


def _s3(cx):
    nc = cx.nc
    SY, TE, VE, SC, GP = cx.SY, cx.TE, cx.VE, cx.SC, cx.GP
    with nc.Block("s3") as blk:
        def emit(_):
            ies = ExitStack()
            dW = CT(nc, ies, "s3_dW"); dL = CT(nc, ies, "s3_dL"); dS = CT(nc, ies, "s3_dS")
            tC = CT(nc, ies, "s3_tC"); vC = CT(nc, ies, "s3_vC"); sC = CT(nc, ies, "s3_sC")
            sbt = lambda n_, s, d: ies.enter_context(nc.sbuf_tensor(n_, list(s), d))
            wq_sb = sbt("s3_wq", [128, 8, D], BF16)
            wk_sb = sbt("s3_wk", [128, 8, D], BF16)
            wv_sb = sbt("s3_wv", [128, 8, D], BF16)
            n1sb = sbt("s3_n1", [128, 8], F32)
            NW = 3
            xd = [sbt(f"s3_xd{i}", [128, 8, 128], BF16) for i in range(NW)]
            xb = [sbt(f"s3_xb{i}", [128, D], BF16) for i in range(NW)]
            st = [sbt(f"s3_st{i}", [128, 40], F32) for i in range(NW)]
            sq = [sbt(f"s3_sq{i}", [128, D], F32) for i in range(NW)]
            junk = [sbt(f"s3_j{i}", [128, D], F32) for i in range(NW)]
            qtm = [sbt(f"s3_qt{i}", [128, D], BF16) for i in range(NW)]
            qdmb = [[sbt(f"s3_qd{i}{w}", [128, 8, 128], BF16) for w in range(2)]
                    for i in range(NW)]
            vtm = [sbt(f"s3_v{i}", [128, D], BF16) for i in range(NW)]
            ptp = [ies.enter_context(nc.psum_tensor(f"s3_p{i}", [128, 1024], BF16))
                   for i in range(2)]
            pmm = [[ies.enter_context(nc.psum_tensor(f"s3_m{i}{n}", [128, 512], F32))
                    for n in range(2)] for i in range(NW)]
            dW.inc(GP.dma_start(out=wq_sb[:], in_=cx.wq_t.rearrange("(k p) n -> p k n", p=128)), 16)
            dW.inc(GP.dma_start(out=wk_sb[:], in_=cx.wk_t.rearrange("(k p) n -> p k n", p=128)), 16)
            dW.inc(GP.dma_start(out=wv_sb[:], in_=cx.wv_t.rearrange("(k p) n -> p k n", p=128)), 16)
            dW.inc(SY.dma_start(out=n1sb[:], in_=cx.n1_dm[:]), 16)
            VE.wait_ge(dW.h, dW.n)
            for wsb in (wq_sb, wk_sb, wv_sb):
                for k in range(8):
                    vC.inc(VE.tensor_scalar(wsb[:, k, :], wsb[:, k, :],
                                            n1sb[:, k:k + 1], None, ALU.mult))
            wN1 = vC.n
            mk_tp = [0, 0]
            pm_mark = [[0, 0] for _ in range(3)]
            done = {}

            def tile_job(t):
                bi = t % 3
                pm = pmm[bi]
                if t >= 3:
                    dv, ds_, dd = done[t - 3]
                    SY.wait_ge(vC.h, dv); SY.wait_ge(sC.h, ds_); SY.wait_ge(dS.h, dd)
                dL.inc(SY.dma_start(
                    out=xd[bi][:],
                    in_=cx.xn_dm.rearrange("(k p) T2 -> p k T2", p=128)
                        [:, :, t * 128:(t + 1) * 128]), 16)
                dL.inc(SY.dma_start(out=xb[bi][:], in_=cx.xn_tm[t * 128:(t + 1) * 128, :]), 16)
                wL = dL.n
                SC.wait_ge(dL.h, wL)
                sC.inc(SC.activation(junk[bi][:], xb[bi][:], AF.Square,
                                     accum_out=st[bi][:, 0:1]))
                SC.wait_ge(sC.h, sC.n)
                sC.inc(SC.activation(st[bi][:, 1:2], st[bi][:, 0:1], AF.Ln,
                                     scale=1.0 / D, bias=EPS_RMS))
                SC.wait_ge(sC.h, sC.n)
                sC.inc(SC.activation(st[bi][:, 2:3], st[bi][:, 1:2], AF.Exp, scale=-0.5))
                wRS = sC.n
                yield
                TE.wait_ge(dL.h, wL)
                if t < 2:
                    TE.wait_ge(vC.h, wN1)
                for n in range(2):
                    TE.wait_ge(vC.h, pm_mark[bi][n])
                    for k in range(8):
                        tC.inc(TE.matmul(pm[n][:], xd[bi][:, k, :],
                                         wv_sb[:, k, n * 512:(n + 1) * 512],
                                         start=(k == 0), stop=(k == 7)))
                wV = tC.n
                VE.wait_ge(tC.h, wV); VE.wait_ge(sC.h, wRS)
                vC.inc(VE.tensor_scalar(vtm[bi][:, 0:512], pm[0][:],
                                        st[bi][:, 2:3], None, ALU.mult))
                vC.inc(VE.tensor_scalar(vtm[bi][:, 512:1024], pm[1][:],
                                        st[bi][:, 2:3], None, ALU.mult))
                pm_mark[bi][0] = vC.n; pm_mark[bi][1] = vC.n
                SY.wait_ge(vC.h, vC.n)
                dS.inc(SY.dma_start(out=cx.v_tm[t * 128:(t + 1) * 128, :], in_=vtm[bi][:]), 16)
                yield
                for w, (wsb, ntile, dst) in enumerate(
                        ((wq_sb, cx.P_qnt, cx.q_dm), (wk_sb, cx.P_knt, cx.k_dm))):
                    for n in range(2):
                        TE.wait_ge(vC.h, pm_mark[bi][n])
                        for k in range(8):
                            tC.inc(TE.matmul(pm[n][:], xd[bi][:, k, :],
                                             wsb[:, k, n * 512:(n + 1) * 512],
                                             start=(k == 0), stop=(k == 7)))
                    wQ = tC.n
                    SC.wait_ge(tC.h, wQ)
                    sC.inc(SC.activation(sq[bi][:, 0:512], pm[0][:], AF.Square))
                    sC.inc(SC.activation(sq[bi][:, 512:1024], pm[1][:], AF.Square))
                    VE.wait_ge(sC.h, sC.n)
                    for hh in range(16):
                        vC.inc(VE.reduce_sum(st[bi][:, 8 + hh:9 + hh],
                                             sq[bi][:, hh * 64:(hh + 1) * 64], axis=AX.X))
                    VE.wait_ge(vC.h, vC.n)
                    SC.wait_ge(vC.h, vC.n)
                    sC.inc(SC.activation(st[bi][:, 24:40], st[bi][:, 8:24], AF.Ln,
                                         scale=1.0 / HD, bias=EPS_RMS))
                    SC.wait_ge(sC.h, sC.n)
                    sC.inc(SC.activation(st[bi][:, 24:40], st[bi][:, 24:40], AF.Exp,
                                         scale=-0.5))
                    VE.wait_ge(sC.h, sC.n)
                    vC.inc(VE.tensor_tensor(
                        qtm[bi][:, 0:512].rearrange("p (h x) -> p h x", x=64),
                        pm[0][:].rearrange("p (h x) -> p h x", x=64),
                        st[bi][:, 24:32].unsqueeze(2).broadcast_to([128, 8, 64]), ALU.mult))
                    vC.inc(VE.tensor_tensor(
                        qtm[bi][:, 512:1024].rearrange("p (h x) -> p h x", x=64),
                        pm[1][:].rearrange("p (h x) -> p h x", x=64),
                        st[bi][:, 32:40].unsqueeze(2).broadcast_to([128, 8, 64]), ALU.mult))
                    pm_mark[bi][0] = vC.n; pm_mark[bi][1] = vC.n
                    VE.wait_ge(vC.h, vC.n)
                    vC.inc(VE.tensor_tensor(qtm[bi][:], qtm[bi][:], ntile[:], ALU.mult))
                    wQT = vC.n
                    yield
                    for k in range(8):
                        ks = k % 2
                        TE.wait_ge(vC.h, max(mk_tp[ks], wQT))
                        tC.inc(TE.matmul(ptp[ks][:, 0:128], qtm[bi][:, k * 128:(k + 1) * 128],
                                         cx.P_idt16[:], is_transpose=True,
                                         start=True, stop=True))
                        VE.wait_ge(tC.h, tC.n)
                        vC.inc(VE.tensor_copy(qdmb[bi][w][:, k, :], ptp[ks][:, 0:128]))
                        mk_tp[ks] = vC.n
                    SY.wait_ge(vC.h, vC.n)
                    dS.inc(SY.dma_start(
                        out=dst.rearrange("(k p) T2 -> p k T2", p=128)
                            [:, :, t * 128:(t + 1) * 128],
                        in_=qdmb[bi][w][:]), 16)
                    yield
                done[t] = (vC.n, sC.n, dS.n)

            nxt = [3, 4, 5]
            alive = [tile_job(0), tile_job(1), tile_job(2)]
            while any(g_ is not None for g_ in alive):
                for sl in range(3):
                    gj = alive[sl]
                    if gj is None:
                        continue
                    try:
                        next(gj)
                    except StopIteration:
                        if nxt[sl] < NT:
                            alive[sl] = tile_job(nxt[sl])
                            nxt[sl] += 3
                        else:
                            alive[sl] = None
            SY.wait_ge(dS.h, dS.n)
            ies.close()
        blk.sync(emit)


def _s4(cx):
    nc = cx.nc
    SY, TE, VE, SC = cx.SY, cx.TE, cx.VE, cx.SC
    with nc.Block("s4") as blk:
        def emit(_):
            ies = ExitStack()
            dL = CT(nc, ies, "s4_dL"); dS = CT(nc, ies, "s4_dS")
            tC = CT(nc, ies, "s4_tC"); vC = CT(nc, ies, "s4_vC"); sC = CT(nc, ies, "s4_sC")
            NS = 6
            qb_ = [ies.enter_context(nc.sbuf_tensor(f"s4_q{i}", [64, L], BF16)) for i in range(2)]
            kb_ = [ies.enter_context(nc.sbuf_tensor(f"s4_k{i}", [64, L], BF16)) for i in range(2)]
            vb_ = [ies.enter_context(nc.sbuf_tensor(f"s4_v{i}", [128, 16, 65], BF16)) for i in range(2)]
            pb_ = [ies.enter_context(nc.sbuf_tensor(f"s4_pb{i}", [128, 512], BF16)) for i in range(NS)]
            o_un = [ies.enter_context(nc.sbuf_tensor(f"s4_ou{i}", [65, 4 * 512], F32)) for i in range(2)]
            rcp = [ies.enter_context(nc.sbuf_tensor(f"s4_rc{i}", [1, 4 * 512], F32)) for i in range(2)]
            rbc = [ies.enter_context(nc.sbuf_tensor(f"s4_rb{i}", [64, 512], F32)) for i in range(2)]
            ob = [ies.enter_context(nc.sbuf_tensor(f"s4_ob{i}", [64, 512], BF16)) for i in range(2)]
            pS = [ies.enter_context(nc.psum_tensor(f"s4_s{i}", [128, 512], F32)) for i in range(NS)]
            pO = [ies.enter_context(nc.psum_tensor(f"s4_o{i}", [65, 512], F32)) for i in range(2)]
            vC.inc(VE.memset(vb_[0][:, :, 64:65], 1.0))
            vC.inc(VE.memset(vb_[1][:, :, 64:65], 1.0))
            sh = {"ks": 0, "mk_S": [0] * NS, "mk_Sv": [0] * NS, "mk_pb": [0] * NS,
                  "mk_O": [0, 0], "mk_ou": [0, 0], "mk_ob": [0, 0], "done": {}}

            def head_job(it):
                b, h = divmod(it, HEADS)
                bi = it % 2
                if it >= 2:
                    SY.wait_ge(tC.h, sh["done"].get(it - 2, 0))
                dL.inc(SY.dma_start(out=qb_[bi][:],
                                    in_=cx.q_dm[h * 64:(h + 1) * 64, b * L:(b + 1) * L]), 16)
                dL.inc(SY.dma_start(out=kb_[bi][:],
                                    in_=cx.k_dm[h * 64:(h + 1) * 64, b * L:(b + 1) * L]), 16)
                dL.inc(SY.dma_start(
                    out=vb_[bi][:, :, 0:64],
                    in_=cx.v_tm[b * L:(b + 1) * L, :]
                        .rearrange("(kt p) (h2 j) -> p kt h2 j", p=128, j=64)[:, :, h, :]), 16)
                wL = dL.n
                first = True
                kslot = [0]
                for qblk in range(4):
                    nkt = 4 * qblk + 4
                    exp_mark = {}
                    s_mark = {}
                    s_slot = {}
                    def emit_S(kt):
                        ks = 3 * bi + (kslot[0] % 3)
                        kslot[0] += 1
                        s_slot[kt] = ks
                        dd = kt - 4 * qblk
                        TE.wait_ge(sC.h, sh["mk_S"][ks])
                        TE.wait_ge(vC.h, sh["mk_Sv"][ks])
                        if first:
                            TE.wait_ge(dL.h, wL)
                        tC.inc(TE.matmul(pS[ks][:], kb_[bi][:, kt * 128:(kt + 1) * 128],
                                         qb_[bi][:, qblk * 512:(qblk + 1) * 512],
                                         start=True, stop=(dd < 0)))
                        if dd >= 0:
                            tC.inc(TE.matmul(pS[ks][:], cx.P_idt16[:],
                                             cx.P_amask16[:, dd * 512:(dd + 1) * 512],
                                             start=False, stop=True))
                        s_mark[kt] = tC.n
                    def emit_exp(kt):
                        ks = s_slot[kt]
                        wS = s_mark[kt]
                        SC.wait_ge(tC.h, max(wS, sh["mk_pb"][ks]))
                        sC.inc(SC.activation(pb_[ks][:], pS[ks][:], AF.Exp,
                                             scale=0.125, bias=cx.P_negc[:, 0:1]))
                        sh["mk_S"][ks] = sC.n
                        exp_mark[kt] = sC.n
                    emit_S(0)
                    if nkt > 1:
                        emit_S(1)
                    if nkt > 2:
                        emit_S(2)
                    emit_exp(0)
                    first = False
                    for kt in range(nkt):
                        if kt + 3 < nkt:
                            emit_S(kt + 3)
                        if kt + 1 < nkt:
                            emit_exp(kt + 1)
                        TE.wait_ge(sC.h, exp_mark[kt])
                        if kt == 0:
                            TE.wait_ge(vC.h, sh["mk_O"][bi])
                        tC.inc(TE.matmul(pO[bi][:], vb_[bi][:, kt, :], pb_[s_slot[kt]][:],
                                         start=(kt == 0), stop=(kt == nkt - 1)))
                        sh["mk_pb"][s_slot[kt]] = tC.n
                        if kt % 3 == 2:
                            yield
                    wPV = tC.n
                    VE.wait_ge(tC.h, wPV)
                    if qblk == 0:
                        VE.wait_ge(vC.h, sh["mk_ou"][bi])
                    vC.inc(VE.tensor_copy(o_un[bi][:, qblk * 512:(qblk + 1) * 512], pO[bi][:]))
                    sh["mk_O"][bi] = vC.n
                    yield
                SC.wait_ge(vC.h, sh["mk_O"][bi])
                sC.inc(SC.activation(rcp[bi][0:1, :], o_un[bi][64:65, :], AF.Ln))
                SC.wait_ge(sC.h, sC.n)
                sC.inc(SC.activation(rcp[bi][0:1, :], rcp[bi][0:1, :], AF.Exp, scale=-1.0))
                wRC = sC.n
                for qblk in range(4):
                    st = 3 * bi + (qblk % 2)
                    TE.wait_ge(sC.h, max(wRC, sh["mk_S"][st]))
                    TE.wait_ge(vC.h, sh["mk_Sv"][st])
                    tC.inc(TE.matmul(pS[st][0:64, :], cx.P_ones_f[0:1, 0:64],
                                     rcp[bi][0:1, qblk * 512:(qblk + 1) * 512],
                                     start=True, stop=True))
                    VE.wait_ge(tC.h, tC.n)
                    vC.inc(VE.tensor_copy(rbc[bi][:], pS[st][0:64, :]))
                    sh["mk_Sv"][st] = vC.n
                    VE.wait_ge(vC.h, vC.n)
                    VE.wait_ge(dS.h, sh["mk_ob"][bi])
                    vC.inc(VE.scalar_tensor_tensor(ob[bi][:],
                                                   o_un[bi][0:64, qblk * 512:(qblk + 1) * 512],
                                                   1.0, rbc[bi][:], ALU.mult, ALU.mult))
                    SY.wait_ge(vC.h, vC.n)
                    dS.inc(SY.dma_start(
                        out=cx.ao_dm[h * 64:(h + 1) * 64,
                                     b * L + qblk * 512:b * L + (qblk + 1) * 512],
                        in_=ob[bi][:]), 16)
                    sh["mk_ob"][bi] = dS.n
                    yield
                sh["mk_ou"][bi] = vC.n
                sh["done"][it] = tC.n

            for it in range(0, BPC * HEADS, 2):
                g1, g2 = head_job(it), head_job(it + 1)
                alive = [g1, g2]
                while alive:
                    for gjob in list(alive):
                        try:
                            next(gjob)
                        except StopIteration:
                            alive.remove(gjob)
            SY.wait_ge(dS.h, dS.n)
            ies.close()
        blk.sync(emit)


def _s4b(cx):
    nc = cx.nc
    SY, TE, VE, GP = cx.SY, cx.TE, cx.VE, cx.GP
    with nc.Block("s4b") as blk:
        def emit(_):
            ies = ExitStack()
            dW = CT(nc, ies, "s4b_dW"); dL = CT(nc, ies, "s4b_dL"); dS = CT(nc, ies, "s4b_dS")
            tC = CT(nc, ies, "s4b_tC"); vC = CT(nc, ies, "s4b_vC")
            wo_sb = ies.enter_context(nc.sbuf_tensor("s4b_wo", [128, 8, D - MZ], BF16))
            ab = [ies.enter_context(nc.sbuf_tensor(f"s4b_a{i}", [128, 8, 128], BF16)) for i in range(2)]
            ob = [ies.enter_context(nc.sbuf_tensor(f"s4b_o{i}", [128, D - MZ], F32)) for i in range(2)]
            pm = [ies.enter_context(nc.psum_tensor(f"s4b_p{i}", [128, 512], F32)) for i in range(2)]
            dW.inc(GP.dma_start(out=wo_sb[:],
                                in_=cx.wo_t[:, MZ:].rearrange("(k p) n -> p k n", p=128)), 16)
            iter_v = []
            for t in range(NT):
                bi = t % 2
                if t >= 2:
                    SY.wait_ge(vC.h, iter_v[t - 2])
                dL.inc(SY.dma_start(
                    out=ab[bi][:],
                    in_=cx.ao_dm.rearrange("(k p) T2 -> p k T2", p=128)[:, :, t * 128:(t + 1) * 128]), 16)
                wL = dL.n
                TE.wait_ge(dL.h, wL)
                if t == 0:
                    TE.wait_ge(dW.h, dW.n)
                if t >= 2:
                    TE.wait_ge(vC.h, iter_v[t - 2])
                for n in range(2):
                    w0, w1 = n * 512, min((n + 1) * 512, D - MZ)
                    for k in range(8):
                        tC.inc(TE.matmul(pm[n][:, 0:w1 - w0], ab[bi][:, k, :],
                                         wo_sb[:, k, w0:w1], start=(k == 0), stop=(k == 7)))
                VE.wait_ge(tC.h, tC.n)
                vC.inc(VE.tensor_copy(ob[bi][:, 0:512], pm[0][:]))
                vC.inc(VE.tensor_copy(ob[bi][:, 512:768], pm[1][:, 0:256]))
                SY.wait_ge(vC.h, vC.n)
                dS.inc(SY.dma_start(out=cx.acc_tm[t * 128:(t + 1) * 128, :], in_=ob[bi][:]), 16)
                iter_v.append(vC.n)
            SY.wait_ge(dS.h, dS.n)
            ies.close()
        blk.sync(emit)


def _s5(cx):
    nc = cx.nc
    SY, TE, VE, SC, GP = cx.SY, cx.TE, cx.VE, cx.SC, cx.GP
    with nc.Block("s5") as blk:
        def emit(_):
            ies = ExitStack()
            dW = CT(nc, ies, "s5_dW"); dL = CT(nc, ies, "s5_dL"); dS = CT(nc, ies, "s5_dS")
            tC = CT(nc, ies, "s5_tC"); vC = CT(nc, ies, "s5_vC"); sC = CT(nc, ies, "s5_sC")
            sbt = lambda n_, s, d: ies.enter_context(nc.sbuf_tensor(n_, list(s), d))
            gw_sb = sbt("s5_gw", [128, 8, 4], BF16)
            wo_sb = sbt("s5_wo", [128, 8, D - MZ], BF16)
            ab = [sbt(f"s5_ab{i}", [128, 8, 128], BF16) for i in range(2)]
            ob = [sbt(f"s5_ob{i}", [128, D - MZ], F32) for i in range(2)]
            pm4 = [ies.enter_context(nc.psum_tensor(f"s5_pm{i}", [128, 512], F32)) for i in range(2)]
            xnb = [sbt(f"s5_x{i}", [128, D], F32) for i in range(2)]
            x2 = [sbt(f"s5_x2{i}", [128, D], F32) for i in range(2)]
            st = [sbt(f"s5_st{i}", [128, 4], F32) for i in range(2)]
            junk = [sbt(f"s5_j{i}", [128, D], F32) for i in range(2)]
            xf16 = [sbt(f"s5_xf{i}", [128, D], BF16) for i in range(2)]
            xfd = [sbt(f"s5_d{i}", [128, 8, 128], BF16) for i in range(2)]
            gate = [sbt(f"s5_g{i}", [128, 24], F32) for i in range(2)]
            gm = [sbt(f"s5_gm{i}", [128, 32], F32) for i in range(2)]
            wt4 = [sbt(f"s5_w{i}", [128, 4], F32) for i in range(2)]
            ptp = [ies.enter_context(nc.psum_tensor(f"s5_p{i}", [128, 1024], BF16)) for i in range(2)]
            pg = [ies.enter_context(nc.psum_tensor(f"s5_pg{i}", [128, 512], F32)) for i in range(2)]
            dW.inc(GP.dma_start(out=gw_sb[:], in_=cx.gw_t.rearrange("(k p) n -> p k n", p=128)), 16)
            dW.inc(GP.dma_start(out=wo_sb[:],
                                in_=cx.wo_t[:, MZ:].rearrange("(k p) n -> p k n", p=128)), 16)
            mk_tp = [0, 0]
            mk_pg = [0, 0]
            mk_pm = [0, 0]
            done = {}

            def tile_job(t):
                bi = t % 2
                if t >= 2:
                    dv, ds_ = done[t - 2]
                    SY.wait_ge(vC.h, dv); SY.wait_ge(sC.h, ds_)
                dL.inc(SY.dma_start(
                    out=ab[bi][:],
                    in_=cx.ao_dm.rearrange("(k p) T2 -> p k T2", p=128)
                        [:, :, t * 128:(t + 1) * 128]), 16)
                dL.inc(SY.dma_start(out=xnb[bi][:], in_=cx.xn_tm32[t * 128:(t + 1) * 128, :]), 16)
                wL = dL.n
                TE.wait_ge(dL.h, wL)
                if t == 0:
                    TE.wait_ge(dW.h, dW.n)
                for n in range(2):
                    w0, w1 = n * 512, min((n + 1) * 512, D - MZ)
                    TE.wait_ge(vC.h, mk_pm[n])
                    for k in range(8):
                        tC.inc(TE.matmul(pm4[n][:, 0:w1 - w0], ab[bi][:, k, :],
                                         wo_sb[:, k, w0:w1], start=(k == 0), stop=(k == 7)))
                wAO = tC.n
                VE.wait_ge(tC.h, wAO)
                vC.inc(VE.tensor_copy(ob[bi][:, 0:512], pm4[0][:]))
                vC.inc(VE.tensor_copy(ob[bi][:, 512:768], pm4[1][:, 0:256]))
                mk_pm[0] = vC.n; mk_pm[1] = vC.n
                SY.wait_ge(vC.h, vC.n)
                dS.inc(SY.dma_start(out=cx.acc_tm[t * 128:(t + 1) * 128, :], in_=ob[bi][:]), 16)
                yield
                VE.wait_ge(dL.h, wL)
                vC.inc(VE.tensor_copy(x2[bi][:, 0:MZ], xnb[bi][:, 0:MZ]))
                vC.inc(VE.tensor_tensor(x2[bi][:, MZ:], xnb[bi][:, MZ:], ob[bi][:], ALU.add))
                VE.wait_ge(vC.h, vC.n)
                SC.wait_ge(vC.h, vC.n)
                sC.inc(SC.activation(junk[bi][:], x2[bi][:], AF.Square, accum_out=st[bi][:, 0:1]))
                SC.wait_ge(sC.h, sC.n)
                sC.inc(SC.activation(st[bi][:, 1:2], st[bi][:, 0:1], AF.Ln,
                                     scale=1.0 / D, bias=EPS_RMS))
                SC.wait_ge(sC.h, sC.n)
                sC.inc(SC.activation(st[bi][:, 2:3], st[bi][:, 1:2], AF.Exp, scale=-0.5))
                VE.wait_ge(sC.h, sC.n)
                vC.inc(VE.tensor_scalar(xf16[bi][:], x2[bi][:], st[bi][:, 2:3], None, ALU.mult))
                VE.wait_ge(vC.h, vC.n)
                vC.inc(VE.tensor_tensor(xf16[bi][:], xf16[bi][:], cx.P_n2t[:], ALU.mult))
                wXF = vC.n
                yield
                TE.wait_ge(vC.h, wXF)
                for k in range(8):
                    ks = k % 2
                    TE.wait_ge(vC.h, mk_tp[ks])
                    tC.inc(TE.matmul(ptp[ks][:, 0:128], xf16[bi][:, k * 128:(k + 1) * 128],
                                     cx.P_idt16[:], is_transpose=True, start=True, stop=True))
                    VE.wait_ge(tC.h, tC.n)
                    vC.inc(VE.tensor_copy(xfd[bi][:, k, :], ptp[ks][:, 0:128]))
                    mk_tp[ks] = vC.n
                wXD = vC.n
                SY.wait_ge(vC.h, wXD)
                dS.inc(SY.dma_start(
                    out=cx.xf_dm.rearrange("(k p) T2 -> p k T2", p=128)[:, :, t * 128:(t + 1) * 128],
                    in_=xfd[bi][:]), 16)
                TE.wait_ge(vC.h, wXD)
                TE.wait_ge(sC.h, mk_pg[bi])
                for k in range(8):
                    tC.inc(TE.matmul(pg[bi][:, 0:4], xfd[bi][:, k, :], gw_sb[:, k, :],
                                     start=(k == 0), stop=(k == 7)))
                wG = tC.n
                yield
                VE.wait_ge(tC.h, wG)
                vC.inc(VE.reduce_max(st[bi][:, 3:4], pg[bi][:, 0:4], axis=AX.X, negate=True))
                VE.wait_ge(vC.h, vC.n)
                SC.wait_ge(vC.h, vC.n)
                sC.inc(SC.activation(gate[bi][:, 0:4], pg[bi][:, 0:4], AF.Exp,
                                     bias=st[bi][:, 3:4], accum_out=gate[bi][:, 4:5]))
                mk_pg[bi] = sC.n
                VE.wait_ge(sC.h, sC.n)
                vC.inc(VE.reciprocal(gate[bi][:, 5:6], gate[bi][:, 4:5]))
                VE.wait_ge(vC.h, vC.n)
                vC.inc(VE.tensor_scalar(gate[bi][:, 0:4], gate[bi][:, 0:4],
                                        gate[bi][:, 5:6], None, ALU.mult))
                VE.wait_ge(vC.h, vC.n)
                vC.inc(VE.tensor_tensor(gm[bi][:, 0:16].rearrange("p (e j) -> p e j", j=4),
                                        gate[bi][:, 0:4].unsqueeze(1).broadcast_to([128, 4, 4]),
                                        gate[bi][:, 0:4].unsqueeze(2).broadcast_to([128, 4, 4]),
                                        ALU.is_gt))
                vC.inc(VE.tensor_tensor(gm[bi][:, 16:32].rearrange("p (e j) -> p e j", j=4),
                                        gate[bi][:, 0:4].unsqueeze(1).broadcast_to([128, 4, 4]),
                                        gate[bi][:, 0:4].unsqueeze(2).broadcast_to([128, 4, 4]),
                                        ALU.is_equal))
                VE.wait_ge(vC.h, vC.n)
                vC.inc(VE.tensor_tensor(gm[bi][:, 16:32], gm[bi][:, 16:32], cx.P_ltm[:], ALU.mult))
                VE.wait_ge(vC.h, vC.n)
                vC.inc(VE.tensor_tensor(gm[bi][:, 0:16], gm[bi][:, 0:16], gm[bi][:, 16:32], ALU.add))
                VE.wait_ge(vC.h, vC.n)
                vC.inc(VE.tensor_reduce(gate[bi][:, 8:12].unsqueeze(2),
                                        gm[bi][:, 0:16].rearrange("p (e j) -> p e j", j=4),
                                        AX.X, ALU.add))
                VE.wait_ge(vC.h, vC.n)
                vC.inc(VE.tensor_scalar(gate[bi][:, 12:16], gate[bi][:, 8:12], 1.5, None, ALU.is_le))
                VE.wait_ge(vC.h, vC.n)
                vC.inc(VE.tensor_tensor(gate[bi][:, 16:20], gate[bi][:, 0:4],
                                        gate[bi][:, 12:16], ALU.mult))
                VE.wait_ge(vC.h, vC.n)
                vC.inc(VE.reduce_sum(gate[bi][:, 20:21], gate[bi][:, 16:20], axis=AX.X))
                VE.wait_ge(vC.h, vC.n)
                vC.inc(VE.tensor_scalar(gate[bi][:, 21:22], gate[bi][:, 20:21], 1e-8, None, ALU.add))
                VE.wait_ge(vC.h, vC.n)
                vC.inc(VE.reciprocal(gate[bi][:, 22:23], gate[bi][:, 21:22]))
                VE.wait_ge(vC.h, vC.n)
                vC.inc(VE.tensor_scalar(wt4[bi][:], gate[bi][:, 16:20],
                                        gate[bi][:, 22:23], None, ALU.mult))
                SY.wait_ge(vC.h, vC.n)
                dS.inc(SY.dma_start(out=cx.wts_tm[t * 128:(t + 1) * 128, :], in_=wt4[bi][:]), 16)
                done[t] = (vC.n, sC.n)

            nxt = [2, 3]
            alive = [tile_job(0), tile_job(1)]
            while any(g is not None for g in alive):
                for sl in range(2):
                    gj = alive[sl]
                    if gj is None:
                        continue
                    try:
                        next(gj)
                    except StopIteration:
                        if nxt[sl] < NT:
                            alive[sl] = tile_job(nxt[sl])
                            nxt[sl] += 2
                        else:
                            alive[sl] = None
            SY.wait_ge(dS.h, dS.n)
            ies.close()
        blk.sync(emit)


def _s6(cx, name, kind):
    nc = cx.nc
    SY, TE, VE, SC, GP = cx.SY, cx.TE, cx.VE, cx.SC, cx.GP
    with nc.Block(name) as blk:
        def emit(_):
            ies = ExitStack()
            dW = CT(nc, ies, f"{name}_dW"); dL = CT(nc, ies, f"{name}_dL")
            dS = CT(nc, ies, f"{name}_dS")
            tC = CT(nc, ies, f"{name}_tC"); vC = CT(nc, ies, f"{name}_vC")
            sC = CT(nc, ies, f"{name}_sC")
            sbt = lambda n_, s, d: ies.enter_context(nc.sbuf_tensor(name + n_, list(s), d))
            pst = lambda n_, s: ies.enter_context(nc.psum_tensor(name + n_, list(s), F32))
            eidx = {"e0": 0, "e1": 1, "e2b": 2, "e3": 3}.get(kind)
            rr = lambda ap_: ap_.rearrange("(k p) n -> p k n", p=128)
            if kind == "e0":
                wup = sbt("w_up", [128, 8, HID], BF16)
                wdn = sbt("w_dn", [128, 16, D - MZ], BF16)
                dW.inc(GP.dma_start(out=wup[:], in_=rr(cx.e0_up_t)), 16)
                dW.inc(GP.dma_start(out=wdn[:], in_=rr(cx.e0_dn_t[:, MZ:])), 16)
            elif kind == "e1":
                wup = sbt("w_u1", [128, 8, HID], BF16)
                wu2 = sbt("w_u2", [128, 8, HID], BF16)
                wdn = sbt("w_dn", [128, 16, D - MZ], BF16)
                dW.inc(GP.dma_start(out=wup[:], in_=rr(cx.e1_w1_t)), 16)
                dW.inc(GP.dma_start(out=wu2[:], in_=rr(cx.e1_w2_t)), 16)
                dW.inc(GP.dma_start(out=wdn[:], in_=rr(cx.e1_dn_t[:, MZ:])), 16)
            elif kind == "e3":
                wd0 = sbt("w_d0", [128, 8, D], BF16)
                wup = sbt("w_up", [128, 8, HID], BF16)
                wdn = sbt("w_dn", [128, 16, D - MZ], BF16)
                dW.inc(GP.dma_start(out=wd0[:], in_=rr(cx.e3_dn_t)), 16)
                dW.inc(GP.dma_start(out=wup[:], in_=rr(cx.e3_up_t)), 16)
                dW.inc(GP.dma_start(out=wdn[:], in_=rr(cx.e3_out_t[:, MZ:])), 16)
            elif kind == "e2a":
                wl1 = sbt("w_l1", [128, 8, HID], BF16)
                wl2 = sbt("w_l2", [128, 16, HID], BF16)
                dW.inc(GP.dma_start(out=wl1[:], in_=rr(cx.e2_l1_t)), 16)
                dW.inc(GP.dma_start(out=wl2[:], in_=rr(cx.e2_l2_t)), 16)
            else:
                wl3 = sbt("w_l3", [128, 16, HID], BF16)
                dW.inc(GP.dma_start(out=wl3[:], in_=rr(cx.e2_l3_t)), 16)
            xfb = [sbt(f"xf{i}", [128, 8, 512], BF16) for i in range(2)]
            h2in = [sbt(f"h2i{i}", [128, 16, 512], BF16) for i in range(2)] if kind == "e2b" else None
            h1 = sbt("h1", [128, 16, 512], BF16)
            h1b = sbt("h1b", [128, 16, 512], BF16) if kind == "e2a" else None
            h0 = sbt("h0", [128, 8, 512], BF16) if kind == "e3" else None
            noacc = kind in ("e2a", "e2b")
            wtb = None if noacc else [sbt(f"wt{i}", [128, 4, 4], F32) for i in range(2)]
            nab = 2
            acb = None if noacc else [sbt(f"ac{i}", [128, 4, D - MZ], F32) for i in range(nab)]
            pu = [pst(f"pu{i}", [128, 512]) for i in range(4)]
            pd = [pst(f"pd{i}", [128, 512]) for i in range(4)]
            NGR = T // 512
            iter_end = []
            pum = [[0, 0] for _ in range(4)]
            pdm = [0, 0, 0, 0]
            ac_mark = [0, 0]
            for gidx in range(NGR):
                bi = gidx % 2
                r0 = gidx * 512
                if gidx >= 2:
                    SY.wait_ge(tC.h, iter_end[gidx - 2][0])
                    SY.wait_ge(vC.h, iter_end[gidx - 2][1])
                if kind == "e2b":
                    dL.inc(SY.dma_start(
                        out=h2in[bi][:],
                        in_=cx.h2_dm.rearrange("(k p) T2 -> p k T2", p=128)[:, :, r0:r0 + 512]), 16)
                else:
                    dL.inc(SY.dma_start(
                        out=xfb[bi][:],
                        in_=cx.xf_dm.rearrange("(k p) T2 -> p k T2", p=128)[:, :, r0:r0 + 512]), 16)
                if not noacc:
                    dL.inc(SY.dma_start(
                        out=wtb[bi][:],
                        in_=cx.wts_tm[r0:r0 + 512, :].rearrange("(s p) e -> p s e", p=128)), 16)
                    SY.wait_ge(dS.h, ac_mark[bi % nab])
                    dL.inc(SY.dma_start(
                        out=acb[bi % nab][:],
                        in_=cx.acc_tm[r0:r0 + 512, :].rearrange("(s p) n -> p s n", p=128)), 16)
                wL = dL.n
                TE.wait_ge(dL.h, wL)
                if gidx == 0:
                    TE.wait_ge(dW.h, dW.n)
                def up_chain(wsb, nkb, src_idx, dst_sb, act, nm=16, mul_sb=None):
                    for m in range(nm):
                        ms = m % 4
                        TE.wait_ge(sC.h, pum[ms][0]); TE.wait_ge(vC.h, pum[ms][1])
                        for k in range(nkb):
                            tC.inc(TE.matmul(pu[ms][:], wsb[:, k, m * 128:(m + 1) * 128],
                                             src_idx(k), start=(k == 0), stop=(k == nkb - 1)))
                        if act is not None:
                            SC.wait_ge(tC.h, tC.n)
                            sC.inc(SC.activation(dst_sb[:, m, :], pu[ms][:], act))
                            pum[ms][0] = sC.n
                        elif mul_sb is not None:
                            VE.wait_ge(tC.h, tC.n)
                            vC.inc(VE.tensor_tensor(dst_sb[:, m, :], mul_sb[:, m, :],
                                                    pu[ms][:], ALU.mult))
                            pum[ms][1] = vC.n
                        else:
                            VE.wait_ge(tC.h, tC.n)
                            vC.inc(VE.tensor_copy(dst_sb[:, m, :], pu[ms][:]))
                            pum[ms][1] = vC.n
                if kind == "e0":
                    up_chain(wup, 8, lambda k: xfb[bi][:, k, :], h1, AF.Gelu)
                elif kind == "e1":
                    up_chain(wup, 8, lambda k: xfb[bi][:, k, :], h1, AF.Silu)
                    TE.wait_ge(sC.h, sC.n)
                    up_chain(wu2, 8, lambda k: xfb[bi][:, k, :], h1, None, mul_sb=h1)
                elif kind == "e3":
                    up_chain(wd0, 8, lambda k: xfb[bi][:, k, :], h0, AF.Gelu, nm=8)
                    TE.wait_ge(sC.h, sC.n)
                    up_chain(wup, 8, lambda k: h0[:, k, :], h1, AF.Gelu)
                elif kind == "e2a":
                    up_chain(wl1, 8, lambda k: xfb[bi][:, k, :], h1b, AF.Gelu)
                    TE.wait_ge(sC.h, sC.n)
                    up_chain(wl2, 16, lambda k: h1b[:, k, :], h1, AF.Gelu)
                    SY.wait_ge(sC.h, sC.n)
                    dS.inc(SY.dma_start(
                        out=cx.h2_dm.rearrange("(k p) T2 -> p k T2", p=128)[:, :, r0:r0 + 512],
                        in_=h1[:]), 16)
                    iter_end.append((tC.n, vC.n))
                    continue
                else:
                    up_chain(wl3, 16, lambda k: h2in[bi][:, k, :], h1, AF.Gelu)
                    SY.wait_ge(sC.h, sC.n)
                    dS.inc(SY.dma_start(
                        out=cx.h3_dm.rearrange("(k p) T2 -> p k T2", p=128)[:, :, r0:r0 + 512],
                        in_=h1[:]), 16)
                    iter_end.append((tC.n, vC.n))
                    continue
                TE.wait_ge(sC.h, sC.n); TE.wait_ge(vC.h, vC.n)
                for msub in range(4):
                    for n in range(2):
                        nd = (msub % 2) * 2 + n
                        w0, w1 = n * 512, min((n + 1) * 512, D - MZ)
                        TE.wait_ge(vC.h, pdm[nd])
                        for k in range(16):
                            tC.inc(TE.matmul(pd[nd][:, 0:w1 - w0],
                                             h1[:, k, msub * 128:(msub + 1) * 128],
                                             wdn[:, k, w0:w1], start=(k == 0), stop=(k == 15)))
                        VE.wait_ge(tC.h, tC.n)
                        if msub == 0 and n == 0:
                            VE.wait_ge(dL.h, wL)
                        vC.inc(VE.scalar_tensor_tensor(acb[bi % nab][:, msub, w0:w1],
                                                       pd[nd][:, 0:w1 - w0],
                                                       wtb[bi][:, msub, eidx:eidx + 1],
                                                       acb[bi % nab][:, msub, w0:w1],
                                                       ALU.mult, ALU.add))
                        pdm[nd] = vC.n
                SY.wait_ge(vC.h, vC.n)
                dS.inc(SY.dma_start(out=cx.acc_tm[r0:r0 + 512, :]
                                    .rearrange("(s p) n -> p s n", p=128),
                                    in_=acb[bi % nab][:]), 16)
                ac_mark[bi % nab] = dS.n
                iter_end.append((tC.n, vC.n))
            SY.wait_ge(dS.h, dS.n)
            ies.close()
        blk.sync(emit)


def _s7(cx):
    nc = cx.nc
    SY, TE, VE, SC, GP = cx.SY, cx.TE, cx.VE, cx.SC, cx.GP
    with nc.Block("s7") as blk:
        def emit(_):
            ies = ExitStack()
            dW = CT(nc, ies, "s7_dW"); dL = CT(nc, ies, "s7_dL"); dS = CT(nc, ies, "s7_dS")
            tC = CT(nc, ies, "s7_tC"); vC = CT(nc, ies, "s7_vC")
            sbt = lambda n_, s, d: ies.enter_context(nc.sbuf_tensor(n_, list(s), d))
            wdn = sbt("s7_wdn", [128, 16, D - MZ], BF16)
            h3in = [sbt(f"s7_h3{i}", [128, 16, 512], BF16) for i in range(2)]
            wtb = [sbt(f"s7_wt{i}", [128, 4, 4], F32) for i in range(2)]
            acb = [sbt(f"s7_ac{i}", [128, 4, D - MZ], F32) for i in range(2)]
            xq = [sbt(f"s7_xq{i}", [128, 4, D], F32) for i in range(2)]
            mq = [sbt(f"s7_mq{i}", [128, 4, D], F32) for i in range(2)]
            pd = [ies.enter_context(nc.psum_tensor(f"s7_pd{i}", [128, 512], F32))
                  for i in range(2)]
            dW.inc(GP.dma_start(out=wdn[:], in_=cx.e2_l4_t[:, MZ:]
                                .rearrange("(k p) n -> p k n", p=128)), 16)
            NGR = T // 512
            iter_end = []
            pdm = [0, 0]
            eidx = 2
            for gidx in range(NGR):
                bi = gidx % 2
                r0 = gidx * 512
                if gidx >= 2:
                    SY.wait_ge(tC.h, iter_end[gidx - 2][0])
                    SY.wait_ge(vC.h, iter_end[gidx - 2][1])
                    SY.wait_ge(dS.h, iter_end[gidx - 2][2])
                dL.inc(SY.dma_start(
                    out=h3in[bi][:],
                    in_=cx.h3_dm.rearrange("(k p) T2 -> p k T2", p=128)[:, :, r0:r0 + 512]), 16)
                dL.inc(SY.dma_start(
                    out=wtb[bi][:],
                    in_=cx.wts_tm[r0:r0 + 512, :].rearrange("(s p) e -> p s e", p=128)), 16)
                dL.inc(SY.dma_start(
                    out=acb[bi][:],
                    in_=cx.acc_tm[r0:r0 + 512, :].rearrange("(s p) n -> p s n", p=128)), 16)
                dL.inc(SY.dma_start(
                    out=xq[bi][:],
                    in_=cx.xpn_tm[r0:r0 + 512, :].rearrange("(s p) n -> p s n", p=128)), 16)
                dL.inc(SY.dma_start(
                    out=mq[bi][:],
                    in_=cx.mamba_tm[r0:r0 + 512, :].rearrange("(s p) n -> p s n", p=128)), 16)
                wL = dL.n
                TE.wait_ge(dL.h, wL)
                if gidx == 0:
                    TE.wait_ge(dW.h, dW.n)
                for msub in range(4):
                    for n in range(2):
                        w0, w1 = n * 512, min((n + 1) * 512, D - MZ)
                        TE.wait_ge(vC.h, pdm[n])
                        for k in range(16):
                            tC.inc(TE.matmul(pd[n][:, 0:w1 - w0],
                                             h3in[bi][:, k, msub * 128:(msub + 1) * 128],
                                             wdn[:, k, w0:w1], start=(k == 0), stop=(k == 15)))
                        VE.wait_ge(tC.h, tC.n)
                        if msub == 0 and n == 0:
                            VE.wait_ge(dL.h, wL)
                        vC.inc(VE.scalar_tensor_tensor(acb[bi][:, msub, w0:w1],
                                                       pd[n][:, 0:w1 - w0],
                                                       wtb[bi][:, msub, eidx:eidx + 1],
                                                       acb[bi][:, msub, w0:w1],
                                                       ALU.mult, ALU.add))
                        pdm[n] = vC.n
                VE.wait_ge(vC.h, vC.n)
                vC.inc(VE.tensor_tensor(xq[bi][:], xq[bi][:], mq[bi][:], ALU.add))
                VE.wait_ge(vC.h, vC.n)
                vC.inc(VE.tensor_tensor(xq[bi][:, :, MZ:], xq[bi][:, :, MZ:],
                                        acb[bi][:], ALU.add))
                SY.wait_ge(vC.h, vC.n)
                dS.inc(SY.dma_start(out=cx.out[r0:r0 + 512, :]
                                    .rearrange("(s p) n -> p s n", p=128),
                                    in_=xq[bi][:]), 16)
                iter_end.append((tC.n, vC.n, dS.n))
            SY.wait_ge(dS.h, dS.n)
            ies.close()
        blk.sync(emit)


def build(dbg=False, stages=("s0", "s1", "s2", "s3", "s4", "s5", "s6", "s7")):
    nc = bass.Bass("TRN2", target_bir_lowering=False)
    cx = _declare(nc, dbg)
    _init_stage(cx)
    if "s0" in stages: _s0(cx)
    if "s1" in stages: _s1(cx)
    if "s2" in stages: _s2(cx)
    if "s3" in stages: _s3(cx)
    if "s4" in stages: _s4(cx)
    if "s5" in stages: _s5(cx)
    if "s6" in stages:
        _s6(cx, "s6e1", "e1")
        _s6(cx, "s6e0", "e0")
        _s6(cx, "s6e3", "e3")
        _s6(cx, "s6e2a", "e2a")
        _s6(cx, "s6e2b", "e2b")
    if "s7" in stages: _s7(cx)
    cx.es.close()
    return nc


_CACHE = {}

def kernel(**inputs):
    consts = _host_consts()
    if "nc" not in _CACHE:
        _CACHE["nc"] = build(dbg=False)
    nc = _CACHE["nc"]
    in_maps = []
    for core in range(NCORES):
        m = dict(_host_layout(inputs, core))
        m.update(consts)
        in_maps.append(m)
    res = run_bass_kernel_spmd(nc, in_maps, core_ids=list(range(NCORES)))
    outs = [np.asarray(r["out"], np.float32).reshape(BPC, L, D) for r in res.results]
    return np.concatenate(outs, axis=0)


# version 14 (final)
# speedup vs baseline: 1.5555x; 1.0207x over previous
import numpy as np
import ml_dtypes
import concourse.bass as bass
import concourse.mybir as mybir
from concourse.bass_utils import run_bass_kernel_spmd
from contextlib import ExitStack

AF = mybir.ActivationFunctionType
ALU = mybir.AluOpType
AX = mybir.AxisListType
F32, BF16, I32 = mybir.dt.float32, mybir.dt.bfloat16, mybir.dt.int32

NCORES = 8
BPC = 4
L = 2048
T = BPC * L
D = 1024
DI = 2048
NH = 32
Q = 128
NCH = L // Q
GRP = 4
NG = NCH // GRP
HEADS = 16
HD = 64
MZ = 256
HID = 2048
NT = T // 128
EPS_RMS = 1.1920929e-07
EPS_LN = 1e-5
TWO_PI = float(2.0 * np.pi)
BIG = 1e30


class CT:
    def __init__(self, nc, es, name):
        self.h = es.enter_context(nc.semaphore(name))
        self.n = 0
    def inc(self, inst, k=1):
        inst.then_inc(self.h, k)
        self.n += k
        return self.n


def _host_consts():
    c = {}
    le = np.where(np.arange(128)[:, None] <= np.arange(128)[None, :], 1.0, 0.0)
    c["c_cum"] = le.astype(np.float32)
    c["c_wmask"] = le.astype(ml_dtypes.bfloat16)
    c["c_idt16"] = np.eye(128, dtype=np.float32).astype(ml_dtypes.bfloat16)
    c["c_idt32"] = np.eye(128, dtype=np.float32)
    oh = np.zeros((32, 32 * 128), np.float32)
    for h in range(32):
        oh[h, h * 128:(h + 1) * 128] = 1.0
    c["c_onehot"] = oh
    inv2pi = (1.0 / 10000.0 ** (np.arange(16, dtype=np.float64) / 16.0)) / (2 * np.pi)
    c["c_if2pi"] = np.broadcast_to(inv2pi.astype(np.float32), (128, 32, 16)).reshape(128, 512).copy()
    am = np.zeros((4, 128, 512), np.float32)
    for i, dd in enumerate((0, 128, 256, 384)):
        am[i] = np.where(np.arange(128)[:, None] + dd <= np.arange(512)[None, :], 0.0, -30000.0)
    c["c_amask"] = np.ascontiguousarray(am.reshape(4 * 128, 512))
    c["c_amask16"] = np.ascontiguousarray(am.reshape(4 * 128, 512)).astype(ml_dtypes.bfloat16)
    c["c_smask16"] = np.where(np.arange(128)[:, None] > np.arange(128)[None, :],
                              np.float32(30000.0), np.float32(0.0)).astype(ml_dtypes.bfloat16)
    ltm = np.zeros((4, 4), np.float32)
    for e in range(4):
        ltm[e, :e] = 1.0
    c["c_ltm"] = np.broadcast_to(ltm.reshape(1, 16), (128, 16)).copy()
    c["c_ones_f"] = np.ones((1, 128), np.float32)
    c["c_onescol"] = np.ones((128, 1), np.float32)
    return c


def _host_layout(inputs, core):
    g = lambda k: np.asarray(inputs[k])
    tr = lambda a: np.ascontiguousarray(np.asarray(a).T)
    d = {}
    d["x"] = np.ascontiguousarray(g("L0_attn_x")[core * BPC:(core + 1) * BPC].reshape(T, D))
    W_in = g("L0_attn_W_in")
    d["w_z"] = np.ascontiguousarray(W_in[:, :DI])
    d["w_x"] = np.ascontiguousarray(W_in[:, DI:2 * DI])
    d["w_bcd"] = np.ascontiguousarray(W_in[:, 2 * DI:])
    d["w_out"] = np.ascontiguousarray(g("L0_attn_W_out"))
    d["wq_t"] = tr(g("L0_ffn_wq")); d["wk_t"] = tr(g("L0_ffn_wk"))
    d["wv_t"] = tr(g("L0_ffn_wv")); d["wo_t"] = tr(g("L0_ffn_wo"))
    d["gw_t"] = tr(g("L0_ffn_gate_w"))
    d["e0_up_t"] = tr(g("L0_ffn_e0_up")); d["e0_dn_t"] = tr(g("L0_ffn_e0_down"))
    d["e1_w1_t"] = tr(g("L0_ffn_e1_w1")); d["e1_w2_t"] = tr(g("L0_ffn_e1_w2"))
    d["e1_dn_t"] = tr(g("L0_ffn_e1_down"))
    d["e2_l1_t"] = tr(g("L0_ffn_e2_l1")); d["e2_l2_t"] = tr(g("L0_ffn_e2_l2"))
    d["e2_l3_t"] = tr(g("L0_ffn_e2_l3")); d["e2_l4_t"] = tr(g("L0_ffn_e2_l4"))
    d["e3_dn_t"] = tr(g("L0_ffn_e3_down")); d["e3_up_t"] = tr(g("L0_ffn_e3_up"))
    d["e3_out_t"] = tr(g("L0_ffn_e3_out"))
    d["a_log_r"] = np.ascontiguousarray(g("L0_attn_A_log").reshape(1, NH))
    d["qn_r"] = np.ascontiguousarray(g("L0_ffn_qn_w").reshape(1, HD))
    d["kn_r"] = np.ascontiguousarray(g("L0_ffn_kn_w").reshape(1, HD))
    d["dtb_bc"] = np.ascontiguousarray(np.broadcast_to(g("L0_attn_dt_bias"), (128, NH)))
    d["dsk_bc"] = np.ascontiguousarray(np.broadcast_to(g("L0_attn_D_skip"), (128, NH)))
    d["n1t"] = np.ascontiguousarray(np.broadcast_to(g("L0_ffn_n1_w"), (128, D)))
    d["n1_dm"] = np.ascontiguousarray(g("L0_ffn_n1_w").reshape(8, 128).T)
    d["n2t"] = np.ascontiguousarray(np.broadcast_to(g("L0_ffn_n2_w"), (128, D)))
    d["qnt"] = np.ascontiguousarray(np.broadcast_to(np.tile(g("L0_ffn_qn_w"), HEADS), (128, D)))
    d["knt"] = np.ascontiguousarray(np.broadcast_to(np.tile(g("L0_ffn_kn_w"), HEADS), (128, D)))
    d["nw_c"] = np.ascontiguousarray(g("L0_attn_norm_weight").reshape(16, 128).T)
    return d


class Ctx:
    pass


def _declare(nc, dbg):
    cx = Ctx()
    cx.nc = nc
    cx.es = ExitStack()
    cx.SY, cx.TE, cx.VE, cx.SC, cx.GP = nc.sync, nc.tensor, nc.vector, nc.scalar, nc.gpsimd
    def din(name, shape, dt=F32):
        ap = nc.dram_tensor(name, list(shape), dt, kind="ExternalInput").ap()
        setattr(cx, name, ap)
    din("x", [T, D])
    din("w_z", [D, DI]); din("w_x", [D, DI]); din("w_bcd", [D, 160])
    din("w_out", [DI, D])
    din("wq_t", [D, D]); din("wk_t", [D, D]); din("wv_t", [D, D])
    din("wo_t", [D, D]); din("gw_t", [D, 4])
    din("e0_up_t", [D, HID]); din("e0_dn_t", [HID, D])
    din("e1_w1_t", [D, HID]); din("e1_w2_t", [D, HID]); din("e1_dn_t", [HID, D])
    din("e2_l1_t", [D, HID]); din("e2_l2_t", [HID, HID])
    din("e2_l3_t", [HID, HID]); din("e2_l4_t", [HID, D])
    din("e3_dn_t", [D, D]); din("e3_up_t", [D, HID]); din("e3_out_t", [HID, D])
    din("a_log_r", [1, NH]); din("qn_r", [1, HD]); din("kn_r", [1, HD])
    din("dtb_bc", [128, NH]); din("dsk_bc", [128, NH])
    din("n1t", [128, D]); din("n2t", [128, D]); din("qnt", [128, D]); din("knt", [128, D])
    din("n1_dm", [128, 8])
    din("nw_c", [128, 16])
    din("c_cum", [128, 128]); din("c_wmask", [128, 128], BF16)
    din("c_idt16", [128, 128], BF16); din("c_idt32", [128, 128])
    din("c_onehot", [32, 32 * 128]); din("c_if2pi", [128, 512])
    din("c_amask", [4 * 128, 512]); din("c_ltm", [128, 16])
    din("c_amask16", [4 * 128, 512], BF16)
    din("c_smask16", [128, 128], BF16)
    din("c_ones_f", [1, 128]); din("c_onescol", [128, 1])
    skind = "ExternalOutput" if dbg else "Internal"
    def dscr(name, shape, dt):
        ap = nc.dram_tensor(name, list(shape), dt, kind=skind).ap()
        setattr(cx, name, ap)
    dscr("xn_tm", [T, D], BF16)
    dscr("xn_tm32", [T, D], F32)
    dscr("xn_dm", [D, T], BF16)
    dscr("sz_tm", [T, DI], BF16)
    dscr("xs_tm", [T, DI], BF16)
    dscr("bc_tm", [T, 128], BF16)
    dscr("dtr_tm", [T, NH], F32)
    dscr("mamba_tm", [T, D], F32)
    dscr("q_dm", [D, T], BF16)
    dscr("k_dm", [D, T], BF16)
    dscr("v_tm", [T, D], BF16)
    dscr("ao_dm", [D, T], BF16)
    dscr("acc_tm", [T, D - MZ], F32)
    dscr("xf_dm", [D, T], BF16)
    dscr("wts_tm", [T, 4], F32)
    dscr("h2_dm", [HID, T], BF16)
    dscr("h3_dm", [HID, T], BF16)
    dscr("xpn_tm", [T, D], F32)
    cx.out = nc.dram_tensor("out", [T, D], F32, kind="ExternalOutput").ap()
    def psb(name, shape, dt):
        setattr(cx, name, cx.es.enter_context(nc.sbuf_tensor(name, list(shape), dt)))
    psb("P_idt16", [128, 128], BF16)
    psb("P_idt32", [128, 128], F32)
    psb("P_cum", [128, 128], F32)
    psb("P_wmask", [128, 128], BF16)
    psb("P_onehot", [32, 32 * 128], F32)
    psb("P_if2pi", [128, 512], F32)
    psb("P_amask16", [128, 4 * 512], BF16)
    psb("P_smask16", [128, 128], BF16)
    psb("P_ltm", [128, 16], F32)
    psb("P_ones_f", [1, 128], F32)
    psb("P_onescol", [128, 1], F32)
    psb("P_dtb", [128, NH], F32)
    psb("P_dsk", [128, NH], F32)
    psb("P_n1t", [128, D], BF16)
    psb("P_n2t", [128, D], BF16)
    psb("P_qnt", [128, D], BF16)
    psb("P_knt", [128, D], BF16)
    psb("P_nwc", [128, 16], F32)
    psb("P_A", [128, NH], F32)
    psb("P_negA", [128, NH], F32)
    psb("P_negc", [128, 1], F32)
    psb("P_tiny", [1, 512], F32)
    psb("P_rowA", [1, 128], F32)
    psb("P_rowNegA", [1, 128], F32)
    psb("P_rowNegc", [1, 128], F32)
    psb("P_const", [128, 2], F32)
    psb("P_zero", [128, 1], F32)
    nc.const_aps.register(F32, EPS_LN, cx.P_const[:, 0:1])
    nc.const_aps.register(F32, EPS_RMS, cx.P_const[:, 1:2])
    return cx


def _init_stage(cx):
    nc = cx.nc
    SY, TE, VE, SC, GP = cx.SY, cx.TE, cx.VE, cx.SC, cx.GP
    with nc.Block("init") as blk:
        def emit(_):
            ies = ExitStack()
            dL = CT(nc, ies, "i_dL"); vC = CT(nc, ies, "i_vC")
            sC = CT(nc, ies, "i_sC"); tC = CT(nc, ies, "i_tC")
            pb = ies.enter_context(nc.psum_tensor("i_pb", [128, 512], F32))
            for dst, src in [(cx.P_cum, cx.c_cum), (cx.P_wmask, cx.c_wmask),
                             (cx.P_idt16, cx.c_idt16), (cx.P_idt32, cx.c_idt32),
                             (cx.P_smask16, cx.c_smask16),
                             (cx.P_onehot, cx.c_onehot), (cx.P_if2pi, cx.c_if2pi),
                             (cx.P_ltm, cx.c_ltm), (cx.P_ones_f, cx.c_ones_f),
                             (cx.P_onescol, cx.c_onescol), (cx.P_dtb, cx.dtb_bc),
                             (cx.P_dsk, cx.dsk_bc), (cx.P_nwc, cx.nw_c)]:
                dL.inc(SY.dma_start(out=dst[:], in_=src[:]), 16)
            dL.inc(SY.dma_start(out=cx.P_amask16[:].rearrange("p (d n) -> p d n", n=512),
                                in_=cx.c_amask16.rearrange("(d p) n -> p d n", p=128)), 16)
            for dst, src in [(cx.P_n1t, cx.n1t), (cx.P_n2t, cx.n2t),
                             (cx.P_qnt, cx.qnt), (cx.P_knt, cx.knt)]:
                dL.inc(GP.dma_start(out=dst[:], in_=src[:]), 16)
            vC.inc(VE.memset(cx.P_const[:, 0:1], EPS_LN))
            vC.inc(VE.memset(cx.P_zero[:], 0.0))
            vC.inc(VE.memset(cx.P_const[:, 1:2], EPS_RMS))
            dL.inc(SY.dma_start(out=cx.P_tiny[0:1, 0:NH], in_=cx.a_log_r[:]), 16)
            dL.inc(SY.dma_start(out=cx.P_tiny[0:1, 32:96], in_=cx.qn_r[:]), 16)
            dL.inc(SY.dma_start(out=cx.P_tiny[0:1, 96:160], in_=cx.kn_r[:]), 16)
            VE.wait_ge(dL.h, dL.n); SC.wait_ge(dL.h, dL.n)
            sC.inc(SC.activation(cx.P_rowA[0:1, 0:NH], cx.P_tiny[0:1, 0:NH], AF.Exp))
            vC.inc(VE.reduce_max(cx.P_tiny[0:1, 160:161], cx.P_tiny[0:1, 32:96], axis=AX.X,
                                 apply_absolute_value=True))
            vC.inc(VE.reduce_max(cx.P_tiny[0:1, 161:162], cx.P_tiny[0:1, 96:160], axis=AX.X,
                                 apply_absolute_value=True))
            VE.wait_ge(vC.h, vC.n)
            vC.inc(VE.tensor_tensor(cx.P_tiny[0:1, 162:163], cx.P_tiny[0:1, 160:161],
                                    cx.P_tiny[0:1, 161:162], ALU.mult))
            VE.wait_ge(vC.h, vC.n)
            vC.inc(VE.tensor_scalar(cx.P_rowNegc[0:1, 0:1], cx.P_tiny[0:1, 162:163],
                                    -8.0, None, ALU.mult))
            SC.wait_ge(sC.h, sC.n)
            sC.inc(SC.mul(cx.P_rowNegA[0:1, 0:NH], cx.P_rowA[0:1, 0:NH], -1.0))
            TE.wait_ge(sC.h, sC.n); TE.wait_ge(vC.h, vC.n)
            tC.inc(TE.matmul(pb[:, 0:NH], cx.P_ones_f[:], cx.P_rowA[0:1, 0:NH],
                             start=True, stop=True))
            VE.wait_ge(tC.h, tC.n)
            vC.inc(VE.tensor_copy(cx.P_A[:], pb[:, 0:NH]))
            VE.wait_ge(vC.h, vC.n)
            TE.wait_ge(vC.h, vC.n)
            tC.inc(TE.matmul(pb[:, 0:NH], cx.P_ones_f[:], cx.P_rowNegA[0:1, 0:NH],
                             start=True, stop=True))
            VE.wait_ge(tC.h, tC.n)
            vC.inc(VE.tensor_copy(cx.P_negA[:], pb[:, 0:NH]))
            VE.wait_ge(vC.h, vC.n)
            TE.wait_ge(vC.h, vC.n)
            tC.inc(TE.matmul(pb[:, 0:1], cx.P_ones_f[:], cx.P_rowNegc[0:1, 0:1],
                             start=True, stop=True))
            VE.wait_ge(tC.h, tC.n)
            vC.inc(VE.tensor_copy(cx.P_negc[:], pb[:, 0:1]))
            SY.wait_ge(vC.h, vC.n)
            ies.close()
        blk.sync(emit)


def _s0(cx):
    nc = cx.nc
    SY, TE, VE, SC = cx.SY, cx.TE, cx.VE, cx.SC
    with nc.Block("s0") as blk:
        def emit(_):
            ies = ExitStack()
            dL = CT(nc, ies, "s0_dL"); dS = CT(nc, ies, "s0_dS")
            tC = CT(nc, ies, "s0_tC"); vC = CT(nc, ies, "s0_vC"); sC = CT(nc, ies, "s0_sC")
            xbuf = [ies.enter_context(nc.sbuf_tensor(f"s0_x{i}", [128, D], F32)) for i in range(2)]
            junk = ies.enter_context(nc.sbuf_tensor("s0_junk", [128, D], F32))
            st = ies.enter_context(nc.sbuf_tensor("s0_st", [128, 8], F32))
            xn16 = [ies.enter_context(nc.sbuf_tensor(f"s0_a{i}", [128, D], BF16)) for i in range(2)]
            xn32 = [ies.enter_context(nc.sbuf_tensor(f"s0_b{i}", [128, D], F32)) for i in range(2)]
            xpn = [ies.enter_context(nc.sbuf_tensor(f"s0_c{i}", [128, D], F32)) for i in range(2)]
            dmb = [ies.enter_context(nc.sbuf_tensor(f"s0_d{i}", [128, 8, 128], BF16)) for i in range(2)]
            ptp = [ies.enter_context(nc.psum_tensor(f"s0_p{i}", [128, 1024], BF16)) for i in range(2)]
            iter_v, iter_s = [], []
            tp_evac = [0, 0]
            for t in range(NT):
                bi = t % 2
                if t >= 2:
                    SY.wait_ge(vC.h, iter_v[t - 2])
                    SY.wait_ge(sC.h, iter_s[t - 2])
                dL.inc(SY.dma_start(out=xbuf[bi][:], in_=cx.x[t * 128:(t + 1) * 128, :]), 16)
                wL = dL.n
                VE.wait_ge(dL.h, wL); SC.wait_ge(dL.h, wL)
                vC.inc(VE.reduce_sum(st[:, 0:1], xbuf[bi][:], axis=AX.X))
                sC.inc(SC.activation(junk[:], xbuf[bi][:], AF.Square, accum_out=st[:, 1:2]))
                VE.wait_ge(sC.h, sC.n); VE.wait_ge(vC.h, vC.n)
                vC.inc(VE.tensor_scalar(st[:, 2:3], st[:, 0:1], 1.0 / D, None, ALU.mult))
                VE.wait_ge(vC.h, vC.n)
                vC.inc(VE.tensor_tensor(st[:, 3:4], st[:, 2:3], st[:, 2:3], ALU.mult))
                VE.wait_ge(vC.h, vC.n)
                vC.inc(VE.tensor_scalar(st[:, 4:5], st[:, 1:2], 1.0 / D, st[:, 3:4],
                                        ALU.mult, ALU.subtract))
                SC.wait_ge(vC.h, vC.n)
                sC.inc(SC.activation(st[:, 5:6], st[:, 4:5], AF.Ln, bias=EPS_LN))
                SC.wait_ge(sC.h, sC.n)
                sC.inc(SC.activation(st[:, 6:7], st[:, 5:6], AF.Exp, scale=-0.5))
                VE.wait_ge(sC.h, sC.n)
                vC.inc(VE.scalar_tensor_tensor(st[:, 7:8], st[:, 2:3], -1.0,
                                               st[:, 6:7], ALU.mult, ALU.mult))
                VE.wait_ge(vC.h, vC.n); SC.wait_ge(vC.h, vC.n)
                sC.inc(SC.activation(xn16[bi][:], xbuf[bi][:], AF.Identity,
                                     bias=st[:, 7:8], scale=st[:, 6:7]))
                vC.inc(VE.scalar_tensor_tensor(xn32[bi][:], xbuf[bi][:], st[:, 6:7],
                                               st[:, 7:8].broadcast_to([128, D]),
                                               ALU.mult, ALU.add))
                VE.wait_ge(vC.h, vC.n)
                vC.inc(VE.tensor_tensor(xpn[bi][:], xbuf[bi][:], xn32[bi][:], ALU.add))
                wS, wV = sC.n, vC.n
                TE.wait_ge(sC.h, wS)
                for k in range(8):
                    ks = k % 2
                    TE.wait_ge(vC.h, tp_evac[ks])
                    tC.inc(TE.matmul(ptp[ks][:, 0:128], xn16[bi][:, k * 128:(k + 1) * 128],
                                     cx.P_idt16[:], is_transpose=True, start=True, stop=True))
                    VE.wait_ge(tC.h, tC.n)
                    vC.inc(VE.tensor_copy(dmb[bi][:, k, :], ptp[ks][:, 0:128]))
                    tp_evac[ks] = vC.n
                SY.wait_ge(sC.h, wS)
                dS.inc(SY.dma_start(out=cx.xn_tm[t * 128:(t + 1) * 128, :], in_=xn16[bi][:]), 16)
                SY.wait_ge(vC.h, wV)
                dS.inc(SY.dma_start(out=cx.xn_tm32[t * 128:(t + 1) * 128, :], in_=xn32[bi][:]), 16)
                dS.inc(SY.dma_start(out=cx.xpn_tm[t * 128:(t + 1) * 128, :], in_=xpn[bi][:]), 16)
                SY.wait_ge(vC.h, vC.n)
                dS.inc(SY.dma_start(
                    out=cx.xn_dm.rearrange("(k p) T2 -> p k T2", p=128)[:, :, t * 128:(t + 1) * 128],
                    in_=dmb[bi][:]), 16)
                iter_v.append(vC.n); iter_s.append(sC.n)
            SY.wait_ge(dS.h, dS.n)
            ies.close()
        blk.sync(emit)


def _s1(cx):
    nc = cx.nc
    SY, TE, VE, SC, GP = cx.SY, cx.TE, cx.VE, cx.SC, cx.GP
    with nc.Block("s1") as blk:
        def emit(_):
            ies = ExitStack()
            dW = CT(nc, ies, "s1_dW"); dL = CT(nc, ies, "s1_dL"); dS = CT(nc, ies, "s1_dS")
            tC = CT(nc, ies, "s1_tC"); vC = CT(nc, ies, "s1_vC"); sC = CT(nc, ies, "s1_sC")
            wz_sb = ies.enter_context(nc.sbuf_tensor("s1_wz", [128, 8, DI], BF16))
            wx_sb = ies.enter_context(nc.sbuf_tensor("s1_wx", [128, 8, DI], BF16))
            wb_sb = ies.enter_context(nc.sbuf_tensor("s1_wb", [128, 8, 160], BF16))
            xb = [ies.enter_context(nc.sbuf_tensor(f"s1_x{i}", [128, 8, 128], BF16)) for i in range(2)]
            zb = [ies.enter_context(nc.sbuf_tensor(f"s1_z{i}", [128, DI], BF16)) for i in range(2)]
            sxb = [ies.enter_context(nc.sbuf_tensor(f"s1_s{i}", [128, DI], BF16)) for i in range(2)]
            bcb = [ies.enter_context(nc.sbuf_tensor(f"s1_b{i}", [128, 128], BF16)) for i in range(2)]
            dtb = [ies.enter_context(nc.sbuf_tensor(f"s1_t{i}", [128, 32], F32)) for i in range(2)]
            pz = [ies.enter_context(nc.psum_tensor(f"s1_p{i}", [128, 512], F32)) for i in range(8)]
            dW.inc(GP.dma_start(out=wz_sb[:], in_=cx.w_z.rearrange("(k p) n -> p k n", p=128)), 16)
            dW.inc(GP.dma_start(out=wx_sb[:], in_=cx.w_x.rearrange("(k p) n -> p k n", p=128)), 16)
            dW.inc(GP.dma_start(out=wb_sb[:], in_=cx.w_bcd.rearrange("(k p) n -> p k n", p=128)), 16)
            iter_v, iter_s = [], []
            for t in range(NT):
                bi = t % 2
                if t >= 2:
                    SY.wait_ge(vC.h, iter_v[t - 2])
                    SY.wait_ge(sC.h, iter_s[t - 2])
                dL.inc(SY.dma_start(
                    out=xb[bi][:],
                    in_=cx.xn_dm.rearrange("(k p) T2 -> p k T2", p=128)[:, :, t * 128:(t + 1) * 128]), 16)
                wL = dL.n
                TE.wait_ge(dL.h, wL)
                if t == 0:
                    TE.wait_ge(dW.h, dW.n)
                if t >= 2:
                    TE.wait_ge(sC.h, iter_s[t - 2])
                    TE.wait_ge(vC.h, iter_v[t - 2])
                for k in range(8):
                    for n in range(8):
                        src = wz_sb[:, k, (n % 4) * 512:(n % 4 + 1) * 512] if n < 4 else \
                              wx_sb[:, k, (n - 4) * 512:(n - 3) * 512]
                        tC.inc(TE.matmul(pz[n][:], xb[bi][:, k, :], src,
                                         start=(k == 0), stop=(k == 7)))
                wT = tC.n
                SC.wait_ge(tC.h, wT); VE.wait_ge(tC.h, wT)
                for n in range(4):
                    sC.inc(SC.activation(zb[bi][:, n * 512:(n + 1) * 512], pz[n][:], AF.Silu))
                for n in range(4):
                    vC.inc(VE.tensor_copy(sxb[bi][:, n * 512:(n + 1) * 512], pz[4 + n][:]))
                wE_s, wE_v = sC.n, vC.n
                TE.wait_ge(sC.h, wE_s)
                for k in range(8):
                    tC.inc(TE.matmul(pz[0][:, 0:160], xb[bi][:, k, :], wb_sb[:, k, :],
                                     start=(k == 0), stop=(k == 7)))
                VE.wait_ge(tC.h, tC.n)
                vC.inc(VE.tensor_copy(bcb[bi][:], pz[0][:, 0:128]))
                vC.inc(VE.tensor_copy(dtb[bi][:], pz[0][:, 128:160]))
                wB = vC.n
                SY.wait_ge(sC.h, wE_s)
                dS.inc(SY.dma_start(out=cx.sz_tm[t * 128:(t + 1) * 128, :], in_=zb[bi][:]), 16)
                SY.wait_ge(vC.h, wE_v)
                dS.inc(SY.dma_start(out=cx.xs_tm[t * 128:(t + 1) * 128, :], in_=sxb[bi][:]), 16)
                SY.wait_ge(vC.h, wB)
                dS.inc(SY.dma_start(out=cx.bc_tm[t * 128:(t + 1) * 128, :], in_=bcb[bi][:]), 16)
                dS.inc(SY.dma_start(out=cx.dtr_tm[t * 128:(t + 1) * 128, :], in_=dtb[bi][:]), 16)
                iter_v.append(vC.n); iter_s.append(sC.n)
            SY.wait_ge(dS.h, dS.n)
            ies.close()
        blk.sync(emit)


S2DBG = False
S3ABL = ("h", "tp", "v", "qk")
S2SEQ = False

def _s2(cx):
    nc = cx.nc
    SY, TE, VE, SC, GP = cx.SY, cx.TE, cx.VE, cx.SC, cx.GP
    dbgt = {}
    if S2DBG:
        for nm_, sh_, dt_ in [("dbg_rot0", [128, GRP * 128], BF16),
                              ("dbg_rot1", [128, GRP * 128], BF16),
                              ("dbg_sd", [128, GRP * 512], BF16),
                              ("dbg_cd", [128, GRP * 512], BF16),
                              ("dbg_Et", [128, 2 * GRP * 128], BF16),
                              ("dbg_Gm", [128, 2 * GRP * 128], BF16),
                              ("dbg_Wt", [128, 2 * GRP * 128], BF16),
                              ("dbg_bnm", [64, 2 * GRP * 128], BF16),
                              ("dbg_cm2", [64, 2 * GRP * 128], BF16),
                              ("dbg_Hst", [64, GRP * 128], BF16),
                              ("dbg_gdtT", [128, GRP * 32], F32),
                              ("dbg_lamT", [128, GRP * 32], F32),
                              ("dbg_cnm", [32, GRP * 128], F32)]:
            dbgt[nm_] = nc.dram_tensor(nm_, sh_, dt_, kind="ExternalOutput").ap()
    with nc.Block("s2") as blk:
        def emit(_):
            ies = ExitStack()
            dW = CT(nc, ies, "s2_dW"); dL = CT(nc, ies, "s2_dL"); dS = CT(nc, ies, "s2_dS")
            tC = CT(nc, ies, "s2_tC"); vC = CT(nc, ies, "s2_vC"); sC = CT(nc, ies, "s2_sC")
            gC = CT(nc, ies, "s2_gC")
            sbt = lambda n_, s, d: ies.enter_context(nc.sbuf_tensor(n_, list(s), d))
            wout_sb = sbt("s2_wo", [128, 16, D], BF16)
            dtr4 = sbt("s2_dtr", [128, GRP, 32], F32)
            bc4 = sbt("s2_bc", [128, GRP, 128], BF16)
            xs4 = sbt("s2_xs", [128, GRP, DI], BF16)
            sz4 = sbt("s2_sz", [128, GRP, DI], BF16)
            dt4 = sbt("s2_dt", [128, GRP, 32], F32)
            tmpa = sbt("s2_tmpa", [128, GRP, 32], F32)
            cumT = sbt("s2_cumT", [128, GRP * 32], F32)
            gcum = sbt("s2_gcum", [128, GRP * 32], F32)
            biasT = [sbt(f"s2_biasT{i}", [128, GRP * 32], F32) for i in range(2)]
            lamT = [sbt(f"s2_lamT{i}", [128, GRP * 32], F32) for i in range(2)]
            gdtT = [sbt(f"s2_gdtT{i}", [128, GRP * 32], F32) for i in range(2)]
            lndt = sbt("s2_lndt", [128, GRP * 32], F32)
            qa = sbt("s2_qa", [128, 512], F32)
            qi = sbt("s2_qi", [128, 512], I32)
            qf = sbt("s2_qf", [128, 512], F32)
            sdup = [sbt(f"s2_sd{i}", [128, GRP * 512], BF16) for i in range(2)]
            cdup = [sbt(f"s2_cd{i}", [128, GRP * 512], BF16) for i in range(2)]
            cnm = [sbt(f"s2_cnm{i}", [32, GRP * 128], F32) for i in range(2)]
            csrow = sbt("s2_cs", [1, 128], F32)
            offr = sbt("s2_off", [1, 160], F32)
            acdr = sbt("s2_acd", [1, 128], F32)
            decr = sbt("s2_dec", [1, 128], F32)
            dec_bc = [sbt(f"s2_decb{i}", [64, 128], F32) for i in range(2)]
            acd_bc = sbt("s2_acdb", [128, 128], F32)
            H = sbt("s2_H", [64, DI], BF16)
            rot = [[sbt(f"s2_rot{i}{j}", [128, GRP, 128], BF16) for j in range(2)]
                   for i in range(2)]
            rtmp = [[sbt(f"s2_rtmp{i}{j}", [128, GRP, 32], F32) for j in range(2)]
                    for i in range(2)]
            bgdt = [sbt(f"s2_bg{i}", [128, 2, GRP, 64], BF16) for i in range(2)]
            cltm = [sbt(f"s2_cl{i}", [128, GRP, 128], BF16) for i in range(2)]
            bnm = [sbt(f"s2_bn{i}", [64, 2, GRP, 128], BF16) for i in range(2)]
            cnm2 = [sbt(f"s2_cm{i}", [64, 2, GRP, 128], BF16) for i in range(2)]
            clnm = [sbt(f"s2_cn{i}", [64, 2, GRP, 128], BF16) for i in range(2)]
            Et = [sbt(f"s2_E{i}", [128, 2, GRP, 128], BF16) for i in range(2)]
            Wt = [sbt(f"s2_W{i}", [128, 2, GRP, 128], BF16) for i in range(2)]
            Hst = [sbt(f"s2_Hs{i}", [64, GRP, 128], BF16) for i in range(2)]
            ybuf = sbt("s2_y", [128, GRP, DI], BF16)
            yst = sbt("s2_yst", [128, GRP, 4], F32)
            ydmb = sbt("s2_ydm", [128, 16, 128], BF16)
            mo = sbt("s2_mo", [128, D], F32)
            pst = lambda n_, s, d=F32: ies.enter_context(nc.psum_tensor(n_, list(s), d))
            tp1 = pst("s2_tp1", [128, 1024], BF16)
            tp2 = pst("s2_tp2", [128, 1024], BF16)
            pM = [pst(f"s2_pm{i}", [128, 512]) for i in range(2)]
            pG = [pst(f"s2_pg{i}", [128, 512]) for i in range(2)]
            pY = [pst(f"s2_py{i}", [128, 512]) for i in range(2)]
            dW.inc(GP.dma_start(out=wout_sb[:], in_=cx.w_out.rearrange("(k p) n -> p k n", p=128)), 16)
            VE.wait_ge(dW.h, dW.n)
            for k in range(16):
                vC.inc(VE.tensor_scalar(wout_sb[:, k, :], wout_sb[:, k, :],
                                        cx.P_nwc[:, k:k + 1], None, ALU.mult))
            wout_ready = vC.n
            mk = {"tp1": 0, "tp2": 0, "pMv": [0, 0], "pMs": [0, 0],
                  "W": [0, 0], "Yg": [0, 0], "Yev": [0, 0],
                  "Hc": [0, 0], "rr": [0, 0], "ybr": 0, "pMd": [0, 0]}

            def phaseA(b, g, gb, first, out_marks):
                g0 = b * L + g * GRP * Q
                if g == 0:
                    VE.wait_ge(tC.h, mk["Yg"][0]); VE.wait_ge(tC.h, mk["Yg"][1])
                    vC.inc(VE.memset(H[:], 0.0))
                    vC.inc(VE.memset(offr[:], 0.0))
                    VE.wait_ge(vC.h, vC.n)
                wP = (vC.n, sC.n, tC.n, gC.n)
                SY.wait_ge(vC.h, wP[0]); SY.wait_ge(sC.h, wP[1])
                dL.inc(SY.dma_start(out=dtr4[:], in_=cx.dtr_tm[g0:g0 + 512, :]
                                    .rearrange("(c p) h -> p c h", p=128)), 16)
                wDTR = dL.n
                SY.wait_ge(tC.h, wP[2]); SY.wait_ge(gC.h, wP[3])
                dL.inc(SY.dma_start(out=bc4[:], in_=cx.bc_tm[g0:g0 + 512, :]
                                    .rearrange("(c p) n -> p c n", p=128)), 16)
                dL.inc(SY.dma_start(out=xs4[:], in_=cx.xs_tm[g0:g0 + 512, :]
                                    .rearrange("(c p) n -> p c n", p=128)), 16)
                if first:
                    dL.inc(SY.dma_start(out=sz4[:], in_=cx.sz_tm[g0:g0 + 512, :]
                                        .rearrange("(c p) n -> p c n", p=128)), 16)
                wL = dL.n
                VE.wait_ge(dL.h, wDTR); SC.wait_ge(dL.h, wDTR)
                vC.inc(VE.tensor_tensor(dt4[:], dtr4[:],
                                        cx.P_dtb[:].unsqueeze(1).broadcast_to([128, GRP, 32]),
                                        ALU.add))
                VE.wait_ge(vC.h, vC.n); SC.wait_ge(vC.h, vC.n)
                sC.inc(SC.activation(tmpa[:], dt4[:], AF.Abs))
                SC.wait_ge(sC.h, sC.n)
                sC.inc(SC.activation(tmpa[:], tmpa[:], AF.Exp, scale=-1.0))
                SC.wait_ge(sC.h, sC.n)
                sC.inc(SC.activation(tmpa[:], tmpa[:], AF.Ln, bias=1.0))
                VE.wait_ge(sC.h, sC.n)
                vC.inc(VE.tensor_scalar(dt4[:], dt4[:], 0.0, None, ALU.max))
                VE.wait_ge(vC.h, vC.n)
                vC.inc(VE.tensor_tensor(dt4[:], dt4[:], tmpa[:], ALU.add))
                VE.wait_ge(vC.h, vC.n)
                wDT = vC.n
                SC.wait_ge(vC.h, wDT)
                sC.inc(SC.activation(lndt[:].rearrange("p (c h) -> p c h", h=32),
                                     dt4[:], AF.Ln))
                yield
                TE.wait_ge(vC.h, max(wDT, mk["W"][0]))
                for c in range(GRP):
                    tC.inc(TE.matmul(pG[0][:, c * 32:(c + 1) * 32], cx.P_cum[:],
                                     dt4[:, c, :], start=(c == 0), stop=(c == GRP - 1)))
                VE.wait_ge(tC.h, tC.n)
                vC.inc(VE.tensor_copy(cumT[:], pG[0][:, 0:128]))
                VE.wait_ge(vC.h, vC.n)
                TE.wait_ge(vC.h, vC.n)
                for c in range(GRP):
                    tC.inc(TE.matmul(pG[0][0:1, c * 32:(c + 1) * 32],
                                     cx.P_onescol[:], dt4[:, c, :],
                                     start=(c == 0), stop=(c == GRP - 1)))
                VE.wait_ge(tC.h, tC.n)
                vC.inc(VE.tensor_copy(csrow[:], pG[0][0:1, 0:128]))
                VE.wait_ge(vC.h, vC.n)
                vC.inc(VE.tensor_copy(offr[0:1, 0:32], offr[0:1, 128:160]))
                for c in range(1, GRP):
                    VE.wait_ge(vC.h, vC.n)
                    vC.inc(VE.tensor_tensor(offr[0:1, c * 32:(c + 1) * 32],
                                            offr[0:1, (c - 1) * 32:c * 32],
                                            csrow[0:1, (c - 1) * 32:c * 32], ALU.add))
                VE.wait_ge(vC.h, vC.n)
                vC.inc(VE.tensor_tensor(offr[0:1, 128:160], offr[0:1, 96:128],
                                        csrow[0:1, 96:128], ALU.add))
                vC.inc(VE.tensor_tensor(acdr[:].rearrange("o (c h) -> o c h", h=32),
                                        csrow[:].rearrange("o (c h) -> o c h", h=32),
                                        cx.P_rowA[0:1, 0:32].unsqueeze(1)
                                        .broadcast_to([1, GRP, 32]), ALU.mult))
                VE.wait_ge(vC.h, vC.n)
                wOFF = vC.n
                SC.wait_ge(vC.h, wOFF)
                sC.inc(SC.activation(decr[:], acdr[:], AF.Exp, scale=-1.0))
                yield
                TE.wait_ge(vC.h, wOFF)
                tC.inc(TE.matmul(pG[0][:, 0:128], cx.P_ones_f[:], offr[0:1, 0:128],
                                 start=True, stop=True))
                VE.wait_ge(tC.h, tC.n)
                vC.inc(VE.tensor_tensor(gcum[:], cumT[:], pG[0][:, 0:128], ALU.add))
                VE.wait_ge(vC.h, vC.n)
                TE.wait_ge(vC.h, vC.n)
                tC.inc(TE.matmul(pG[0][:, 0:128], cx.P_ones_f[:], acdr[:],
                                 start=True, stop=True))
                VE.wait_ge(tC.h, tC.n)
                vC.inc(VE.tensor_copy(acd_bc[:], pG[0][:, 0:128]))
                VE.wait_ge(vC.h, vC.n)
                TE.wait_ge(vC.h, vC.n); TE.wait_ge(sC.h, sC.n)
                tC.inc(TE.matmul(pG[0][0:64, 0:128], cx.P_ones_f[0:1, 0:64], decr[:],
                                 start=True, stop=True))
                VE.wait_ge(tC.h, tC.n)
                vC.inc(VE.tensor_copy(dec_bc[gb][:], pG[0][0:64, 0:128]))
                VE.wait_ge(vC.h, vC.n)
                vC.inc(VE.tensor_tensor(biasT[gb][:].rearrange("p (c h) -> p c h", h=32),
                                        cumT[:].rearrange("p (c h) -> p c h", h=32),
                                        cx.P_A[:, 0:32].unsqueeze(1)
                                        .broadcast_to([128, GRP, 32]), ALU.mult))
                VE.wait_ge(vC.h, vC.n); SC.wait_ge(vC.h, vC.n)
                sC.inc(SC.activation(lamT[gb][:], biasT[gb][:], AF.Exp, scale=-1.0))
                VE.wait_ge(sC.h, sC.n)
                vC.inc(VE.tensor_tensor(biasT[gb][:], biasT[gb][:], lndt[:], ALU.add))
                VE.wait_ge(vC.h, vC.n)
                vC.inc(VE.tensor_tensor(gdtT[gb][:], biasT[gb][:], acd_bc[:], ALU.subtract))
                VE.wait_ge(vC.h, vC.n); SC.wait_ge(vC.h, vC.n)
                sC.inc(SC.activation(gdtT[gb][:], gdtT[gb][:], AF.Exp))
                yield
                for c in range(GRP):
                    VE.wait_ge(sC.h, sC.n)
                    vC.inc(VE.tensor_tensor(
                        qa[:].rearrange("p (h j) -> p h j", j=16),
                        gcum[:, c * 32:(c + 1) * 32].unsqueeze(2).broadcast_to([128, 32, 16]),
                        cx.P_if2pi[:].rearrange("p (h j) -> p h j", j=16), ALU.mult))
                    VE.wait_ge(vC.h, vC.n)
                    vC.inc(VE.tensor_copy(qi[:], qa[:]))
                    VE.wait_ge(vC.h, vC.n)
                    vC.inc(VE.tensor_copy(qf[:], qi[:]))
                    VE.wait_ge(vC.h, vC.n)
                    vC.inc(VE.tensor_tensor(qf[:], qa[:], qf[:], ALU.subtract))
                    VE.wait_ge(vC.h, vC.n)
                    SC.wait_ge(vC.h, vC.n)
                    sC.inc(SC.activation(sdup[gb][:, c * 512:(c + 1) * 512],
                                         qf[:], AF.Sin, scale=TWO_PI))
                    vC.inc(VE.tensor_scalar(qa[:], qa[:], 0.25, None, ALU.add))
                    VE.wait_ge(vC.h, vC.n)
                    vC.inc(VE.tensor_copy(qi[:], qa[:]))
                    VE.wait_ge(vC.h, vC.n)
                    VE.wait_ge(sC.h, sC.n)
                    vC.inc(VE.tensor_copy(qf[:], qi[:]))
                    VE.wait_ge(vC.h, vC.n)
                    vC.inc(VE.tensor_tensor(qf[:], qa[:], qf[:], ALU.subtract))
                    VE.wait_ge(vC.h, vC.n)
                    SC.wait_ge(vC.h, vC.n)
                    sC.inc(SC.activation(cdup[gb][:, c * 512:(c + 1) * 512],
                                         qf[:], AF.Sin, scale=TWO_PI))
                    if c % 2 == 1:
                        yield
                TE.wait_ge(vC.h, vC.n)
                for c in range(GRP):
                    tC.inc(TE.matmul(pG[0][0:32, c * 128:(c + 1) * 128],
                                     cumT[:, c * 32:(c + 1) * 32], cx.P_idt32[:],
                                     is_transpose=True, start=(c == 0), stop=(c == GRP - 1)))
                VE.wait_ge(tC.h, tC.n)
                vC.inc(VE.tensor_copy(cnm[gb][:], pG[0][0:32, 0:512]))
                mk["W"][0] = vC.n
                out_marks[:] = [vC.n, sC.n, wL]

            def rot_head(h, slot, gb):
                hb = h % 2
                co2 = cdup[gb][:].rearrange("p (c h2 j) -> p c h2 j", h2=32, j=16)[:, :, h, :] \
                    .unsqueeze(2).broadcast_to([128, GRP, 2, 16])
                si2 = sdup[gb][:].rearrange("p (c h2 j) -> p c h2 j", h2=32, j=16)[:, :, h, :] \
                    .unsqueeze(2).broadcast_to([128, GRP, 2, 16])
                bview = bc4[:].rearrange("p c (g n) -> p c g n", g=2)
                rview = rot[slot][hb][:].rearrange("p c (g n) -> p c g n", g=2)
                bev, bod = bview[:, :, :, 0:32:2], bview[:, :, :, 1:32:2]
                rev, rod = rview[:, :, :, 0:32:2], rview[:, :, :, 1:32:2]
                rt = rtmp[slot][hb][:].rearrange("p c (g j) -> p c g j", g=2)
                vC.inc(VE.tensor_copy(rview[:, :, :, 32:64], bview[:, :, :, 32:64]))
                vC.inc(VE.tensor_tensor(rev, bev, co2, ALU.mult))
                vC.inc(VE.tensor_tensor(rt, bod, si2, ALU.mult))
                VE.wait_ge(vC.h, vC.n)
                vC.inc(VE.tensor_tensor(rev, rev, rt, ALU.add))
                vC.inc(VE.tensor_tensor(rod, bod, co2, ALU.mult))
                vC.inc(VE.tensor_tensor(rt, bev, si2, ALU.mult))
                VE.wait_ge(vC.h, vC.n)
                vC.inc(VE.tensor_tensor(rod, rod, rt, ALU.subtract))
                VE.wait_ge(vC.h, vC.n)
                vC.inc(VE.tensor_tensor(
                    bgdt[slot][:, hb, :, :], rot[slot][hb][:, :, 0:64],
                    gdtT[gb][:].rearrange("p (c h2) -> p c h2", h2=32)[:, :, h:h + 1]
                        .broadcast_to([128, GRP, 64]), ALU.mult))
                vC.inc(VE.tensor_tensor(
                    cltm[slot][:, :, hb * 64:(hb + 1) * 64], rot[slot][hb][:, :, 64:128],
                    lamT[gb][:].rearrange("p (c h2) -> p c h2", h2=32)[:, :, h:h + 1]
                        .broadcast_to([128, GRP, 64]), ALU.mult))
                return vC.n

            dbg_ctx = [None]

            def pair_job(pj, wA_v, wA_s, wL, gb):
                slot = pj % 2
                h0 = pj * 2
                VE.wait_ge(tC.h, mk["rr"][slot])
                VE.wait_ge(sC.h, wA_s)
                VE.wait_ge(gC.h, mk["ybr"])
                VE.wait_ge(dL.h, wL)
                rot_head(h0, slot, gb)
                wrot = rot_head(h0 + 1, slot, gb)
                yield
                TE.wait_ge(sC.h, mk["tp1"])
                TE.wait_ge(vC.h, wrot)
                for hb in range(2):
                    for c in range(GRP):
                        tC.inc(TE.matmul(tp1[:, (hb * 4 + c) * 128:(hb * 4 + c + 1) * 128],
                                         rot[slot][hb][:, c, :], cx.P_idt16[:],
                                         is_transpose=True, start=(hb == 0 and c == 0),
                                         stop=(hb == 1 and c == GRP - 1)))
                wT1 = tC.n
                TE.wait_ge(sC.h, mk["tp2"])
                for c in range(GRP):
                    tC.inc(TE.matmul(tp2[:, c * 128:(c + 1) * 128],
                                     cltm[slot][:, c, :], cx.P_idt16[:],
                                     is_transpose=True, start=(c == 0),
                                     stop=(c == GRP - 1)))
                wT2 = tC.n
                SC.wait_ge(tC.h, wT1)
                sC.inc(SC.activation(bnm[slot][:].rearrange("q a b2 n -> q (a b2 n)"),
                                     tp1[0:64, :], AF.Identity))
                sC.inc(SC.activation(cnm2[slot][:].rearrange("q a b2 n -> q (a b2 n)"),
                                     tp1[64:128, :], AF.Identity))
                mk["tp1"] = sC.n
                SC.wait_ge(tC.h, wT2)
                sC.inc(SC.activation(clnm[slot][:, 0].rearrange("q b2 n -> q (b2 n)"),
                                     tp2[0:64, 0:512], AF.Identity))
                sC.inc(SC.activation(clnm[slot][:, 1].rearrange("q b2 n -> q (b2 n)"),
                                     tp2[64:128, 0:512], AF.Identity))
                mk["tp2"] = sC.n
                wTev = sC.n
                yield
                for hb in range(2):
                    h = h0 + hb
                    TE.wait_ge(vC.h, max(mk["pMv"][slot], wA_v))
                    TE.wait_ge(sC.h, mk["pMs"][slot])
                    TE.wait_ge(dS.h, mk["pMd"][slot])
                    for c in range(GRP):
                        tC.inc(TE.matmul(pM[slot][:, c * 128:(c + 1) * 128],
                                         cx.P_onehot[:, h * 128:(h + 1) * 128],
                                         cnm[gb][:, c * 128:(c + 1) * 128],
                                         start=(c == 0), stop=False))
                    for c in range(GRP):
                        tC.inc(TE.matmul(pM[slot][:, c * 128:(c + 1) * 128],
                                         cx.P_idt16[:], cx.P_smask16[:],
                                         start=False, stop=(c == GRP - 1)))
                    wR = tC.n
                    SC.wait_ge(tC.h, wR)
                    SC.wait_ge(vC.h, mk["W"][slot])
                    for c in range(GRP):
                        sC.inc(SC.activation(Et[slot][:, hb, c, :],
                                             pM[slot][:, c * 128:(c + 1) * 128], AF.Exp,
                                             scale=cx.P_negA[:, h:h + 1],
                                             bias=biasT[gb][:, c * 32 + h:c * 32 + h + 1]))
                    mk["pMs"][slot] = sC.n
                wE = sC.n
                yield
                TE.wait_ge(sC.h, wTev)
                for hb in range(2):
                    TE.wait_ge(vC.h, mk["W"][slot] if hb == 0 else vC.n)
                    for c in range(GRP):
                        tC.inc(TE.matmul(pG[slot][:, c * 128:(c + 1) * 128],
                                         bnm[slot][:, hb, c, :], cnm2[slot][:, hb, c, :],
                                         start=(c == 0), stop=(c == GRP - 1)))
                    wG = tC.n
                    VE.wait_ge(tC.h, max(wG, mk["Yg"][slot] if hb == 0 else 0))
                    if hb == 0:
                        VE.wait_ge(sC.h, wE)
                    vC.inc(VE.scalar_tensor_tensor(
                        Wt[slot][:, hb], Et[slot][:, hb], BIG,
                        pG[slot][:].rearrange("p (x n) -> p x n", n=128),
                        ALU.min, ALU.mult))
                mk["W"][slot] = vC.n
                yield
                TE.wait_ge(sC.h, wE)
                TE.wait_ge(vC.h, mk["Hc"][slot])
                for c in range(GRP):
                    for hb in range(2):
                        h = h0 + hb
                        ofs = (c * 2 + hb) * 64
                        tC.inc(TE.matmul(pM[slot][0:64, ofs:ofs + 64],
                                         bgdt[slot][:, hb, c, :],
                                         xs4[:, c, h * 64:(h + 1) * 64],
                                         start=(c == 0 and hb == 0),
                                         stop=(c == GRP - 1 and hb == 1)))
                wS = tC.n
                hsl = H[:, h0 * 64:(h0 + 2) * 64]
                VE.wait_ge(tC.h, max(wS, mk["Yg"][slot]))
                vC.inc(VE.tensor_copy(Hst[slot][:, 0, :], hsl))
                r3 = lambda ap_: ap_.rearrange("q (a x) -> q a x", a=2)
                for c in range(GRP):
                    dv = dec_bc[gb][0:64, c * 32 + h0:c * 32 + h0 + 2].unsqueeze(2) \
                        .broadcast_to([64, 2, 64])
                    dst = Hst[slot][:, c + 1, :] if c < GRP - 1 else hsl
                    VE.wait_ge(vC.h, vC.n)
                    vC.inc(VE.tensor_tensor(r3(dst), r3(Hst[slot][:, c, :]), dv, ALU.mult))
                    VE.wait_ge(vC.h, vC.n)
                    vC.inc(VE.tensor_tensor(dst, dst, pM[slot][0:64, c * 128:(c + 1) * 128],
                                            ALU.add))
                mk["Hc"][slot] = vC.n
                mk["pMv"][slot] = vC.n
                yield
                TE.wait_ge(vC.h, max(mk["Yev"][slot], mk["W"][slot], mk["Hc"][slot]))
                TE.wait_ge(sC.h, wTev)
                for hb in range(2):
                    h = h0 + hb
                    for c in range(GRP):
                        base = (hb * 4 + c) * 64
                        tC.inc(TE.matmul(pY[slot][:, base:base + 64], Wt[slot][:, hb, c, :],
                                         xs4[:, c, h * 64:(h + 1) * 64],
                                         start=(hb == 0 and c == 0), stop=False))
                        tC.inc(TE.matmul(pY[slot][:, base:base + 64], clnm[slot][:, hb, c, :],
                                         Hst[slot][:, c, hb * 64:(hb + 1) * 64],
                                         start=False, stop=(hb == 1 and c == GRP - 1)))
                mk["Yg"][slot] = tC.n
                mk["rr"][slot] = tC.n
                wY = tC.n
                VE.wait_ge(tC.h, wY)
                VE.wait_ge(gC.h, mk["ybr"])
                for hb in range(2):
                    h = h0 + hb
                    vC.inc(VE.scalar_tensor_tensor(
                        ybuf[:, :, h * 64:(h + 1) * 64],
                        xs4[:, :, h * 64:(h + 1) * 64],
                        cx.P_dsk[:, h:h + 1],
                        pY[slot][:, hb * 256:hb * 256 + 256].rearrange("p (c x) -> p c x", x=64),
                        ALU.mult, ALU.add))
                mk["Yev"][slot] = vC.n
                if S2DBG and dbg_ctx[0] == (0, 0) and pj == 0:
                    SY.wait_ge(vC.h, vC.n); SY.wait_ge(sC.h, sC.n); SY.wait_ge(gC.h, gC.n)
                    for nm_, src_ in [("dbg_rot0", rot[slot][0][:].rearrange("p a b2 -> p (a b2)")),
                                      ("dbg_rot1", rot[slot][1][:].rearrange("p a b2 -> p (a b2)")),
                                      ("dbg_sd", sdup[:]), ("dbg_cd", cdup[:]),
                                      ("dbg_Et", Et[slot][:].rearrange("p a b2 n -> p (a b2 n)")),
                                      ("dbg_Gm", Gm[slot][:].rearrange("p a b2 n -> p (a b2 n)")),
                                      ("dbg_Wt", Wt[slot][:].rearrange("p a b2 n -> p (a b2 n)")),
                                      ("dbg_bnm", bnm[slot][:].rearrange("q a b2 n -> q (a b2 n)")),
                                      ("dbg_cm2", cnm2[slot][:].rearrange("q a b2 n -> q (a b2 n)")),
                                      ("dbg_Hst", Hst[slot][:].rearrange("q a x -> q (a x)")),
                                      ("dbg_gdtT", gdtT[:]), ("dbg_lamT", lamT[:]),
                                      ("dbg_cnm", cnm[:])]:
                        if nm_ in dbgt:
                            dS.inc(SY.dma_start(out=dbgt[nm_][:], in_=src_), 16)
                    SY.wait_ge(dS.h, dS.n)

            def phaseC(b, g, first, nxt_g0):
                g0 = b * L + g * GRP * Q
                GP.wait_ge(vC.h, vC.n)
                gC.inc(GP.tensor_tensor(ybuf[:], ybuf[:], sz4[:], ALU.mult))
                if nxt_g0 is not None:
                    SY.wait_ge(gC.h, gC.n)
                    dL.inc(SY.dma_start(out=sz4[:], in_=cx.sz_tm[nxt_g0:nxt_g0 + 512, :]
                                        .rearrange("(c p) n -> p c n", p=128)), 16)
                SC.wait_ge(gC.h, gC.n)
                for c in range(GRP):
                    for hf in range(2):
                        SC.wait_ge(sC.h, sC.n)
                        sC.inc(SC.activation(ydmb[:, hf * 8:hf * 8 + 8, :]
                                             .rearrange("p a x -> p (a x)"),
                                             ybuf[:, c, hf * 1024:(hf + 1) * 1024],
                                             AF.Square, accum_out=yst[:, c, hf:hf + 1]))
                        yield
                SC.wait_ge(sC.h, sC.n)
                VE.wait_ge(sC.h, sC.n)
                vC.inc(VE.tensor_tensor(yst[:, :, 0:1], yst[:, :, 0:1],
                                        yst[:, :, 1:2], ALU.add))
                SC.wait_ge(vC.h, vC.n)
                for c in range(GRP):
                    sC.inc(SC.activation(yst[:, c, 1:2], yst[:, c, 0:1], AF.Ln,
                                         scale=1.0 / DI, bias=EPS_LN))
                SC.wait_ge(sC.h, sC.n)
                for c in range(GRP):
                    sC.inc(SC.activation(yst[:, c, 2:3], yst[:, c, 1:2], AF.Exp, scale=-0.5))
                GP.wait_ge(sC.h, sC.n)
                gC.inc(GP.tensor_tensor(ybuf[:], ybuf[:],
                                        yst[:, :, 2:3].broadcast_to([128, GRP, DI]),
                                        ALU.mult))
                wYGN = gC.n
                mk["ybr"] = gC.n
                if first:
                    TE.wait_ge(vC.h, wout_ready)
                yield
                for c in range(GRP):
                    TE.wait_ge(gC.h, wYGN)
                    TE.wait_ge(sC.h, mk["tp1"])
                    for k in range(8):
                        tC.inc(TE.matmul(tp1[:, k * 128:(k + 1) * 128],
                                         ybuf[:, c, k * 128:(k + 1) * 128],
                                         cx.P_idt16[:], is_transpose=True,
                                         start=(k == 0), stop=(k == 7)))
                    wTa = tC.n
                    TE.wait_ge(sC.h, mk["tp2"])
                    for k in range(8):
                        tC.inc(TE.matmul(tp2[:, k * 128:(k + 1) * 128],
                                         ybuf[:, c, (k + 8) * 128:(k + 9) * 128],
                                         cx.P_idt16[:], is_transpose=True,
                                         start=(k == 0), stop=(k == 7)))
                    wTb = tC.n
                    SC.wait_ge(tC.h, wTa)
                    sC.inc(SC.activation(ydmb[:, 0:8, :].rearrange("p a x -> p (a x)"),
                                         tp1[:, :], AF.Identity))
                    mk["tp1"] = sC.n
                    SC.wait_ge(tC.h, wTb)
                    sC.inc(SC.activation(ydmb[:, 8:16, :].rearrange("p a x -> p (a x)"),
                                         tp2[:, :], AF.Identity))
                    mk["tp2"] = sC.n
                    wYD = sC.n
                    TE.wait_ge(sC.h, max(wYD, mk["pMs"][0], mk["pMs"][1]))
                    TE.wait_ge(vC.h, max(mk["pMv"][0], mk["pMv"][1],
                                         mk["Hc"][0], mk["Hc"][1]))
                    for k in range(16):
                        for n in range(2):
                            tC.inc(TE.matmul(pM[n][:, 0:512], ydmb[:, k, :],
                                             wout_sb[:, k, n * 512:(n + 1) * 512],
                                             start=(k == 0), stop=(k == 15)))
                    VE.wait_ge(tC.h, tC.n)
                    VE.wait_ge(dS.h, mk["pMd"][0])
                    vC.inc(VE.tensor_copy(mo[:, 0:512], pM[0][:, 0:512]))
                    vC.inc(VE.tensor_copy(mo[:, 512:1024], pM[1][:, 0:512]))
                    mk["pMv"] = [vC.n, vC.n]
                    r0 = g0 + c * 128
                    SY.wait_ge(vC.h, vC.n)
                    dS.inc(SY.dma_start(out=cx.mamba_tm[r0:r0 + 128, :], in_=mo[:]), 16)
                    mk["pMd"] = [dS.n, dS.n]
                    yield

            seq = [(b, g) for b in range(BPC) for g in range(NG)]
            marks = {}

            def drain(gens):
                gens = [g_ for g_ in gens if g_ is not None]
                while gens:
                    for g_ in list(gens):
                        try:
                            next(g_)
                        except StopIteration:
                            gens.remove(g_)

            marks[0] = []
            drain([phaseA(seq[0][0], seq[0][1], 0, True, marks[0])])
            for i, (b, g) in enumerate(seq):
                dbg_ctx[0] = (b, g)
                gb = i % 2
                wA_v, wA_s, wL = marks[i]
                nxt = [2, 3]
                alive = [pair_job(0, wA_v, wA_s, wL, gb),
                         pair_job(1, wA_v, wA_s, wL, gb)]
                while any(g_ is not None for g_ in alive):
                    for sl in range(2):
                        gj = alive[sl]
                        if gj is None:
                            continue
                        try:
                            next(gj)
                        except StopIteration:
                            if nxt[sl] < HEADS:
                                alive[sl] = pair_job(nxt[sl], wA_v, wA_s, wL, gb)
                                nxt[sl] += 2
                            else:
                                alive[sl] = None
                tail = []
                if i + 1 < len(seq):
                    marks[i + 1] = []
                    tail.append(phaseA(seq[i + 1][0], seq[i + 1][1], (i + 1) % 2,
                                       False, marks[i + 1]))
                    nb, ng = seq[i + 1]
                    nxt_g0 = nb * L + ng * GRP * Q
                else:
                    nxt_g0 = None
                tail.append(phaseC(b, g, i == 0, nxt_g0))
                drain(tail)
            SY.wait_ge(dS.h, dS.n)
            ies.close()
        blk.sync(emit)


def _s3(cx):
    nc = cx.nc
    SY, TE, VE, SC, GP = cx.SY, cx.TE, cx.VE, cx.SC, cx.GP
    with nc.Block("s3") as blk:
        def emit(_):
            ies = ExitStack()
            dW = CT(nc, ies, "s3_dW"); dL = CT(nc, ies, "s3_dL"); dS = CT(nc, ies, "s3_dS")
            tC = CT(nc, ies, "s3_tC"); vC = CT(nc, ies, "s3_vC"); sC = CT(nc, ies, "s3_sC")
            sbt = lambda n_, s, d: ies.enter_context(nc.sbuf_tensor(n_, list(s), d))
            wq_sb = sbt("s3_wq", [128, 8, D], BF16)
            wk_sb = sbt("s3_wk", [128, 8, D], BF16)
            wv_sb = sbt("s3_wv", [128, 8, D], BF16)
            n1sb = sbt("s3_n1", [128, 8], F32)
            NW = 3
            xd = [sbt(f"s3_xd{i}", [128, 8, 128], BF16) for i in range(NW)]
            xb = [sbt(f"s3_xb{i}", [128, D], BF16) for i in range(NW)]
            st = [sbt(f"s3_st{i}", [128, 40], F32) for i in range(NW)]
            sq = [sbt(f"s3_sq{i}", [128, D], F32) for i in range(NW)]
            junk = [sbt(f"s3_j{i}", [128, D], F32) for i in range(NW)]
            qtm = [sbt(f"s3_qt{i}", [128, D], BF16) for i in range(NW)]
            qdmb = [[sbt(f"s3_qd{i}{w}", [128, 8, 128], BF16) for w in range(2)]
                    for i in range(NW)]
            vtm = [sbt(f"s3_v{i}", [128, D], BF16) for i in range(NW)]
            ptp = [ies.enter_context(nc.psum_tensor(f"s3_p{i}", [128, 1024], BF16))
                   for i in range(2)]
            pmm = [[ies.enter_context(nc.psum_tensor(f"s3_m{i}{n}", [128, 512], F32))
                    for n in range(2)] for i in range(NW)]
            dW.inc(GP.dma_start(out=wq_sb[:], in_=cx.wq_t.rearrange("(k p) n -> p k n", p=128)), 16)
            dW.inc(GP.dma_start(out=wk_sb[:], in_=cx.wk_t.rearrange("(k p) n -> p k n", p=128)), 16)
            dW.inc(GP.dma_start(out=wv_sb[:], in_=cx.wv_t.rearrange("(k p) n -> p k n", p=128)), 16)
            dW.inc(SY.dma_start(out=n1sb[:], in_=cx.n1_dm[:]), 16)
            VE.wait_ge(dW.h, dW.n)
            for wsb in (wq_sb, wk_sb, wv_sb):
                for k in range(8):
                    vC.inc(VE.tensor_scalar(wsb[:, k, :], wsb[:, k, :],
                                            n1sb[:, k:k + 1], None, ALU.mult))
            wN1 = vC.n
            mk_tp = [0, 0]
            pm_mark = [[0, 0] for _ in range(3)]
            done = {}

            def tile_job(t):
                bi = t % 3
                pm = pmm[bi]
                if t >= 3:
                    dv, ds_, dd = done[t - 3]
                    SY.wait_ge(vC.h, dv); SY.wait_ge(sC.h, ds_); SY.wait_ge(dS.h, dd)
                dL.inc(SY.dma_start(
                    out=xd[bi][:],
                    in_=cx.xn_dm.rearrange("(k p) T2 -> p k T2", p=128)
                        [:, :, t * 128:(t + 1) * 128]), 16)
                dL.inc(SY.dma_start(out=xb[bi][:], in_=cx.xn_tm[t * 128:(t + 1) * 128, :]), 16)
                wL = dL.n
                SC.wait_ge(dL.h, wL)
                sC.inc(SC.activation(junk[bi][:], xb[bi][:], AF.Square,
                                     accum_out=st[bi][:, 0:1]))
                SC.wait_ge(sC.h, sC.n)
                sC.inc(SC.activation(st[bi][:, 1:2], st[bi][:, 0:1], AF.Ln,
                                     scale=1.0 / D, bias=EPS_RMS))
                SC.wait_ge(sC.h, sC.n)
                sC.inc(SC.activation(st[bi][:, 2:3], st[bi][:, 1:2], AF.Exp, scale=-0.5))
                wRS = sC.n
                yield
                TE.wait_ge(dL.h, wL)
                if t < 2:
                    TE.wait_ge(vC.h, wN1)
                for n in range(2):
                    TE.wait_ge(vC.h, pm_mark[bi][n])
                    for k in range(8):
                        tC.inc(TE.matmul(pm[n][:], xd[bi][:, k, :],
                                         wv_sb[:, k, n * 512:(n + 1) * 512],
                                         start=(k == 0), stop=(k == 7)))
                wV = tC.n
                VE.wait_ge(tC.h, wV); VE.wait_ge(sC.h, wRS)
                vC.inc(VE.tensor_scalar(vtm[bi][:, 0:512], pm[0][:],
                                        st[bi][:, 2:3], None, ALU.mult))
                vC.inc(VE.tensor_scalar(vtm[bi][:, 512:1024], pm[1][:],
                                        st[bi][:, 2:3], None, ALU.mult))
                pm_mark[bi][0] = vC.n; pm_mark[bi][1] = vC.n
                SY.wait_ge(vC.h, vC.n)
                dS.inc(SY.dma_start(out=cx.v_tm[t * 128:(t + 1) * 128, :], in_=vtm[bi][:]), 16)
                yield
                for w, (wsb, ntile, dst) in enumerate(
                        ((wq_sb, cx.P_qnt, cx.q_dm), (wk_sb, cx.P_knt, cx.k_dm))):
                    for n in range(2):
                        TE.wait_ge(vC.h, pm_mark[bi][n])
                        for k in range(8):
                            tC.inc(TE.matmul(pm[n][:], xd[bi][:, k, :],
                                             wsb[:, k, n * 512:(n + 1) * 512],
                                             start=(k == 0), stop=(k == 7)))
                    wQ = tC.n
                    SC.wait_ge(tC.h, wQ)
                    sC.inc(SC.activation(sq[bi][:, 0:512], pm[0][:], AF.Square))
                    sC.inc(SC.activation(sq[bi][:, 512:1024], pm[1][:], AF.Square))
                    VE.wait_ge(sC.h, sC.n)
                    for hh in range(16):
                        vC.inc(VE.reduce_sum(st[bi][:, 8 + hh:9 + hh],
                                             sq[bi][:, hh * 64:(hh + 1) * 64], axis=AX.X))
                    VE.wait_ge(vC.h, vC.n)
                    SC.wait_ge(vC.h, vC.n)
                    sC.inc(SC.activation(st[bi][:, 24:40], st[bi][:, 8:24], AF.Ln,
                                         scale=1.0 / HD, bias=EPS_RMS))
                    SC.wait_ge(sC.h, sC.n)
                    sC.inc(SC.activation(st[bi][:, 24:40], st[bi][:, 24:40], AF.Exp,
                                         scale=-0.5))
                    VE.wait_ge(sC.h, sC.n)
                    vC.inc(VE.tensor_tensor(
                        qtm[bi][:, 0:512].rearrange("p (h x) -> p h x", x=64),
                        pm[0][:].rearrange("p (h x) -> p h x", x=64),
                        st[bi][:, 24:32].unsqueeze(2).broadcast_to([128, 8, 64]), ALU.mult))
                    vC.inc(VE.tensor_tensor(
                        qtm[bi][:, 512:1024].rearrange("p (h x) -> p h x", x=64),
                        pm[1][:].rearrange("p (h x) -> p h x", x=64),
                        st[bi][:, 32:40].unsqueeze(2).broadcast_to([128, 8, 64]), ALU.mult))
                    pm_mark[bi][0] = vC.n; pm_mark[bi][1] = vC.n
                    VE.wait_ge(vC.h, vC.n)
                    vC.inc(VE.tensor_tensor(qtm[bi][:], qtm[bi][:], ntile[:], ALU.mult))
                    wQT = vC.n
                    yield
                    for k in range(8):
                        ks = k % 2
                        TE.wait_ge(vC.h, max(mk_tp[ks], wQT))
                        tC.inc(TE.matmul(ptp[ks][:, 0:128], qtm[bi][:, k * 128:(k + 1) * 128],
                                         cx.P_idt16[:], is_transpose=True,
                                         start=True, stop=True))
                        VE.wait_ge(tC.h, tC.n)
                        vC.inc(VE.tensor_copy(qdmb[bi][w][:, k, :], ptp[ks][:, 0:128]))
                        mk_tp[ks] = vC.n
                    SY.wait_ge(vC.h, vC.n)
                    dS.inc(SY.dma_start(
                        out=dst.rearrange("(k p) T2 -> p k T2", p=128)
                            [:, :, t * 128:(t + 1) * 128],
                        in_=qdmb[bi][w][:]), 16)
                    yield
                done[t] = (vC.n, sC.n, dS.n)

            nxt = [3, 4, 5]
            alive = [tile_job(0), tile_job(1), tile_job(2)]
            while any(g_ is not None for g_ in alive):
                for sl in range(3):
                    gj = alive[sl]
                    if gj is None:
                        continue
                    try:
                        next(gj)
                    except StopIteration:
                        if nxt[sl] < NT:
                            alive[sl] = tile_job(nxt[sl])
                            nxt[sl] += 3
                        else:
                            alive[sl] = None
            SY.wait_ge(dS.h, dS.n)
            ies.close()
        blk.sync(emit)


def _s4(cx):
    nc = cx.nc
    SY, TE, VE, SC = cx.SY, cx.TE, cx.VE, cx.SC
    with nc.Block("s4") as blk:
        def emit(_):
            ies = ExitStack()
            dL = CT(nc, ies, "s4_dL"); dS = CT(nc, ies, "s4_dS")
            tC = CT(nc, ies, "s4_tC"); vC = CT(nc, ies, "s4_vC"); sC = CT(nc, ies, "s4_sC")
            NS = 6
            qb_ = [ies.enter_context(nc.sbuf_tensor(f"s4_q{i}", [64, L], BF16)) for i in range(2)]
            kb_ = [ies.enter_context(nc.sbuf_tensor(f"s4_k{i}", [64, L], BF16)) for i in range(2)]
            vb_ = [ies.enter_context(nc.sbuf_tensor(f"s4_v{i}", [128, 16, 65], BF16)) for i in range(2)]
            pb_ = [ies.enter_context(nc.sbuf_tensor(f"s4_pb{i}", [128, 512], BF16)) for i in range(NS)]
            o_un = [ies.enter_context(nc.sbuf_tensor(f"s4_ou{i}", [65, 4 * 512], F32)) for i in range(2)]
            rcp = [ies.enter_context(nc.sbuf_tensor(f"s4_rc{i}", [1, 4 * 512], F32)) for i in range(2)]
            rbc = [ies.enter_context(nc.sbuf_tensor(f"s4_rb{i}", [64, 512], F32)) for i in range(2)]
            ob = [ies.enter_context(nc.sbuf_tensor(f"s4_ob{i}", [64, 512], BF16)) for i in range(2)]
            pS = [ies.enter_context(nc.psum_tensor(f"s4_s{i}", [128, 512], F32)) for i in range(NS)]
            pO = [ies.enter_context(nc.psum_tensor(f"s4_o{i}", [65, 512], F32)) for i in range(2)]
            vC.inc(VE.memset(vb_[0][:, :, 64:65], 1.0))
            vC.inc(VE.memset(vb_[1][:, :, 64:65], 1.0))
            sh = {"ks": 0, "mk_S": [0] * NS, "mk_Sv": [0] * NS, "mk_pb": [0] * NS,
                  "mk_O": [0, 0], "mk_ou": [0, 0], "mk_ob": [0, 0], "done": {}}

            def head_job(it):
                b, h = divmod(it, HEADS)
                bi = it % 2
                if it >= 2:
                    SY.wait_ge(tC.h, sh["done"].get(it - 2, 0))
                dL.inc(SY.dma_start(out=qb_[bi][:],
                                    in_=cx.q_dm[h * 64:(h + 1) * 64, b * L:(b + 1) * L]), 16)
                dL.inc(SY.dma_start(out=kb_[bi][:],
                                    in_=cx.k_dm[h * 64:(h + 1) * 64, b * L:(b + 1) * L]), 16)
                dL.inc(SY.dma_start(
                    out=vb_[bi][:, :, 0:64],
                    in_=cx.v_tm[b * L:(b + 1) * L, :]
                        .rearrange("(kt p) (h2 j) -> p kt h2 j", p=128, j=64)[:, :, h, :]), 16)
                wL = dL.n
                first = True
                kslot = [0]
                for qblk in range(4):
                    nkt = 4 * qblk + 4
                    exp_mark = {}
                    s_mark = {}
                    s_slot = {}
                    def emit_S(kt):
                        ks = 3 * bi + (kslot[0] % 3)
                        kslot[0] += 1
                        s_slot[kt] = ks
                        dd = kt - 4 * qblk
                        TE.wait_ge(sC.h, sh["mk_S"][ks])
                        TE.wait_ge(vC.h, sh["mk_Sv"][ks])
                        if first:
                            TE.wait_ge(dL.h, wL)
                        tC.inc(TE.matmul(pS[ks][:], kb_[bi][:, kt * 128:(kt + 1) * 128],
                                         qb_[bi][:, qblk * 512:(qblk + 1) * 512],
                                         start=True, stop=(dd < 0)))
                        if dd >= 0:
                            tC.inc(TE.matmul(pS[ks][:], cx.P_idt16[:],
                                             cx.P_amask16[:, dd * 512:(dd + 1) * 512],
                                             start=False, stop=True))
                        s_mark[kt] = tC.n
                    def emit_exp(kt):
                        ks = s_slot[kt]
                        wS = s_mark[kt]
                        SC.wait_ge(tC.h, max(wS, sh["mk_pb"][ks]))
                        sC.inc(SC.activation(pb_[ks][:], pS[ks][:], AF.Exp,
                                             scale=0.125, bias=cx.P_negc[:, 0:1]))
                        sh["mk_S"][ks] = sC.n
                        exp_mark[kt] = sC.n
                    emit_S(0)
                    if nkt > 1:
                        emit_S(1)
                    if nkt > 2:
                        emit_S(2)
                    emit_exp(0)
                    first = False
                    for kt in range(nkt):
                        if kt + 3 < nkt:
                            emit_S(kt + 3)
                        if kt + 1 < nkt:
                            emit_exp(kt + 1)
                        TE.wait_ge(sC.h, exp_mark[kt])
                        if kt == 0:
                            TE.wait_ge(vC.h, sh["mk_O"][bi])
                        tC.inc(TE.matmul(pO[bi][:], vb_[bi][:, kt, :], pb_[s_slot[kt]][:],
                                         start=(kt == 0), stop=(kt == nkt - 1)))
                        sh["mk_pb"][s_slot[kt]] = tC.n
                        if kt % 3 == 2:
                            yield
                    wPV = tC.n
                    VE.wait_ge(tC.h, wPV)
                    if qblk == 0:
                        VE.wait_ge(vC.h, sh["mk_ou"][bi])
                    vC.inc(VE.tensor_copy(o_un[bi][:, qblk * 512:(qblk + 1) * 512], pO[bi][:]))
                    sh["mk_O"][bi] = vC.n
                    yield
                SC.wait_ge(vC.h, sh["mk_O"][bi])
                sC.inc(SC.activation(rcp[bi][0:1, :], o_un[bi][64:65, :], AF.Ln))
                SC.wait_ge(sC.h, sC.n)
                sC.inc(SC.activation(rcp[bi][0:1, :], rcp[bi][0:1, :], AF.Exp, scale=-1.0))
                wRC = sC.n
                for qblk in range(4):
                    st = 3 * bi + (qblk % 2)
                    TE.wait_ge(sC.h, max(wRC, sh["mk_S"][st]))
                    TE.wait_ge(vC.h, sh["mk_Sv"][st])
                    tC.inc(TE.matmul(pS[st][0:64, :], cx.P_ones_f[0:1, 0:64],
                                     rcp[bi][0:1, qblk * 512:(qblk + 1) * 512],
                                     start=True, stop=True))
                    VE.wait_ge(tC.h, tC.n)
                    vC.inc(VE.tensor_copy(rbc[bi][:], pS[st][0:64, :]))
                    sh["mk_Sv"][st] = vC.n
                    VE.wait_ge(vC.h, vC.n)
                    VE.wait_ge(dS.h, sh["mk_ob"][bi])
                    vC.inc(VE.scalar_tensor_tensor(ob[bi][:],
                                                   o_un[bi][0:64, qblk * 512:(qblk + 1) * 512],
                                                   1.0, rbc[bi][:], ALU.mult, ALU.mult))
                    SY.wait_ge(vC.h, vC.n)
                    dS.inc(SY.dma_start(
                        out=cx.ao_dm[h * 64:(h + 1) * 64,
                                     b * L + qblk * 512:b * L + (qblk + 1) * 512],
                        in_=ob[bi][:]), 16)
                    sh["mk_ob"][bi] = dS.n
                    yield
                sh["mk_ou"][bi] = vC.n
                sh["done"][it] = tC.n

            for it in range(0, BPC * HEADS, 2):
                g1, g2 = head_job(it), head_job(it + 1)
                alive = [g1, g2]
                while alive:
                    for gjob in list(alive):
                        try:
                            next(gjob)
                        except StopIteration:
                            alive.remove(gjob)
            SY.wait_ge(dS.h, dS.n)
            ies.close()
        blk.sync(emit)


def _s4b(cx):
    nc = cx.nc
    SY, TE, VE, GP = cx.SY, cx.TE, cx.VE, cx.GP
    with nc.Block("s4b") as blk:
        def emit(_):
            ies = ExitStack()
            dW = CT(nc, ies, "s4b_dW"); dL = CT(nc, ies, "s4b_dL"); dS = CT(nc, ies, "s4b_dS")
            tC = CT(nc, ies, "s4b_tC"); vC = CT(nc, ies, "s4b_vC")
            wo_sb = ies.enter_context(nc.sbuf_tensor("s4b_wo", [128, 8, D - MZ], BF16))
            ab = [ies.enter_context(nc.sbuf_tensor(f"s4b_a{i}", [128, 8, 128], BF16)) for i in range(2)]
            ob = [ies.enter_context(nc.sbuf_tensor(f"s4b_o{i}", [128, D - MZ], F32)) for i in range(2)]
            pm = [ies.enter_context(nc.psum_tensor(f"s4b_p{i}", [128, 512], F32)) for i in range(2)]
            dW.inc(GP.dma_start(out=wo_sb[:],
                                in_=cx.wo_t[:, MZ:].rearrange("(k p) n -> p k n", p=128)), 16)
            iter_v = []
            for t in range(NT):
                bi = t % 2
                if t >= 2:
                    SY.wait_ge(vC.h, iter_v[t - 2])
                dL.inc(SY.dma_start(
                    out=ab[bi][:],
                    in_=cx.ao_dm.rearrange("(k p) T2 -> p k T2", p=128)[:, :, t * 128:(t + 1) * 128]), 16)
                wL = dL.n
                TE.wait_ge(dL.h, wL)
                if t == 0:
                    TE.wait_ge(dW.h, dW.n)
                if t >= 2:
                    TE.wait_ge(vC.h, iter_v[t - 2])
                for n in range(2):
                    w0, w1 = n * 512, min((n + 1) * 512, D - MZ)
                    for k in range(8):
                        tC.inc(TE.matmul(pm[n][:, 0:w1 - w0], ab[bi][:, k, :],
                                         wo_sb[:, k, w0:w1], start=(k == 0), stop=(k == 7)))
                VE.wait_ge(tC.h, tC.n)
                vC.inc(VE.tensor_copy(ob[bi][:, 0:512], pm[0][:]))
                vC.inc(VE.tensor_copy(ob[bi][:, 512:768], pm[1][:, 0:256]))
                SY.wait_ge(vC.h, vC.n)
                dS.inc(SY.dma_start(out=cx.acc_tm[t * 128:(t + 1) * 128, :], in_=ob[bi][:]), 16)
                iter_v.append(vC.n)
            SY.wait_ge(dS.h, dS.n)
            ies.close()
        blk.sync(emit)


def _s5(cx):
    nc = cx.nc
    SY, TE, VE, SC, GP = cx.SY, cx.TE, cx.VE, cx.SC, cx.GP
    with nc.Block("s5") as blk:
        def emit(_):
            ies = ExitStack()
            dW = CT(nc, ies, "s5_dW"); dL = CT(nc, ies, "s5_dL"); dS = CT(nc, ies, "s5_dS")
            tC = CT(nc, ies, "s5_tC"); vC = CT(nc, ies, "s5_vC"); sC = CT(nc, ies, "s5_sC")
            sbt = lambda n_, s, d: ies.enter_context(nc.sbuf_tensor(n_, list(s), d))
            gw_sb = sbt("s5_gw", [128, 8, 4], BF16)
            wo_sb = sbt("s5_wo", [128, 8, D - MZ], BF16)
            ab = [sbt(f"s5_ab{i}", [128, 8, 128], BF16) for i in range(2)]
            ob = [sbt(f"s5_ob{i}", [128, D - MZ], F32) for i in range(2)]
            pm4 = [ies.enter_context(nc.psum_tensor(f"s5_pm{i}", [128, 512], F32)) for i in range(2)]
            xnb = [sbt(f"s5_x{i}", [128, D], F32) for i in range(2)]
            x2 = [sbt(f"s5_x2{i}", [128, D], F32) for i in range(2)]
            st = [sbt(f"s5_st{i}", [128, 4], F32) for i in range(2)]
            junk = [sbt(f"s5_j{i}", [128, D], F32) for i in range(2)]
            xf16 = [sbt(f"s5_xf{i}", [128, D], BF16) for i in range(2)]
            xfd = [sbt(f"s5_d{i}", [128, 8, 128], BF16) for i in range(2)]
            gate = [sbt(f"s5_g{i}", [128, 24], F32) for i in range(2)]
            gm = [sbt(f"s5_gm{i}", [128, 32], F32) for i in range(2)]
            wt4 = [sbt(f"s5_w{i}", [128, 4], F32) for i in range(2)]
            ptp = [ies.enter_context(nc.psum_tensor(f"s5_p{i}", [128, 1024], BF16)) for i in range(2)]
            pg = [ies.enter_context(nc.psum_tensor(f"s5_pg{i}", [128, 512], F32)) for i in range(2)]
            dW.inc(GP.dma_start(out=gw_sb[:], in_=cx.gw_t.rearrange("(k p) n -> p k n", p=128)), 16)
            dW.inc(GP.dma_start(out=wo_sb[:],
                                in_=cx.wo_t[:, MZ:].rearrange("(k p) n -> p k n", p=128)), 16)
            mk_tp = [0, 0]
            mk_pg = [0, 0]
            mk_pm = [0, 0]
            done = {}

            def tile_job(t):
                bi = t % 2
                if t >= 2:
                    dv, ds_ = done[t - 2]
                    SY.wait_ge(vC.h, dv); SY.wait_ge(sC.h, ds_)
                dL.inc(SY.dma_start(
                    out=ab[bi][:],
                    in_=cx.ao_dm.rearrange("(k p) T2 -> p k T2", p=128)
                        [:, :, t * 128:(t + 1) * 128]), 16)
                dL.inc(SY.dma_start(out=xnb[bi][:], in_=cx.xn_tm32[t * 128:(t + 1) * 128, :]), 16)
                wL = dL.n
                TE.wait_ge(dL.h, wL)
                if t == 0:
                    TE.wait_ge(dW.h, dW.n)
                for n in range(2):
                    w0, w1 = n * 512, min((n + 1) * 512, D - MZ)
                    TE.wait_ge(vC.h, mk_pm[n])
                    for k in range(8):
                        tC.inc(TE.matmul(pm4[n][:, 0:w1 - w0], ab[bi][:, k, :],
                                         wo_sb[:, k, w0:w1], start=(k == 0), stop=(k == 7)))
                wAO = tC.n
                VE.wait_ge(tC.h, wAO)
                vC.inc(VE.tensor_copy(ob[bi][:, 0:512], pm4[0][:]))
                vC.inc(VE.tensor_copy(ob[bi][:, 512:768], pm4[1][:, 0:256]))
                mk_pm[0] = vC.n; mk_pm[1] = vC.n
                SY.wait_ge(vC.h, vC.n)
                dS.inc(SY.dma_start(out=cx.acc_tm[t * 128:(t + 1) * 128, :], in_=ob[bi][:]), 16)
                yield
                VE.wait_ge(dL.h, wL)
                vC.inc(VE.tensor_copy(x2[bi][:, 0:MZ], xnb[bi][:, 0:MZ]))
                vC.inc(VE.tensor_tensor(x2[bi][:, MZ:], xnb[bi][:, MZ:], ob[bi][:], ALU.add))
                VE.wait_ge(vC.h, vC.n)
                SC.wait_ge(vC.h, vC.n)
                sC.inc(SC.activation(junk[bi][:], x2[bi][:], AF.Square, accum_out=st[bi][:, 0:1]))
                SC.wait_ge(sC.h, sC.n)
                sC.inc(SC.activation(st[bi][:, 1:2], st[bi][:, 0:1], AF.Ln,
                                     scale=1.0 / D, bias=EPS_RMS))
                SC.wait_ge(sC.h, sC.n)
                sC.inc(SC.activation(st[bi][:, 2:3], st[bi][:, 1:2], AF.Exp, scale=-0.5))
                VE.wait_ge(sC.h, sC.n)
                vC.inc(VE.tensor_scalar(xf16[bi][:], x2[bi][:], st[bi][:, 2:3], None, ALU.mult))
                VE.wait_ge(vC.h, vC.n)
                vC.inc(VE.tensor_tensor(xf16[bi][:], xf16[bi][:], cx.P_n2t[:], ALU.mult))
                wXF = vC.n
                yield
                TE.wait_ge(vC.h, wXF)
                for k in range(8):
                    ks = k % 2
                    TE.wait_ge(vC.h, mk_tp[ks])
                    tC.inc(TE.matmul(ptp[ks][:, 0:128], xf16[bi][:, k * 128:(k + 1) * 128],
                                     cx.P_idt16[:], is_transpose=True, start=True, stop=True))
                    VE.wait_ge(tC.h, tC.n)
                    vC.inc(VE.tensor_copy(xfd[bi][:, k, :], ptp[ks][:, 0:128]))
                    mk_tp[ks] = vC.n
                wXD = vC.n
                SY.wait_ge(vC.h, wXD)
                dS.inc(SY.dma_start(
                    out=cx.xf_dm.rearrange("(k p) T2 -> p k T2", p=128)[:, :, t * 128:(t + 1) * 128],
                    in_=xfd[bi][:]), 16)
                TE.wait_ge(vC.h, wXD)
                TE.wait_ge(sC.h, mk_pg[bi])
                for k in range(8):
                    tC.inc(TE.matmul(pg[bi][:, 0:4], xfd[bi][:, k, :], gw_sb[:, k, :],
                                     start=(k == 0), stop=(k == 7)))
                wG = tC.n
                yield
                VE.wait_ge(tC.h, wG)
                vC.inc(VE.reduce_max(st[bi][:, 3:4], pg[bi][:, 0:4], axis=AX.X, negate=True))
                VE.wait_ge(vC.h, vC.n)
                SC.wait_ge(vC.h, vC.n)
                sC.inc(SC.activation(gate[bi][:, 0:4], pg[bi][:, 0:4], AF.Exp,
                                     bias=st[bi][:, 3:4], accum_out=gate[bi][:, 4:5]))
                mk_pg[bi] = sC.n
                VE.wait_ge(sC.h, sC.n)
                vC.inc(VE.reciprocal(gate[bi][:, 5:6], gate[bi][:, 4:5]))
                VE.wait_ge(vC.h, vC.n)
                vC.inc(VE.tensor_scalar(gate[bi][:, 0:4], gate[bi][:, 0:4],
                                        gate[bi][:, 5:6], None, ALU.mult))
                VE.wait_ge(vC.h, vC.n)
                vC.inc(VE.tensor_tensor(gm[bi][:, 0:16].rearrange("p (e j) -> p e j", j=4),
                                        gate[bi][:, 0:4].unsqueeze(1).broadcast_to([128, 4, 4]),
                                        gate[bi][:, 0:4].unsqueeze(2).broadcast_to([128, 4, 4]),
                                        ALU.is_gt))
                vC.inc(VE.tensor_tensor(gm[bi][:, 16:32].rearrange("p (e j) -> p e j", j=4),
                                        gate[bi][:, 0:4].unsqueeze(1).broadcast_to([128, 4, 4]),
                                        gate[bi][:, 0:4].unsqueeze(2).broadcast_to([128, 4, 4]),
                                        ALU.is_equal))
                VE.wait_ge(vC.h, vC.n)
                vC.inc(VE.tensor_tensor(gm[bi][:, 16:32], gm[bi][:, 16:32], cx.P_ltm[:], ALU.mult))
                VE.wait_ge(vC.h, vC.n)
                vC.inc(VE.tensor_tensor(gm[bi][:, 0:16], gm[bi][:, 0:16], gm[bi][:, 16:32], ALU.add))
                VE.wait_ge(vC.h, vC.n)
                vC.inc(VE.tensor_reduce(gate[bi][:, 8:12].unsqueeze(2),
                                        gm[bi][:, 0:16].rearrange("p (e j) -> p e j", j=4),
                                        AX.X, ALU.add))
                VE.wait_ge(vC.h, vC.n)
                vC.inc(VE.tensor_scalar(gate[bi][:, 12:16], gate[bi][:, 8:12], 1.5, None, ALU.is_le))
                VE.wait_ge(vC.h, vC.n)
                vC.inc(VE.tensor_tensor(gate[bi][:, 16:20], gate[bi][:, 0:4],
                                        gate[bi][:, 12:16], ALU.mult))
                VE.wait_ge(vC.h, vC.n)
                vC.inc(VE.reduce_sum(gate[bi][:, 20:21], gate[bi][:, 16:20], axis=AX.X))
                VE.wait_ge(vC.h, vC.n)
                vC.inc(VE.tensor_scalar(gate[bi][:, 21:22], gate[bi][:, 20:21], 1e-8, None, ALU.add))
                VE.wait_ge(vC.h, vC.n)
                vC.inc(VE.reciprocal(gate[bi][:, 22:23], gate[bi][:, 21:22]))
                VE.wait_ge(vC.h, vC.n)
                vC.inc(VE.tensor_scalar(wt4[bi][:], gate[bi][:, 16:20],
                                        gate[bi][:, 22:23], None, ALU.mult))
                SY.wait_ge(vC.h, vC.n)
                dS.inc(SY.dma_start(out=cx.wts_tm[t * 128:(t + 1) * 128, :], in_=wt4[bi][:]), 16)
                done[t] = (vC.n, sC.n)

            nxt = [2, 3]
            alive = [tile_job(0), tile_job(1)]
            while any(g is not None for g in alive):
                for sl in range(2):
                    gj = alive[sl]
                    if gj is None:
                        continue
                    try:
                        next(gj)
                    except StopIteration:
                        if nxt[sl] < NT:
                            alive[sl] = tile_job(nxt[sl])
                            nxt[sl] += 2
                        else:
                            alive[sl] = None
            SY.wait_ge(dS.h, dS.n)
            ies.close()
        blk.sync(emit)


def _s6(cx, name, kind):
    nc = cx.nc
    SY, TE, VE, SC, GP = cx.SY, cx.TE, cx.VE, cx.SC, cx.GP
    with nc.Block(name) as blk:
        def emit(_):
            ies = ExitStack()
            dW = CT(nc, ies, f"{name}_dW"); dL = CT(nc, ies, f"{name}_dL")
            dS = CT(nc, ies, f"{name}_dS")
            tC = CT(nc, ies, f"{name}_tC"); vC = CT(nc, ies, f"{name}_vC")
            sC = CT(nc, ies, f"{name}_sC")
            sbt = lambda n_, s, d: ies.enter_context(nc.sbuf_tensor(name + n_, list(s), d))
            pst = lambda n_, s: ies.enter_context(nc.psum_tensor(name + n_, list(s), F32))
            eidx = {"e0": 0, "e1": 1, "e2b": 2, "e3": 3}.get(kind)
            rr = lambda ap_: ap_.rearrange("(k p) n -> p k n", p=128)
            if kind == "e0":
                wup = sbt("w_up", [128, 8, HID], BF16)
                wdn = sbt("w_dn", [128, 16, D - MZ], BF16)
                dW.inc(GP.dma_start(out=wup[:], in_=rr(cx.e0_up_t)), 16)
                dW.inc(GP.dma_start(out=wdn[:], in_=rr(cx.e0_dn_t[:, MZ:])), 16)
            elif kind == "e1":
                wup = sbt("w_u1", [128, 8, HID], BF16)
                wu2 = sbt("w_u2", [128, 8, HID], BF16)
                wdn = sbt("w_dn", [128, 16, D - MZ], BF16)
                dW.inc(GP.dma_start(out=wup[:], in_=rr(cx.e1_w1_t)), 16)
                dW.inc(GP.dma_start(out=wu2[:], in_=rr(cx.e1_w2_t)), 16)
                dW.inc(GP.dma_start(out=wdn[:], in_=rr(cx.e1_dn_t[:, MZ:])), 16)
            elif kind == "e3":
                wd0 = sbt("w_d0", [128, 8, D], BF16)
                wup = sbt("w_up", [128, 8, HID], BF16)
                wdn = sbt("w_dn", [128, 16, D - MZ], BF16)
                dW.inc(GP.dma_start(out=wd0[:], in_=rr(cx.e3_dn_t)), 16)
                dW.inc(GP.dma_start(out=wup[:], in_=rr(cx.e3_up_t)), 16)
                dW.inc(GP.dma_start(out=wdn[:], in_=rr(cx.e3_out_t[:, MZ:])), 16)
            elif kind == "e2a":
                wl1 = sbt("w_l1", [128, 8, HID], BF16)
                wl2 = sbt("w_l2", [128, 16, HID], BF16)
                dW.inc(GP.dma_start(out=wl1[:], in_=rr(cx.e2_l1_t)), 16)
                dW.inc(GP.dma_start(out=wl2[:], in_=rr(cx.e2_l2_t)), 16)
            else:
                wl3 = sbt("w_l3", [128, 16, HID], BF16)
                dW.inc(GP.dma_start(out=wl3[:], in_=rr(cx.e2_l3_t)), 16)
            xfb = [sbt(f"xf{i}", [128, 8, 512], BF16) for i in range(2)]
            h2in = [sbt(f"h2i{i}", [128, 16, 512], BF16) for i in range(2)] if kind == "e2b" else None
            h1 = sbt("h1", [128, 16, 512], BF16)
            h1b = sbt("h1b", [128, 16, 512], BF16) if kind == "e2a" else None
            h0 = sbt("h0", [128, 8, 512], BF16) if kind == "e3" else None
            noacc = kind in ("e2a", "e2b")
            wtb = None if noacc else [sbt(f"wt{i}", [128, 4, 4], F32) for i in range(2)]
            nab = 2
            acb = None if noacc else [sbt(f"ac{i}", [128, 4, D - MZ], F32) for i in range(nab)]
            pu = [pst(f"pu{i}", [128, 512]) for i in range(4)]
            pd = [pst(f"pd{i}", [128, 512]) for i in range(4)]
            NGR = T // 512
            iter_end = []
            pum = [[0, 0] for _ in range(4)]
            pdm = [0, 0, 0, 0]
            ac_mark = [0, 0]
            for gidx in range(NGR):
                bi = gidx % 2
                r0 = gidx * 512
                if gidx >= 2:
                    SY.wait_ge(tC.h, iter_end[gidx - 2][0])
                    SY.wait_ge(vC.h, iter_end[gidx - 2][1])
                if kind == "e2b":
                    dL.inc(SY.dma_start(
                        out=h2in[bi][:],
                        in_=cx.h2_dm.rearrange("(k p) T2 -> p k T2", p=128)[:, :, r0:r0 + 512]), 16)
                else:
                    dL.inc(SY.dma_start(
                        out=xfb[bi][:],
                        in_=cx.xf_dm.rearrange("(k p) T2 -> p k T2", p=128)[:, :, r0:r0 + 512]), 16)
                if not noacc:
                    dL.inc(SY.dma_start(
                        out=wtb[bi][:],
                        in_=cx.wts_tm[r0:r0 + 512, :].rearrange("(s p) e -> p s e", p=128)), 16)
                    SY.wait_ge(dS.h, ac_mark[bi % nab])
                    dL.inc(SY.dma_start(
                        out=acb[bi % nab][:],
                        in_=cx.acc_tm[r0:r0 + 512, :].rearrange("(s p) n -> p s n", p=128)), 16)
                wL = dL.n
                TE.wait_ge(dL.h, wL)
                if gidx == 0:
                    TE.wait_ge(dW.h, 16)
                def up_chain(wsb, nkb, src_idx, dst_sb, act, nm=16, mul_sb=None):
                    for m in range(nm):
                        ms = m % 4
                        TE.wait_ge(sC.h, pum[ms][0]); TE.wait_ge(vC.h, pum[ms][1])
                        for k in range(nkb):
                            tC.inc(TE.matmul(pu[ms][:], wsb[:, k, m * 128:(m + 1) * 128],
                                             src_idx(k), start=(k == 0), stop=(k == nkb - 1)))
                        if act is not None:
                            SC.wait_ge(tC.h, tC.n)
                            sC.inc(SC.activation(dst_sb[:, m, :], pu[ms][:], act))
                            pum[ms][0] = sC.n
                        elif mul_sb is not None:
                            VE.wait_ge(tC.h, tC.n)
                            vC.inc(VE.tensor_tensor(dst_sb[:, m, :], mul_sb[:, m, :],
                                                    pu[ms][:], ALU.mult))
                            pum[ms][1] = vC.n
                        else:
                            VE.wait_ge(tC.h, tC.n)
                            vC.inc(VE.tensor_copy(dst_sb[:, m, :], pu[ms][:]))
                            pum[ms][1] = vC.n
                if kind == "e0":
                    up_chain(wup, 8, lambda k: xfb[bi][:, k, :], h1, AF.Gelu)
                elif kind == "e1":
                    up_chain(wup, 8, lambda k: xfb[bi][:, k, :], h1, AF.Silu)
                    TE.wait_ge(sC.h, sC.n)
                    if gidx == 0:
                        TE.wait_ge(dW.h, 32)
                    up_chain(wu2, 8, lambda k: xfb[bi][:, k, :], h1, None, mul_sb=h1)
                elif kind == "e3":
                    up_chain(wd0, 8, lambda k: xfb[bi][:, k, :], h0, AF.Gelu, nm=8)
                    TE.wait_ge(sC.h, sC.n)
                    if gidx == 0:
                        TE.wait_ge(dW.h, 32)
                    up_chain(wup, 8, lambda k: h0[:, k, :], h1, AF.Gelu)
                elif kind == "e2a":
                    up_chain(wl1, 8, lambda k: xfb[bi][:, k, :], h1b, AF.Gelu)
                    TE.wait_ge(sC.h, sC.n)
                    if gidx == 0:
                        TE.wait_ge(dW.h, 32)
                    up_chain(wl2, 16, lambda k: h1b[:, k, :], h1, AF.Gelu)
                    SY.wait_ge(sC.h, sC.n)
                    dS.inc(SY.dma_start(
                        out=cx.h2_dm.rearrange("(k p) T2 -> p k T2", p=128)[:, :, r0:r0 + 512],
                        in_=h1[:]), 16)
                    iter_end.append((tC.n, vC.n))
                    continue
                else:
                    up_chain(wl3, 16, lambda k: h2in[bi][:, k, :], h1, AF.Gelu)
                    SY.wait_ge(sC.h, sC.n)
                    dS.inc(SY.dma_start(
                        out=cx.h3_dm.rearrange("(k p) T2 -> p k T2", p=128)[:, :, r0:r0 + 512],
                        in_=h1[:]), 16)
                    iter_end.append((tC.n, vC.n))
                    continue
                TE.wait_ge(sC.h, sC.n); TE.wait_ge(vC.h, vC.n)
                if gidx == 0:
                    TE.wait_ge(dW.h, dW.n)
                for msub in range(4):
                    for n in range(2):
                        nd = (msub % 2) * 2 + n
                        w0, w1 = n * 512, min((n + 1) * 512, D - MZ)
                        TE.wait_ge(vC.h, pdm[nd])
                        for k in range(16):
                            tC.inc(TE.matmul(pd[nd][:, 0:w1 - w0],
                                             h1[:, k, msub * 128:(msub + 1) * 128],
                                             wdn[:, k, w0:w1], start=(k == 0), stop=(k == 15)))
                        VE.wait_ge(tC.h, tC.n)
                        if msub == 0 and n == 0:
                            VE.wait_ge(dL.h, wL)
                        vC.inc(VE.scalar_tensor_tensor(acb[bi % nab][:, msub, w0:w1],
                                                       pd[nd][:, 0:w1 - w0],
                                                       wtb[bi][:, msub, eidx:eidx + 1],
                                                       acb[bi % nab][:, msub, w0:w1],
                                                       ALU.mult, ALU.add))
                        pdm[nd] = vC.n
                SY.wait_ge(vC.h, vC.n)
                dS.inc(SY.dma_start(out=cx.acc_tm[r0:r0 + 512, :]
                                    .rearrange("(s p) n -> p s n", p=128),
                                    in_=acb[bi % nab][:]), 16)
                ac_mark[bi % nab] = dS.n
                iter_end.append((tC.n, vC.n))
            SY.wait_ge(dS.h, dS.n)
            ies.close()
        blk.sync(emit)


def _s7(cx):
    nc = cx.nc
    SY, TE, VE, SC, GP = cx.SY, cx.TE, cx.VE, cx.SC, cx.GP
    with nc.Block("s7") as blk:
        def emit(_):
            ies = ExitStack()
            dW = CT(nc, ies, "s7_dW"); dL = CT(nc, ies, "s7_dL"); dS = CT(nc, ies, "s7_dS")
            tC = CT(nc, ies, "s7_tC"); vC = CT(nc, ies, "s7_vC")
            sbt = lambda n_, s, d: ies.enter_context(nc.sbuf_tensor(n_, list(s), d))
            wdn = sbt("s7_wdn", [128, 16, D - MZ], BF16)
            h3in = [sbt(f"s7_h3{i}", [128, 16, 512], BF16) for i in range(2)]
            wtb = [sbt(f"s7_wt{i}", [128, 4, 4], F32) for i in range(2)]
            acb = [sbt(f"s7_ac{i}", [128, 4, D - MZ], F32) for i in range(2)]
            xq = [sbt(f"s7_xq{i}", [128, 4, D], F32) for i in range(2)]
            mq = [sbt(f"s7_mq{i}", [128, 4, D], F32) for i in range(2)]
            pd = [ies.enter_context(nc.psum_tensor(f"s7_pd{i}", [128, 512], F32))
                  for i in range(2)]
            dW.inc(GP.dma_start(out=wdn[:], in_=cx.e2_l4_t[:, MZ:]
                                .rearrange("(k p) n -> p k n", p=128)), 16)
            NGR = T // 512
            iter_end = []
            pdm = [0, 0]
            eidx = 2
            for gidx in range(NGR):
                bi = gidx % 2
                r0 = gidx * 512
                if gidx >= 2:
                    SY.wait_ge(tC.h, iter_end[gidx - 2][0])
                    SY.wait_ge(vC.h, iter_end[gidx - 2][1])
                    SY.wait_ge(dS.h, iter_end[gidx - 2][2])
                dL.inc(SY.dma_start(
                    out=h3in[bi][:],
                    in_=cx.h3_dm.rearrange("(k p) T2 -> p k T2", p=128)[:, :, r0:r0 + 512]), 16)
                dL.inc(SY.dma_start(
                    out=wtb[bi][:],
                    in_=cx.wts_tm[r0:r0 + 512, :].rearrange("(s p) e -> p s e", p=128)), 16)
                dL.inc(SY.dma_start(
                    out=acb[bi][:],
                    in_=cx.acc_tm[r0:r0 + 512, :].rearrange("(s p) n -> p s n", p=128)), 16)
                dL.inc(SY.dma_start(
                    out=xq[bi][:],
                    in_=cx.xpn_tm[r0:r0 + 512, :].rearrange("(s p) n -> p s n", p=128)), 16)
                dL.inc(SY.dma_start(
                    out=mq[bi][:],
                    in_=cx.mamba_tm[r0:r0 + 512, :].rearrange("(s p) n -> p s n", p=128)), 16)
                wL = dL.n
                TE.wait_ge(dL.h, wL)
                if gidx == 0:
                    TE.wait_ge(dW.h, dW.n)
                for msub in range(4):
                    for n in range(2):
                        w0, w1 = n * 512, min((n + 1) * 512, D - MZ)
                        TE.wait_ge(vC.h, pdm[n])
                        for k in range(16):
                            tC.inc(TE.matmul(pd[n][:, 0:w1 - w0],
                                             h3in[bi][:, k, msub * 128:(msub + 1) * 128],
                                             wdn[:, k, w0:w1], start=(k == 0), stop=(k == 15)))
                        VE.wait_ge(tC.h, tC.n)
                        if msub == 0 and n == 0:
                            VE.wait_ge(dL.h, wL)
                        vC.inc(VE.scalar_tensor_tensor(acb[bi][:, msub, w0:w1],
                                                       pd[n][:, 0:w1 - w0],
                                                       wtb[bi][:, msub, eidx:eidx + 1],
                                                       acb[bi][:, msub, w0:w1],
                                                       ALU.mult, ALU.add))
                        pdm[n] = vC.n
                VE.wait_ge(vC.h, vC.n)
                vC.inc(VE.tensor_tensor(xq[bi][:], xq[bi][:], mq[bi][:], ALU.add))
                VE.wait_ge(vC.h, vC.n)
                vC.inc(VE.tensor_tensor(xq[bi][:, :, MZ:], xq[bi][:, :, MZ:],
                                        acb[bi][:], ALU.add))
                SY.wait_ge(vC.h, vC.n)
                dS.inc(SY.dma_start(out=cx.out[r0:r0 + 512, :]
                                    .rearrange("(s p) n -> p s n", p=128),
                                    in_=xq[bi][:]), 16)
                iter_end.append((tC.n, vC.n, dS.n))
            SY.wait_ge(dS.h, dS.n)
            ies.close()
        blk.sync(emit)


def build(dbg=False, stages=("s0", "s1", "s2", "s3", "s4", "s5", "s6", "s7")):
    nc = bass.Bass("TRN2", target_bir_lowering=False)
    cx = _declare(nc, dbg)
    _init_stage(cx)
    if "s0" in stages: _s0(cx)
    if "s1" in stages: _s1(cx)
    if "s2" in stages: _s2(cx)
    if "s3" in stages: _s3(cx)
    if "s4" in stages: _s4(cx)
    if "s5" in stages: _s5(cx)
    if "s6" in stages:
        _s6(cx, "s6e1", "e1")
        _s6(cx, "s6e0", "e0")
        _s6(cx, "s6e3", "e3")
        _s6(cx, "s6e2a", "e2a")
        _s6(cx, "s6e2b", "e2b")
    if "s7" in stages: _s7(cx)
    cx.es.close()
    return nc


_CACHE = {}

def kernel(**inputs):
    consts = _host_consts()
    if "nc" not in _CACHE:
        _CACHE["nc"] = build(dbg=False)
    nc = _CACHE["nc"]
    in_maps = []
    for core in range(NCORES):
        m = dict(_host_layout(inputs, core))
        m.update(consts)
        in_maps.append(m)
    res = run_bass_kernel_spmd(nc, in_maps, core_ids=list(range(NCORES)))
    outs = [np.asarray(r["out"], np.float32).reshape(BPC, L, D) for r in res.results]
    return np.concatenate(outs, axis=0)
